# version 16
# baseline (speedup 1.0000x reference)
import numpy as np
import ml_dtypes
from contextlib import ExitStack

import concourse.bass as bass
import concourse.mybir as mybir
from concourse.bass_utils import run_bass_kernel_spmd

F32 = mybir.dt.float32
BF16 = mybir.dt.bfloat16
I32 = mybir.dt.int32
AF = mybir.ActivationFunctionType
ALU = mybir.AluOpType
AX = mybir.AxisListType.X

B, N, D, H, DK, HID = 2, 8192, 1024, 8, 128, 3072
T = 64
NSLOT = 16
NPAIR = 8
STRIPES = [(0, 24), (24, 44), (44, 64)]
INV_N = 1.0 / float(N)
INV127 = 1.0 / 127.0
EPS = 1e-5

ENGS = ("sync", "tensor", "scalar", "vector", "gpsimd")


class Sem:
    def __init__(self, ctx, nc, name):
        self.h = ctx.enter_context(nc.semaphore(name))
        self.v = 0


def build_nc():
    nc = bass.Bass("TRN2", target_bir_lowering=False)

    x_in = nc.dram_tensor("x", [N, D], F32, kind="ExternalInput").ap()
    wqT_in = nc.dram_tensor("wqT", [D, H * DK], F32, kind="ExternalInput").ap()
    wkT_in = nc.dram_tensor("wkT", [D, H * DK], F32, kind="ExternalInput").ap()
    wvT_in = nc.dram_tensor("wvT", [H, D, D], F32, kind="ExternalInput").ap()
    wgT_in = nc.dram_tensor("wgT", [D, HID], I32, kind="ExternalInput").ap()
    wuT_in = nc.dram_tensor("wuT", [D, HID], I32, kind="ExternalInput").ap()
    wdT_in = nc.dram_tensor("wdT", [HID, D], I32, kind="ExternalInput").ap()
    ident_in = nc.dram_tensor("ident", [128, 128], BF16, kind="ExternalInput").ap()
    kglob_in = nc.dram_tensor("kglob", [128, T], F32, kind="ExternalInput").ap()
    qglob_in = nc.dram_tensor("qglob", [NPAIR, 128, 256], F32, kind="ExternalInput").ap()
    y_out = nc.dram_tensor("y", [NSLOT * 128, D], F32, kind="ExternalOutput").ap()

    xnkm_d = nc.dram_tensor("xnkm_d", [T, 128, D], BF16, kind="Internal").ap()
    xnfm_d = nc.dram_tensor("xnfm_d", [16, 128, 4096], BF16, kind="Internal").ap()
    xnf32_d = nc.dram_tensor("xnf32_d", [NSLOT, 128, D], F32, kind="Internal").ap()
    apart_d = nc.dram_tensor("apart_d", [NPAIR, H, 128, 2048], F32, kind="Internal").ap()
    a_d = nc.dram_tensor("a_d", [NPAIR, H, 128, 2048], BF16, kind="Internal").ap()
    ht_d = nc.dram_tensor("ht_d", [4, 128, 24 * 512], BF16, kind="Internal").ap()
    g_d = nc.dram_tensor("g_d", [NSLOT, 128, D], F32, kind="Internal").ap()

    x3 = x_in.rearrange("(t p) d -> t p d", p=128)

    cur = {k: [] for k in ENGS}

    def W(eng, sem, val):
        if val > 0:
            cur[eng].append(lambda e, s=sem.h, v=val: e.wait_ge(s, v))

    def E(eng, fn, sem=None, n=1):
        if sem is None:
            cur[eng].append(fn)
            return None
        sem.v += n
        cur[eng].append(lambda e, f=fn, s=sem.h, m=n: f(e).then_inc(s, m))
        return sem.v

    def flush(name):
        nonlocal cur
        bodies = cur
        cur = {k: [] for k in ENGS}
        with nc.Block(name) as block:
            for eng in ENGS:
                ops = bodies[eng]
                if ops:
                    getattr(block, eng)(lambda engine, ops=ops: [f(engine) for f in ops])

    with ExitStack() as S0:
        cbank = S0.enter_context(nc.sbuf_tensor("cbank", [128, 4], F32))
        kglob_sb = S0.enter_context(nc.sbuf_tensor("kglob_sb", [128, T], F32))
        nc.const_aps.register(F32, EPS, cbank[:, 0:1])
        nc.const_aps.register(F32, INV_N, cbank[:, 1:2])
        nc.const_aps.register(F32, INV127, cbank[:, 2:3])

        sem_names = [
            "s_x", "s_x0", "s_x1", "s_x2", "s_n", "s_ng", "s_red", "s_rcp", "s_sq", "s_cols", "s_tr1", "s_fme",
            "s_w10", "s_w11", "s_w12",
            "s_qpld0", "s_qpld1", "s_qmm", "s_qe", "s_wc",
            "s_kmld", "s_fmld0", "s_fmld1", "s_kmm", "s_ke", "s_sc", "s_sim", "s_vm",
            "s_amm", "s_ae", "s_aes", "s_adma0", "s_adma1", "s_admb0", "s_admb1", "s_api0", "s_api1", "s_qgl0", "s_qgl1",
            "s_fw", "s_fwc0", "s_fwc1", "s_fwc2", "s_fwc3", "s_fwc4", "s_fwc5",
            "s_fwc6", "s_fwc7",
            "s_fmtb0", "s_fmtb1", "s_gu", "s_silu", "s_hv", "s_htw",
            "s_htr0", "s_htr1", "s_dmm", "s_gs", "s_gw0", "s_gw1",
            "s_wv0", "s_wv1", "s_wvc", "s_ard0", "s_ard1", "s_ard2", "s_tr3", "s_atc", "s_hom", "s_hoa",
            "s_er0", "s_er1", "s_ev", "s_out0", "s_out1",
        ]
        sm = {n: Sem(S0, nc, n) for n in sem_names}

        with ExitStack() as S_qt:
            qT_sb = S_qt.enter_context(nc.sbuf_tensor("qT_sb", [128, H * NSLOT * 128], BF16))
            with ExitStack() as S_wk:
                wkbf = S_wk.enter_context(nc.sbuf_tensor("wkbf", [128, 8 * H * DK], BF16))
                with ExitStack() as S_wq:
                    wqbf = S_wq.enter_context(nc.sbuf_tensor("wqbf", [128, 8 * H * DK], BF16))
                    ident1 = S_wq.enter_context(nc.sbuf_tensor("ident1", [128, 128], BF16))

                    with ExitStack() as S_i:
                        stage32 = S_i.enter_context(nc.sbuf_tensor("stage32", [128, 8192], F32))

                        E("gpsimd", lambda e: e.memset(cbank[:, 0:1], EPS))
                        E("gpsimd", lambda e: e.memset(cbank[:, 1:2], INV_N))
                        E("gpsimd", lambda e: e.memset(cbank[:, 2:3], INV127))
                        v_id = E("sync", lambda e: e.dma_start(out=ident1[:, :], in_=ident_in[:, :]), sm["s_x"], 16)
                        E("sync", lambda e: e.dma_start(out=kglob_sb[:, :], in_=kglob_in[:, :]), sm["s_x"], 16)
                        wq3 = wqT_in.rearrange("(c p) e -> p c e", p=128)
                        v_wq = E("sync", lambda e, wq3=wq3: e.dma_start(
                            out=stage32.rearrange("p (c e) -> p c e", c=8), in_=wq3), sm["s_x"], 16)
                        W("scalar", sm["s_x"], v_wq)
                        v_cq = E("scalar", lambda e: e.activation(out=wqbf[:, :], in_=stage32[:, :], func=AF.Copy),
                                 sm["s_wc"])
                        W("sync", sm["s_wc"], v_cq)
                        wk3 = wkT_in.rearrange("(c p) e -> p c e", p=128)
                        v_wk = E("sync", lambda e, wk3=wk3: e.dma_start(
                            out=stage32.rearrange("p (c e) -> p c e", c=8), in_=wk3), sm["s_x"], 16)
                        W("scalar", sm["s_x"], v_wk)
                        E("scalar", lambda e: e.activation(out=wkbf[:, :], in_=stage32[:, :], func=AF.Copy),
                          sm["s_wc"])
                        flush("init")

                    with ExitStack() as S_p1:
                        xstage = S_p1.enter_context(nc.sbuf_tensor("xstage", [128, 3 * 4096], F32))
                        xnkm_b = S_p1.enter_context(nc.sbuf_tensor("xnkm_b", [128, 3 * 4096], BF16))
                        xnfm_b = S_p1.enter_context(nc.sbuf_tensor("xnfm_b", [128, 3 * 4096], BF16))
                        xf32_b = S_p1.enter_context(nc.sbuf_tensor("xf32_b", [128, 3 * 1024], F32))
                        cols = S_p1.enter_context(nc.sbuf_tensor("cols", [128, 3 * 24], F32))
                        dump = S_p1.enter_context(nc.sbuf_tensor("dump", [128, 4 * 1024], F32))
                        qp_buf = S_p1.enter_context(nc.sbuf_tensor("qp_buf", [128, 2 * 4096], BF16))
                        ps_t = S_p1.enter_context(nc.psum_tensor("ps_t", [128, 4096], BF16))
                        ps_q = S_p1.enter_context(nc.psum_tensor("ps_q", [128, 1024], F32))

                        vn = [0] * 16
                        vng = [0] * 16
                        vred = [0] * 16
                        vsq = [0] * 16
                        vtr_last = [0] * 16
                        vfme = [0] * 16
                        vw_blk = [0] * 16
                        vfme_slot = {}
                        vqmm_g = [0] * 4
                        vqe_seq = []

                        def emit_qgroup(g):
                            par = g % 2
                            qpb = qp_buf[:, par * 4096:(par + 1) * 4096]
                            qpb3 = qp_buf.rearrange("p (b c q) -> p b c q", b=2, c=8)
                            if g >= 2:
                                W("sync", sm["s_qmm"], vqmm_g[g - 2])
                            sqp = sm["s_qpld0" if par == 0 else "s_qpld1"]
                            for pp in (0, 1, 2):
                                blks = [15 - (4 * g + s) for s in range(4) if (15 - (4 * g + s)) % 3 == pp]
                                if blks:
                                    W("sync", sm[f"s_w1{pp}"], max(vw_blk[b] for b in blks))
                            vld = 0
                            for s in range(4):
                                blkj = 15 - (4 * g + s)
                                srcq = xnfm_d[blkj].rearrange("p (c q) -> p c q", c=8)[:, :, 384:512]
                                vld = E("sync", lambda e, srcq=srcq, qpb3=qpb3, par=par, s=s: e.dma_start(
                                    out=qpb3[:, par, :, s * 128:(s + 1) * 128], in_=srcq), sqp, 16)
                            W("tensor", sqp, vld)
                            for h in range(H):
                                gidx = g * H + h
                                hp = gidx % 2
                                if gidx >= 2:
                                    W("tensor", sm["s_qe"], vqe_seq[gidx - 2])
                                for c in range(8):
                                    r = E("tensor", lambda e, hp=hp, c=c, h=h, qpb=qpb: e.matmul(
                                        ps_q[:, hp * 512:(hp + 1) * 512],
                                        wqbf[:, c * 1024 + h * 128: c * 1024 + (h + 1) * 128],
                                        qpb[:, c * 512:(c + 1) * 512],
                                        start=(c == 0), stop=(c == 7)),
                                        sm["s_qmm"] if c == 7 else None)
                                    if r:
                                        vqmm_g[g] = r
                                W("scalar", sm["s_qmm"], vqmm_g[g])
                                vqe_seq.append(E("scalar", lambda e, hp=hp, h=h, g=g: e.copy(
                                    out=qT_sb[:, h * 2048 + g * 512: h * 2048 + (g + 1) * 512],
                                    in_=ps_q[:, hp * 512:(hp + 1) * 512]), sm["s_qe"]))

                        for idx in range(16):
                            j = 15 - idx
                            par = j % 3
                            bpar = j % 2
                            xs = xstage[:, par * 4096:(par + 1) * 4096]
                            xkb = xnkm_b[:, par * 4096:(par + 1) * 4096]
                            xfb = xnfm_b[:, par * 4096:(par + 1) * 4096]
                            co = cols[:, par * 24:(par + 1) * 24]

                            if idx >= 3:
                                W("sync", sm["s_n"], vn[j + 3])
                                W("sync", sm["s_ng"], vng[j + 3])
                                W("sync", sm["s_sq"], vsq[j + 3])
                            sx = sm[f"s_x{par}"]
                            vx = E("sync", lambda e, xs=xs, j=j: e.dma_start(
                                out=xs.rearrange("p (t d) -> p t d", t=4),
                                in_=x3[4 * j:4 * j + 4, :, :].transpose([1, 0, 2])), sx, 16)

                            W("vector", sx, vx)
                            for t in range(4):
                                E("vector", lambda e, xs=xs, co=co, t=t: e.reduce_sum(
                                    co[:, t:t + 1], xs[:, t * 1024:(t + 1) * 1024], axis=AX),
                                    sm["s_red"] if t == 3 else None)
                            vsum = sm["s_red"].v
                            vred[j] = vsum

                            W("scalar", sx, vx)
                            if idx >= 1:
                                W("scalar", sm["s_sq"], vsq[j + 1])
                            for t in range(4):
                                r = E("scalar", lambda e, xs=xs, co=co, t=t: e.activation(
                                    out=dump[:, t * 1024:(t + 1) * 1024], in_=xs[:, t * 1024:(t + 1) * 1024],
                                    func=AF.Square, accum_out=co[:, 4 + t:5 + t]),
                                    sm["s_sq"] if t == 3 else None)
                                if r:
                                    vsq[j] = r

                            W("gpsimd", sm["s_sq"], vsq[j])
                            W("gpsimd", sm["s_red"], vsum)
                            E("gpsimd", lambda e, co=co: e.tensor_scalar_mul(co[:, 8:12], co[:, 0:4], -1.0 / D))
                            v1 = E("gpsimd", lambda e, co=co: e.tensor_scalar_mul(co[:, 12:16], co[:, 4:8], 1.0 / D),
                                   sm["s_cols"])
                            W("gpsimd", sm["s_cols"], v1)
                            v2 = E("gpsimd", lambda e, co=co: e.tensor_mul(
                                co[:, 16:20], co[:, 8:12], co[:, 8:12]), sm["s_cols"])
                            W("gpsimd", sm["s_cols"], v2)
                            v3 = E("gpsimd", lambda e, co=co: e.tensor_sub(co[:, 16:20], co[:, 12:16], co[:, 16:20]),
                                   sm["s_cols"])
                            W("scalar", sm["s_cols"], v3)
                            v4 = E("scalar", lambda e, co=co: e.activation(
                                out=co[:, 20:24], in_=co[:, 16:20], func=AF.Sqrt, bias=EPS), sm["s_sq"])
                            vsq[j] = v4
                            W("vector", sm["s_sq"], v4)
                            v5 = E("vector", lambda e, co=co: e.reciprocal(co[:, 20:24], co[:, 20:24]),
                                   sm["s_rcp"])
                            W("gpsimd", sm["s_rcp"], v5)
                            v6 = E("gpsimd", lambda e, co=co: e.tensor_mul(
                                co[:, 16:20], co[:, 8:12], co[:, 20:24]), sm["s_cols"])
                            W("gpsimd", sm["s_cols"], v6)

                            if idx >= 3:
                                W("vector", sm[f"s_w1{par}"], vw_blk[j + 3])
                                W("vector", sm["s_tr1"], vtr_last[j + 3])
                            W("vector", sm["s_cols"], sm["s_cols"].v)
                            W("vector", sm["s_rcp"], v5)
                            for t in (0, 1, 2, 3):
                                vng[j] = E("vector", lambda e, xs=xs, xkb=xkb, co=co, t=t: e.tensor_scalar(
                                    out=xkb[:, t * 1024:(t + 1) * 1024], in0=xs[:, t * 1024:(t + 1) * 1024],
                                    scalar1=co[:, 20 + t:21 + t], scalar2=co[:, 16 + t:17 + t],
                                    op0=ALU.mult, op1=ALU.add), sm["s_ng"])
                            vn[j] = E("vector", lambda e, xs=xs, co=co, par=par: e.tensor_scalar(
                                out=xf32_b[:, par * 1024:(par + 1) * 1024], in0=xs[:, 3 * 1024:4 * 1024],
                                scalar1=co[:, 23:24], scalar2=co[:, 19:20],
                                op0=ALU.mult, op1=ALU.add), sm["s_n"])

                            W("tensor", sm["s_n"], vn[j])
                            W("tensor", sm["s_ng"], vng[j])
                            if idx == 0:
                                W("tensor", sm["s_x"], 48)
                            for c in range(8):
                                slot = 2 * bpar + (c % 2)
                                if slot in vfme_slot:
                                    W("tensor", sm["s_fme"], vfme_slot[slot])
                                for t in range(4):
                                    r = E("tensor", lambda e, xkb=xkb, c=c, t=t, slot=slot: e.transpose(
                                        ps_t[:, slot * 1024 + t * 128: slot * 1024 + (t + 1) * 128],
                                        xkb[:, t * 1024 + c * 128: t * 1024 + (c + 1) * 128],
                                        ident1[:, :]), sm["s_tr1"] if t == 3 else None)
                                    if r:
                                        vtr_last[j] = r
                                W("scalar", sm["s_tr1"], vtr_last[j])
                                if c == 0 and idx >= 3:
                                    W("scalar", sm[f"s_w1{par}"], vw_blk[j + 3])
                                vfme_slot[slot] = E("scalar", lambda e, xfb=xfb, c=c, slot=slot: e.copy(
                                    out=xfb[:, c * 512:(c + 1) * 512],
                                    in_=ps_t[:, slot * 1024:slot * 1024 + 512]), sm["s_fme"])
                            vfme[j] = sm["s_fme"].v

                            W("gpsimd", sm["s_n"], vn[j])
                            W("gpsimd", sm["s_ng"], vng[j])
                            sw = sm[f"s_w1{par}"]
                            E("gpsimd", lambda e, xkb=xkb, j=j: e.dma_start(
                                out=xnkm_d[4 * j:4 * j + 4, :, :].transpose([1, 0, 2]),
                                in_=xkb.rearrange("p (t d) -> p t d", t=4)), sw, 16)
                            E("gpsimd", lambda e, par=par, so=15 - j: e.dma_start(
                                out=xnf32_d[so, :, :],
                                in_=xf32_b[:, par * 1024:(par + 1) * 1024]), sw, 16)
                            W("gpsimd", sm["s_fme"], vfme[j])
                            vw_blk[j] = E("gpsimd", lambda e, xfb=xfb, j=j: e.dma_start(
                                out=xnfm_d[j, :, :], in_=xfb[:, :]), sw, 16)

                            if j % 4 == 0:
                                emit_qgroup((12 - j) // 4)
                        W("gpsimd", sm["s_w10"], sm["s_w10"].v)
                        W("gpsimd", sm["s_w11"], sm["s_w11"].v)
                        W("gpsimd", sm["s_w12"], sm["s_w12"].v)
                        flush("p1")

                with ExitStack() as S_p2:
                    xnkm_s = S_p2.enter_context(nc.sbuf_tensor("xnkm_s", [128, 24 * 1024], BF16))
                    kT_s = S_p2.enter_context(nc.sbuf_tensor("kT_s", [128, H * 24 * 128], BF16))
                    xnfm2 = S_p2.enter_context(nc.sbuf_tensor("xnfm2", [128, 2 * 4096], BF16))
                    sim_sb = S_p2.enter_context(nc.sbuf_tensor("sim_sb", [128, 4 * 512], BF16))
                    mask_sb = S_p2.enter_context(nc.sbuf_tensor("mask_sb", [128, 8 * 256], BF16))
                    qgl_sb = S_p2.enter_context(nc.sbuf_tensor("qgl_sb", [128, 2 * 256], F32))
                    abf_sb = S_p2.enter_context(nc.sbuf_tensor("abf_sb", [128, 2 * 1024], BF16))
                    apo_sb = S_p2.enter_context(nc.sbuf_tensor("apo_sb", [128, 2 * 1024], F32))
                    api_sb = S_p2.enter_context(nc.sbuf_tensor("api_sb", [128, 4 * 1024], F32))
                    stg_sb = S_p2.enter_context(nc.sbuf_tensor("stg_sb", [128, 2 * 1024], F32))
                    ps_a = S_p2.enter_context(nc.psum_tensor("ps_a", [128, 2048], F32))
                    ps_s = S_p2.enter_context(nc.psum_tensor("ps_s", [128, 2048], F32))
                    ps_k = ps_s[:, 1024:2048]

                    def pairs_in(si):
                        return [p for p in range(NPAIR) if 64 - 8 * p > STRIPES[si][0]]

                    last_stripe_of = {}
                    for p in range(NPAIR):
                        for si in range(3):
                            if p in pairs_in(si):
                                last_stripe_of[p] = si

                    vA_free = 0
                    vA_free_s = 0
                    vstg_free = 0
                    vabf_free = [0, 0]
                    vapo_free = [0, 0]
                    vapi_used = [0, 0, 0, 0]
                    ci_uid = 0
                    vqgl_used = [0, 0]
                    prev_vsim = [None, None, None, None]
                    vfm_pre = {}

                    for si in range(3):
                        v0, v1 = STRIPES[si]
                        nblk = (v1 - v0) // 4
                        prs = pairs_in(si)

                        kmm_base = sm["s_kmm"].v
                        vkm_last = 0
                        vkm_blk = [0] * nblk
                        vfm_blk = [0] * nblk
                        for bb in range(nblk):
                            j = v0 // 4 + bb
                            vkm_last = E("sync", lambda e, bb=bb, j=j: e.dma_start(
                                out=xnkm_s.rearrange("p (t d) -> p t d", t=24)[:, 4 * bb:4 * bb + 4, :],
                                in_=xnkm_d[4 * j:4 * j + 4, :, :].transpose([1, 0, 2])), sm["s_kmld"], 16)
                            vkm_blk[bb] = vkm_last
                        qgl_pre = {}
                        order_pairs = sorted(prs, key=lambda q: min(64 - 8 * q, v1))
                        seen_par = set()
                        for p in order_pairs:
                            NKAp = 64 - 8 * p
                            u1p = min(NKAp, v1)
                            tl = [v for v in range(v0, u1p) if v >= NKAp - 8]
                            if tl and (p % 2) not in seen_par:
                                seen_par.add(p % 2)
                                sqg = sm["s_qgl0" if p % 2 == 0 else "s_qgl1"]
                                W("sync", sm["s_vm"], vqgl_used[p % 2])
                                qgl_pre[p] = E("sync", lambda e, p=p: e.dma_start(
                                    out=qgl_sb[:, (p % 2) * 256:(p % 2 + 1) * 256],
                                    in_=qglob_in[p, :, :]), sqg, 16)
                        for bb in range(min(nblk, 2)):
                            j = v0 // 4 + bb
                            if (si, bb) in vfm_pre:
                                vfm_blk[bb] = vfm_pre[(si, bb)]
                                continue
                            vfm_blk[bb] = E("sync", lambda e, bb=bb, j=j: e.dma_start(
                                out=xnfm2[:, (bb % 2) * 4096:(bb % 2 + 1) * 4096],
                                in_=xnfm_d[j, :, :]), sm["s_fmld0" if bb % 2 == 0 else "s_fmld1"], 16)

                        vke_seq = []

                        def emit_kblock(bb):
                            W("tensor", sm["s_fmld0" if bb % 2 == 0 else "s_fmld1"], vfm_blk[bb])
                            for pv in (2, 3):
                                if prev_vsim[pv] is not None:
                                    sh, sv = prev_vsim[pv]
                                    W("tensor", sh, sv)
                            for h in range(H):
                                idx = bb * H + h
                                bank = idx % 2
                                if idx >= 2:
                                    W("tensor", sm["s_ke"], vke_seq[idx - 2])
                                vkmm = 0
                                for c in range(8):
                                    r = E("tensor", lambda e, bank=bank, c=c, h=h, bb=bb: e.matmul(
                                        ps_k[:, bank * 512:(bank + 1) * 512],
                                        wkbf[:, c * 1024 + h * 128: c * 1024 + (h + 1) * 128],
                                        xnfm2[:, (bb % 2) * 4096 + c * 512:(bb % 2) * 4096 + (c + 1) * 512],
                                        start=(c == 0), stop=(c == 7)),
                                        sm["s_kmm"] if c == 7 else None)
                                    if r:
                                        vkmm = r
                                W("scalar", sm["s_kmm"], vkmm)
                                vke_seq.append(E("scalar", lambda e, bank=bank, h=h, bb=bb: e.copy(
                                    out=kT_s[:, h * 3072 + bb * 512: h * 3072 + (bb + 1) * 512],
                                    in_=ps_k[:, bank * 512:(bank + 1) * 512]), sm["s_ke"]))
                            if bb + 2 < nblk:
                                j2 = v0 // 4 + bb + 2
                                W("sync", sm["s_kmm"], sm["s_kmm"].v)
                                vfm_blk[bb + 2] = E("sync", lambda e, bb=bb, j2=j2: e.dma_start(
                                    out=xnfm2[:, (bb % 2) * 4096:(bb % 2 + 1) * 4096],
                                    in_=xnfm_d[j2, :, :]), sm["s_fmld0" if bb % 2 == 0 else "s_fmld1"], 16)

                        bb_done = 0
                        for p in sorted(prs, key=lambda q: min(64 - 8 * q, v1)):
                            NKA = 64 - 8 * p
                            NKB = NKA - 4
                            u0, u1 = v0, min(NKA, v1)
                            is_last = (si == last_stripe_of[p])
                            has_ci = (si > 0)

                            need_blocks = (u1 - v0 + 3) // 4
                            while bb_done < need_blocks:
                                emit_kblock(bb_done)
                                bb_done += 1
                            vke_pair = vke_seq[need_blocks * H - 1]
                            vkm_pair = vkm_last
                            pair_first = True

                            tail_vis = [v for v in range(u0, u1) if v >= NKB - 4]
                            vmask_p = 0
                            if tail_vis:
                                sqg = sm["s_qgl0" if p % 2 == 0 else "s_qgl1"]
                                if p in qgl_pre:
                                    vqgl_p = qgl_pre[p]
                                else:
                                    W("sync", sm["s_vm"], vqgl_used[p % 2])
                                    vqgl_p = E("sync", lambda e, p=p: e.dma_start(
                                        out=qgl_sb[:, (p % 2) * 256:(p % 2 + 1) * 256],
                                        in_=qglob_in[p, :, :]), sqg, 16)
                                W("vector", sqg, vqgl_p)
                                for idx, v in enumerate(tail_vis):
                                    wid = 256 if v < NKB else 128
                                    vmask_p = E("vector", lambda e, idx=idx, v=v, p=p, wid=wid: e.tensor_scalar(
                                        out=mask_sb[:, idx * 256: idx * 256 + wid],
                                        in0=qgl_sb[:, (p % 2) * 256:(p % 2) * 256 + wid],
                                        scalar1=kglob_sb[:, v:v + 1], scalar2=INV_N,
                                        op0=ALU.is_ge, op1=ALU.mult), sm["s_vm"])
                                vqgl_used[p % 2] = vmask_p

                            head_cache = None
                            for h in range(H):
                                nv = u1 - u0
                                groups = [(u0 + 2 * k, min(u0 + 2 * k + 2, u1))
                                          for k in range((nv + 1) // 2)]
                                ng = len(groups)
                                vsc_g = [0] * ng
                                vsim_g = [None] * ng

                                if pair_first:
                                    W("tensor", sm["s_ke"], vke_pair)
                                    W("tensor", sm["s_kmld"], vkm_pair)
                                    pair_first = False
                                vA_free_u = vA_free
                                vA_free_su = vA_free_s
                                vke_floor = vke_seq[-1] if vke_seq else 0

                                def emit_scores(gi, hh, vsc_a, vsim_a, p=p, NKB=NKB,
                                                groups=groups, v0=v0):
                                    glo, ghi = groups[gi]
                                    gpar = gi % 4
                                    if gi >= 4:
                                        sh, sv = vsim_a[gi - 4]
                                        W("tensor", sh, sv)
                                    else:
                                        if prev_vsim[gpar] is not None:
                                            sh, sv = prev_vsim[gpar]
                                            W("tensor", sh, sv)
                                        if gpar >= 2:
                                            W("tensor", sm["s_ke"], vke_floor)
                                    for vv in range(glo, ghi):
                                        wid = 256 if vv < NKB else 128
                                        lt = vv - v0
                                        half = vv - glo
                                        r = E("tensor", lambda e, gpar=gpar, half=half, wid=wid, lt=lt, hh=hh: e.matmul(
                                            ps_s[:, gpar * 512 + half * 256: gpar * 512 + half * 256 + wid],
                                            kT_s[:, hh * 3072 + lt * 128: hh * 3072 + (lt + 1) * 128],
                                            qT_sb[:, hh * 2048 + p * 256: hh * 2048 + p * 256 + wid],
                                            start=True, stop=True),
                                            sm["s_sc"] if vv == ghi - 1 else None)
                                        if r:
                                            vsc_a[gi] = r

                                def emit_relu(gi, vsc_a, vsim_a, p=p, NKB=NKB, groups=groups,
                                              tail_vis=tail_vis, vmask_p=vmask_p):
                                    glo, ghi = groups[gi]
                                    gpar = gi % 4
                                    wid = (ghi - glo - 1) * 256 + (256 if ghi - 1 < NKB else 128)
                                    masked = (glo >= NKB - 4)
                                    W("scalar", sm["s_sc"], vsc_a[gi])
                                    if not masked:
                                        v = E("scalar", lambda e, gpar=gpar, wid=wid: e.activation(
                                            out=sim_sb[:, gpar * 512: gpar * 512 + wid],
                                            in_=ps_s[:, gpar * 512: gpar * 512 + wid],
                                            func=AF.Relu, scale=INV_N), sm["s_sim"])
                                        vsim_a[gi] = (sm["s_sim"], v)
                                    else:
                                        vr = E("scalar", lambda e, gpar=gpar, wid=wid: e.activation(
                                            out=sim_sb[:, gpar * 512: gpar * 512 + wid],
                                            in_=ps_s[:, gpar * 512: gpar * 512 + wid],
                                            func=AF.Relu, scale=1.0), sm["s_sim"])
                                        W("vector", sm["s_sim"], vr)
                                        W("vector", sm["s_vm"], vmask_p)
                                        vlast = 0
                                        for vv in range(glo, ghi):
                                            midx = tail_vis.index(vv)
                                            wid_v = 256 if vv < NKB else 128
                                            half = vv - glo
                                            vlast = E("vector", lambda e, gpar=gpar, half=half, midx=midx, wid_v=wid_v: e.tensor_mul(
                                                sim_sb[:, gpar * 512 + half * 256: gpar * 512 + half * 256 + wid_v],
                                                sim_sb[:, gpar * 512 + half * 256: gpar * 512 + half * 256 + wid_v],
                                                mask_sb[:, midx * 256: midx * 256 + wid_v]), sm["s_vm"])
                                        vsim_a[gi] = (sm["s_vm"], vlast)

                                last_vis = {0: u1 - 1, 1: min(u1, NKB) - 1}

                                def emit_amm(gi, NKB=NKB, groups=groups, ng=ng,
                                             vsim_g=vsim_g, u0=u0, v0=v0, last_vis=last_vis):
                                    glo, ghi = groups[gi]
                                    gpar = gi % 4
                                    sh, sv = vsim_g[gi]
                                    if gi == 0:
                                        W("tensor", sm["s_ae"], vA_free_u)
                                        W("tensor", sm["s_aes"], vA_free_su)
                                    W("tensor", sh, sv)
                                    for vv in range(glo, ghi):
                                        halfcol = (vv - glo) * 256
                                        lt = vv - v0
                                        tiles = (0, 1) if vv < NKB else (0,)
                                        for tt in tiles:
                                            for dh in range(2):
                                                first = (vv == u0)
                                                stop = (vv == last_vis[tt])
                                                is_final = (gi == ng - 1 and vv == ghi - 1
                                                            and tt == tiles[-1] and dh == 1)
                                                E("tensor", lambda e, gpar=gpar, halfcol=halfcol, lt=lt, tt=tt, dh=dh, first=first, stop=stop: e.matmul(
                                                    ps_a[:, tt * 1024 + dh * 512: tt * 1024 + (dh + 1) * 512],
                                                    sim_sb[:, gpar * 512 + halfcol + tt * 128: gpar * 512 + halfcol + (tt + 1) * 128],
                                                    xnkm_s[:, lt * 1024 + dh * 512: lt * 1024 + (dh + 1) * 512],
                                                    start=first, stop=stop),
                                                    sm["s_amm"] if is_final else None)

                                vapi_ready = 0
                                api_base = 0
                                sapi = None
                                if has_ci:
                                    api_base = (ci_uid % 2) * 2
                                    sapi = sm["s_api0" if ci_uid % 2 == 0 else "s_api1"]
                                    ci_uid += 1
                                    for tt in range(2):
                                        W("sync", sm["s_ae"], vapi_used[api_base + tt])
                                        vapi_ready = E("sync", lambda e, p=p, h=h, tt=tt, api_base=api_base: e.dma_start(
                                            out=api_sb[:, (api_base + tt) * 1024:(api_base + tt + 1) * 1024],
                                            in_=apart_d[p, h, :, tt * 1024:(tt + 1) * 1024]), sapi, 16)

                                if head_cache is not None:
                                    vsc_g[0], vsim_g[0] = head_cache[0]
                                    if ng > 1:
                                        vsc_g[1], vsim_g[1] = head_cache[1]
                                    head_cache = None
                                else:
                                    emit_scores(0, h, vsc_g, vsim_g)
                                    emit_relu(0, vsc_g, vsim_g)
                                    if ng > 1:
                                        emit_scores(1, h, vsc_g, vsim_g)
                                        emit_relu(1, vsc_g, vsim_g)

                                last_b01 = max(k for k in range(ng) if k % 4 in (0, 1))
                                for gi in range(ng):
                                    if gi + 2 < ng:
                                        emit_scores(gi + 2, h, vsc_g, vsim_g)
                                        emit_relu(gi + 2, vsc_g, vsim_g)
                                    emit_amm(gi)
                                    if gi == last_b01 and h + 1 < H:
                                        for g4 in range(4):
                                            cand = [vsim_g[k] for k in range(ng) if k % 4 == g4]
                                            if cand:
                                                prev_vsim[g4] = cand[-1]
                                        nvsc = [0] * ng
                                        nvsim = [None] * ng
                                        emit_scores(0, h + 1, nvsc, nvsim)
                                        emit_relu(0, nvsc, nvsim)
                                        hc = [(nvsc[0], nvsim[0])]
                                        if ng > 1:
                                            emit_scores(1, h + 1, nvsc, nvsim)
                                            emit_relu(1, nvsc, nvsim)
                                            hc.append((nvsc[1], nvsim[1]))
                                        head_cache = hc
                                vamm = sm["s_amm"].v
                                if h + 1 >= H:
                                    for gpar2 in range(4):
                                        cand = [vsim_g[k] for k in range(ng) if k % 4 == gpar2]
                                        if cand:
                                            prev_vsim[gpar2] = cand[-1]

                                tileB_has = (u0 < min(NKB, u1))

                                if is_last:
                                    sad0, sad1 = sm["s_adma0"], sm["s_adma1"]
                                    dst, freev = abf_sb, vabf_free
                                else:
                                    sad0, sad1 = sm["s_admb0"], sm["s_admb1"]
                                    dst, freev = apo_sb, vapo_free
                                W("scalar", sm["s_amm"], vamm)
                                W("vector", sm["s_amm"], vamm)
                                if not has_ci:
                                    W("scalar", sad0, freev[0])
                                    v_e0 = E("scalar", lambda e, dst=dst: e.copy(
                                        out=dst[:, 0:1024], in_=ps_a[:, 0:1024]), sm["s_aes"])
                                    W("vector", sad1, freev[1])
                                    v_e1 = E("vector", lambda e, dst=dst: e.tensor_copy(
                                        dst[:, 1024:2048], ps_a[:, 1024:2048]), sm["s_ae"])
                                    W("gpsimd", sm["s_aes"], v_e0)
                                    tgt = a_d if is_last else apart_d
                                    freev[0] = E("gpsimd", lambda e, tgt=tgt, dst=dst, p=p, h=h: e.dma_start(
                                        out=tgt[p, h, :, 0:1024], in_=dst[:, 0:1024]), sad0, 16)
                                    W("gpsimd", sm["s_ae"], v_e1)
                                    freev[1] = E("gpsimd", lambda e, tgt=tgt, dst=dst, p=p, h=h: e.dma_start(
                                        out=tgt[p, h, :, 1024:2048], in_=dst[:, 1024:2048]), sad1, 16)
                                else:
                                    W("scalar", sm["s_ae"], vstg_free)
                                    v_e0 = E("scalar", lambda e: e.copy(
                                        out=stg_sb[:, 0:1024], in_=ps_a[:, 0:1024]), sm["s_aes"])
                                    if tileB_has:
                                        W("vector", sm["s_ae"], vstg_free)
                                        v_e1 = E("vector", lambda e: e.tensor_copy(
                                            stg_sb[:, 1024:2048], ps_a[:, 1024:2048]), sm["s_ae"])
                                    else:
                                        v_e1 = sm["s_ae"].v
                                    W("vector", sapi, vapi_ready)
                                    W("vector", sm["s_aes"], v_e0)
                                    W("vector", sad0, freev[0])
                                    va0 = E("vector", lambda e, dst=dst, api_base=api_base: e.tensor_add(
                                        dst[:, 0:1024], stg_sb[:, 0:1024],
                                        api_sb[:, api_base * 1024:(api_base + 1) * 1024]), sm["s_ae"])
                                    vapi_used[api_base] = va0
                                    W("vector", sad1, freev[1])
                                    if tileB_has:
                                        W("vector", sm["s_ae"], v_e1)
                                        va1 = E("vector", lambda e, dst=dst, api_base=api_base: e.tensor_add(
                                            dst[:, 1024:2048], stg_sb[:, 1024:2048],
                                            api_sb[:, (api_base + 1) * 1024:(api_base + 2) * 1024]), sm["s_ae"])
                                    else:
                                        va1 = E("vector", lambda e, dst=dst, api_base=api_base: e.tensor_copy(
                                            dst[:, 1024:2048],
                                            api_sb[:, (api_base + 1) * 1024:(api_base + 2) * 1024]), sm["s_ae"])
                                    vapi_used[api_base + 1] = va1
                                    vstg_free = va1
                                    W("gpsimd", sm["s_ae"], va0)
                                    tgt = a_d if is_last else apart_d
                                    freev[0] = E("gpsimd", lambda e, tgt=tgt, dst=dst, p=p, h=h: e.dma_start(
                                        out=tgt[p, h, :, 0:1024], in_=dst[:, 0:1024]), sad0, 16)
                                    W("gpsimd", sm["s_ae"], va1)
                                    freev[1] = E("gpsimd", lambda e, tgt=tgt, dst=dst, p=p, h=h: e.dma_start(
                                        out=tgt[p, h, :, 1024:2048], in_=dst[:, 1024:2048]), sad1, 16)
                                vA_free = v_e1 if not has_ci else max(v_e1, 0)
                                vA_free = sm["s_ae"].v if False else vA_free
                                vA_free_s = sm["s_aes"].v
                                vA_free = v_e1
                        if si < 2:
                            kmm_end = sm["s_kmm"].v
                            nv0 = STRIPES[si + 1][0]
                            for bb in range(2):
                                j = nv0 // 4 + bb
                                W("sync", sm["s_kmm"], kmm_end)
                                vfm_pre[(si + 1, bb)] = E("sync", lambda e, bb=bb, j=j: e.dma_start(
                                    out=xnfm2[:, (bb % 2) * 4096:(bb % 2 + 1) * 4096],
                                    in_=xnfm_d[j, :, :]), sm["s_fmld0" if bb % 2 == 0 else "s_fmld1"], 16)
                        for snm in ("s_adma0", "s_adma1", "s_admb0", "s_admb1"):
                            W("gpsimd", sm[snm], sm[snm].v)
                        flush(f"p2_s{si}")

        with ExitStack() as S_f1:
            wgbf = S_f1.enter_context(nc.sbuf_tensor("wgbf", [128, 8 * HID], BF16))
            wubf = S_f1.enter_context(nc.sbuf_tensor("wubf", [128, 8 * HID], BF16))
            xnfm_tb = S_f1.enter_context(nc.sbuf_tensor("xnfm_tb", [128, 2 * 4096], BF16))
            hT_buf = S_f1.enter_context(nc.sbuf_tensor("hT_buf", [128, 24 * 512], BF16))
            silu_sb = S_f1.enter_context(nc.sbuf_tensor("silu_sb", [128, 4 * 512], F32))
            ps_g = S_f1.enter_context(nc.psum_tensor("ps_g", [128, 2048], F32))
            ps_u = S_f1.enter_context(nc.psum_tensor("ps_u", [128, 2048], F32))

            vfwc = [0] * 8
            wg3 = wgT_in.rearrange("(c p) j -> p c j", p=128)
            wu3 = wuT_in.rearrange("(c p) j -> p c j", p=128)
            wgo = wgbf.rearrange("p (c j) -> p c j", c=8)
            wuo = wubf.rearrange("p (c j) -> p c j", c=8)
            for r in range(8):
                E("gpsimd", lambda e, r=r: e.dma_start(
                    out=wgo[:, :, r * 384:(r + 1) * 384],
                    in_=wg3[:, :, r * 384:(r + 1) * 384]), sm[f"s_fwc{r}"], 16)
                vfwc[r] = E("gpsimd", lambda e, r=r: e.dma_start(
                    out=wuo[:, :, r * 384:(r + 1) * 384],
                    in_=wu3[:, :, r * 384:(r + 1) * 384]), sm[f"s_fwc{r}"], 16)

            vhv_seq = []
            vsil_seq = []
            vhtw = [0] * 4
            vgu_tb = [0] * 4
            for tb in range(4):
                par = tb % 2
                xft = xnfm_tb[:, par * 4096:(par + 1) * 4096]
                xft3 = xnfm_tb.rearrange("p (b c q) -> p b c q", b=2, c=8)
                if tb >= 2:
                    W("sync", sm["s_gu"], vgu_tb[tb - 2])
                sft = sm["s_fmtb0" if par == 0 else "s_fmtb1"]
                vld = 0
                for s in range(4):
                    blkj = 15 - (4 * tb + s)
                    src = xnfm_d[blkj].rearrange("p (c q) -> p c q", c=8)[:, :, 384:512]
                    vld = E("sync", lambda e, src=src, xft3=xft3, par=par, s=s: e.dma_start(
                        out=xft3[:, par, :, s * 128:(s + 1) * 128], in_=src), sft, 16)
                W("tensor", sft, vld)
                for jc in range(24):
                    if tb == 0 and jc % 3 == 0:
                        W("tensor", sm[f"s_fwc{jc // 3}"], vfwc[jc // 3])
                    idx = tb * 24 + jc
                    bank = idx % 4
                    if idx >= 4:
                        W("tensor", sm["s_silu"], vsil_seq[idx - 4])
                        W("tensor", sm["s_hv"], vhv_seq[idx - 4])
                    for c in range(8):
                        E("tensor", lambda e, bank=bank, c=c, jc=jc, xft=xft: e.matmul(
                            ps_g[:, bank * 512:(bank + 1) * 512],
                            wgbf[:, c * HID + jc * 128: c * HID + (jc + 1) * 128],
                            xft[:, c * 512:(c + 1) * 512],
                            start=(c == 0), stop=(c == 7)))
                    for c in range(8):
                        r = E("tensor", lambda e, bank=bank, c=c, jc=jc, xft=xft: e.matmul(
                            ps_u[:, bank * 512:(bank + 1) * 512],
                            wubf[:, c * HID + jc * 128: c * HID + (jc + 1) * 128],
                            xft[:, c * 512:(c + 1) * 512],
                            start=(c == 0), stop=(c == 7)),
                            sm["s_gu"] if c == 7 else None)
                        if r:
                            vgu_tb[tb] = r
                    W("scalar", sm["s_gu"], vgu_tb[tb])
                    vsil_seq.append(E("scalar", lambda e, bank=bank: e.activation(
                        out=silu_sb[:, bank * 512:(bank + 1) * 512],
                        in_=ps_g[:, bank * 512:(bank + 1) * 512],
                        func=AF.Silu, scale=INV127), sm["s_silu"]))
                    W("vector", sm["s_silu"], vsil_seq[-1])
                    W("vector", sm["s_gu"], vgu_tb[tb])
                    if jc == 0 and tb >= 1:
                        W("vector", sm["s_htw"], vhtw[tb - 1])
                    vhv_seq.append(E("vector", lambda e, bank=bank, jc=jc: e.scalar_tensor_tensor(
                        out=hT_buf[:, jc * 512:(jc + 1) * 512],
                        in0=ps_u[:, bank * 512:(bank + 1) * 512], scalar=INV127,
                        in1=silu_sb[:, bank * 512:(bank + 1) * 512],
                        op0=ALU.mult, op1=ALU.mult), sm["s_hv"]))
                W("gpsimd", sm["s_hv"], vhv_seq[-1])
                vhtw[tb] = E("gpsimd", lambda e, tb=tb: e.dma_start(
                    out=ht_d[tb, :, :], in_=hT_buf[:, :]), sm["s_htw"], 16)
            W("gpsimd", sm["s_htw"], sm["s_htw"].v)
            flush("f1")

        S_wvx = ExitStack()
        wv32 = S_wvx.enter_context(nc.sbuf_tensor("wv32", [128, 8192], F32))
        wvbf = S_wvx.enter_context(nc.sbuf_tensor("wvbf", [128, 2 * 8192], BF16))
        ident2 = S_wvx.enter_context(nc.sbuf_tensor("ident2", [128, 128], BF16))
        vwvc = [0] * H
        with ExitStack() as S_f2:
            wdbf = S_f2.enter_context(nc.sbuf_tensor("wdbf", [128, 24 * 1024], BF16))
            hT_rd = S_f2.enter_context(nc.sbuf_tensor("hT_rd", [128, 2 * 12288], BF16))
            g_sb = S_f2.enter_context(nc.sbuf_tensor("g_sb", [128, 2 * 1024], F32))
            ps_d = S_f2.enter_context(nc.psum_tensor("ps_d", [128, 2048], F32))

            v_id2 = E("sync", lambda e: e.dma_start(out=ident2[:, :], in_=ident_in[:, :]), sm["s_wvc"], 16)
            vwv0 = E("sync", lambda e: e.dma_start(
                out=wv32.rearrange("p (c v) -> p c v", c=8),
                in_=wvT_in[0].rearrange("(c p) v -> p c v", p=128)), sm["s_wv0"], 16)
            W("vector", sm["s_wv0"], vwv0)
            E("vector", lambda e: e.tensor_copy(wvbf[:, 0:4096], wv32[:, 0:4096]), sm["s_wv1"])
            vwvc[0] = E("vector", lambda e: e.tensor_copy(wvbf[:, 4096:8192], wv32[:, 4096:8192]),
                        sm["s_wv1"])

            vfwd = [0] * 8
            wd3 = wdT_in.rearrange("(c p) d -> p c d", p=128)
            wdo = wdbf.rearrange("p (c d) -> p c d", c=24)
            for r in range(8):
                vfwd[r] = E("gpsimd", lambda e, r=r: e.dma_start(
                    out=wdo[:, r * 3:(r + 1) * 3, :],
                    in_=wd3[:, r * 3:(r + 1) * 3, :]), sm[f"s_fwc{r}"], 16)

            vgs_seq = []
            vgw_seq = []
            vdm_seq = []
            for tb in range(4):
                par = tb % 2
                hrt = hT_rd[:, par * 12288:(par + 1) * 12288]
                if tb >= 2:
                    W("sync", sm["s_dmm"], vdm_seq[4 * (tb - 2) + 3])
                sht = sm["s_htr0" if par == 0 else "s_htr1"]
                vht = E("sync", lambda e, hrt=hrt, tb=tb: e.dma_start(
                    out=hrt[:, :], in_=ht_d[tb, :, :]), sht, 16)
                for s in range(4):
                    u = 4 * tb + s
                    bank = u % 2
                    W("tensor", sht, vht)
                    if u >= 2:
                        W("tensor", sm["s_gs"], vgs_seq[u - 2])
                    vdm = 0
                    for jc in range(24):
                        if u == 0 and jc % 3 == 0:
                            W("tensor", sm[f"s_fwc{jc // 3}"], vfwd[jc // 3])
                        for dh in range(2):
                            r = E("tensor", lambda e, bank=bank, jc=jc, dh=dh, hrt=hrt, s=s: e.matmul(
                                ps_d[:, bank * 1024 + dh * 512: bank * 1024 + (dh + 1) * 512],
                                hrt[:, jc * 512 + s * 128: jc * 512 + (s + 1) * 128],
                                wdbf[:, jc * 1024 + dh * 512: jc * 1024 + (dh + 1) * 512],
                                start=(jc == 0), stop=(jc == 23)),
                                sm["s_dmm"] if (jc == 23 and dh == 1) else None)
                            if r:
                                vdm = r
                    vdm_seq.append(vdm)
                    W("scalar", sm["s_dmm"], vdm)
                    if u >= 2:
                        W("scalar", sm["s_gw0" if u % 2 == 0 else "s_gw1"], vgw_seq[u - 2])
                    vgs = 0
                    for dh in range(2):
                        vgs = E("scalar", lambda e, bank=bank, dh=dh: e.activation(
                            out=g_sb[:, bank * 1024 + dh * 512: bank * 1024 + (dh + 1) * 512],
                            in_=ps_d[:, bank * 1024 + dh * 512: bank * 1024 + (dh + 1) * 512],
                            func=AF.Silu, scale=INV127), sm["s_gs"])
                    vgs_seq.append(vgs)
                    W("gpsimd", sm["s_gs"], vgs)
                    vgw_seq.append(E("gpsimd", lambda e, bank=bank, u=u: e.dma_start(
                        out=g_d[u, :, :], in_=g_sb[:, bank * 1024:(bank + 1) * 1024]),
                        sm["s_gw0" if u % 2 == 0 else "s_gw1"], 16))
            W("gpsimd", sm["s_gw0"], sm["s_gw0"].v)
            W("gpsimd", sm["s_gw1"], sm["s_gw1"].v)
            flush("f2")

        with ExitStack() as S_p3o:
            ho_sb = S_p3o.enter_context(nc.sbuf_tensor("ho_sb", [128, NSLOT * 1024], F32))
            a_rd = S_p3o.enter_context(nc.sbuf_tensor("a_rd", [128, 3 * 1024], BF16))
            at_sb = S_p3o.enter_context(nc.sbuf_tensor("at_sb", [128, 3 * 1024], BF16))
            xn_b = S_p3o.enter_context(nc.sbuf_tensor("xn_b", [128, 2 * 1024], F32))
            g_b = S_p3o.enter_context(nc.sbuf_tensor("g_b", [128, 2 * 1024], F32))
            x_b = S_p3o.enter_context(nc.sbuf_tensor("x_b", [128, 2 * 1024], F32))
            s_b = S_p3o.enter_context(nc.sbuf_tensor("s_b", [128, 2 * 1024], F32))
            o_b = S_p3o.enter_context(nc.sbuf_tensor("o_b", [128, 2 * 1024], F32))
            ps_tr3 = S_p3o.enter_context(nc.psum_tensor("ps_tr3", [128, 3 * 1024], BF16))
            ps_ho = S_p3o.enter_context(nc.psum_tensor("ps_ho", [128, 2048], F32))

            if True:
                UTOT = H * NSLOT
                vard = [0] * UTOT
                vtr3 = [0] * UTOT
                vatc = [0] * UTOT
                vhom = [0] * UTOT
                vhoa = [0] * UTOT
                vev = [0] * NSLOT
                vout = [0] * NSLOT

                vwvl = [vwv0] + [0] * (H - 1)

                def plan_wv(h):
                    W("sync", sm["s_wv1"], vwvc[h - 1])
                    vwvl[h] = E("sync", lambda e, h=h: e.dma_start(
                        out=wv32.rearrange("p (c v) -> p c v", c=8),
                        in_=wvT_in[h].rearrange("(c p) v -> p c v", p=128)),
                        sm["s_wv0"], 16)
                    W("vector", sm["s_wv0"], vwvl[h])
                    if h >= 2:
                        W("vector", sm["s_hom"], vhom[(h - 1) * NSLOT - 1])
                    base = (h % 2) * 8192
                    E("vector", lambda e, base=base: e.tensor_copy(
                        wvbf[:, base:base + 4096], wv32[:, 0:4096]), sm["s_wv1"])
                    vwvc[h] = E("vector", lambda e, base=base: e.tensor_copy(
                        wvbf[:, base + 4096:base + 8192], wv32[:, 4096:8192]), sm["s_wv1"])

                def plan_ard(u):
                    h, i = u // NSLOT, u % NSLOT
                    p, tt = i // 2, i % 2
                    if u >= 3:
                        W("sync", sm["s_tr3"], vtr3[u - 3])
                    vard[u] = E("sync", lambda e, p=p, h=h, tt=tt, u=u: e.dma_start(
                        out=a_rd[:, (u % 3) * 1024:(u % 3 + 1) * 1024],
                        in_=a_d[p, h, :, tt * 1024:(tt + 1) * 1024]),
                        sm[f"s_ard{u % 3}"], 16)

                def plan_tr(u):
                    W("tensor", sm[f"s_ard{u % 3}"], vard[u])
                    if u >= 3:
                        W("tensor", sm["s_atc"], vatc[u - 3])
                    if u == 0:
                        W("tensor", sm["s_wvc"], 16)
                    for c in range(8):
                        r = E("tensor", lambda e, c=c, u=u: e.transpose(
                            ps_tr3[:, (u % 3) * 1024 + c * 128: (u % 3) * 1024 + (c + 1) * 128],
                            a_rd[:, (u % 3) * 1024 + c * 128: (u % 3) * 1024 + (c + 1) * 128],
                            ident2[:, :]), sm["s_tr3"] if c == 7 else None)
                        if r:
                            vtr3[u] = r

                def plan_atc(u):
                    W("scalar", sm["s_tr3"], vtr3[u])
                    vatc[u] = E("scalar", lambda e, u=u: e.copy(
                        out=at_sb[:, (u % 3) * 1024:(u % 3 + 1) * 1024],
                        in_=ps_tr3[:, (u % 3) * 1024:(u % 3 + 1) * 1024]), sm["s_atc"])

                def plan_mm(u):
                    h = u // NSLOT
                    W("tensor", sm["s_atc"], vatc[u])
                    if (u % NSLOT) == 0:
                        W("tensor", sm["s_wv1"], vwvc[h])
                    if u >= 2:
                        W("tensor", sm["s_hoa"], vhoa[u - 2])
                    for c in range(8):
                        for vh in range(2):
                            r = E("tensor", lambda e, c=c, vh=vh, u=u, h=h: e.matmul(
                                ps_ho[:, (u % 2) * 1024 + vh * 512: (u % 2) * 1024 + vh * 512 + 512],
                                at_sb[:, (u % 3) * 1024 + c * 128: (u % 3) * 1024 + (c + 1) * 128],
                                wvbf[:, (h % 2) * 8192 + c * 1024 + vh * 512: (h % 2) * 8192 + c * 1024 + vh * 512 + 512],
                                start=(c == 0), stop=(c == 7)),
                                sm["s_hom"] if (c == 7 and vh == 1) else None)
                            if r:
                                vhom[u] = r

                def plan_add(u):
                    h, i = u // NSLOT, u % NSLOT
                    W("vector", sm["s_hom"], vhom[u])
                    if h == 0:
                        vhoa[u] = E("vector", lambda e, u=u, i=i: e.tensor_copy(
                            ho_sb[:, i * 1024:(i + 1) * 1024],
                            ps_ho[:, (u % 2) * 1024:(u % 2 + 1) * 1024]), sm["s_hoa"])
                    else:
                        vhoa[u] = E("vector", lambda e, u=u, i=i: e.tensor_add(
                            ho_sb[:, i * 1024:(i + 1) * 1024],
                            ps_ho[:, (u % 2) * 1024:(u % 2 + 1) * 1024],
                            ho_sb[:, i * 1024:(i + 1) * 1024]), sm["s_hoa"])

                def plan_epi(i):
                    par = i % 2
                    ser = sm["s_er0" if par == 0 else "s_er1"]
                    sou = sm["s_out0" if par == 0 else "s_out1"]
                    if i >= 2:
                        W("sync", sm["s_ev"], vev[i - 2])
                    E("sync", lambda e, i=i, par=par: e.dma_start(
                        out=xn_b[:, par * 1024:(par + 1) * 1024], in_=xnf32_d[i, :, :]), ser, 16)
                    E("sync", lambda e, i=i, par=par: e.dma_start(
                        out=g_b[:, par * 1024:(par + 1) * 1024], in_=g_d[i, :, :]), ser, 16)
                    ver = E("sync", lambda e, i=i, par=par: e.dma_start(
                        out=x_b[:, par * 1024:(par + 1) * 1024], in_=x3[63 - 4 * i, :, :]), ser, 16)

                    W("vector", ser, ver)
                    W("vector", sm["s_hoa"], vhoa_slot_final[i])
                    if i >= 2:
                        W("vector", sou, vout[i - 2])
                    v1 = E("vector", lambda e, i=i, par=par: e.tensor_add(
                        s_b[:, par * 1024:(par + 1) * 1024],
                        xn_b[:, par * 1024:(par + 1) * 1024],
                        ho_sb[:, i * 1024:(i + 1) * 1024]), sm["s_ev"])
                    W("vector", sm["s_ev"], v1)
                    v2 = E("vector", lambda e, par=par: e.tensor_mul(
                        s_b[:, par * 1024:(par + 1) * 1024],
                        s_b[:, par * 1024:(par + 1) * 1024],
                        g_b[:, par * 1024:(par + 1) * 1024]), sm["s_ev"])
                    W("vector", sm["s_ev"], v2)
                    vev[i] = E("vector", lambda e, par=par: e.tensor_add(
                        o_b[:, par * 1024:(par + 1) * 1024],
                        s_b[:, par * 1024:(par + 1) * 1024],
                        x_b[:, par * 1024:(par + 1) * 1024]), sm["s_ev"])
                    W("gpsimd", sm["s_ev"], vev[i])
                    vout[i] = E("gpsimd", lambda e, i=i, par=par: e.dma_start(
                        out=y_out[i * 128:(i + 1) * 128, :],
                        in_=o_b[:, par * 1024:(par + 1) * 1024]), sou, 16)

                vhoa_slot_final = [0] * NSLOT

                def maybe_epi(u):
                    h, i = u // NSLOT, u % NSLOT
                    if h == H - 1:
                        vhoa_slot_final[i] = vhoa[u]
                        plan_epi(i)

                plan_ard(0)
                plan_tr(0)
                plan_ard(1)
                plan_tr(1)
                for u in range(UTOT):
                    if u % NSLOT == 0 and u > 0:
                        plan_wv(u // NSLOT)
                    if u + 2 < UTOT:
                        plan_ard(u + 2)
                        plan_tr(u + 2)
                    plan_atc(u)
                    plan_mm(u)
                    plan_add(u)
                    maybe_epi(u)
                W("gpsimd", sm["s_out0"], sm["s_out0"].v)
                W("gpsimd", sm["s_out1"], sm["s_out1"].v)
                flush("p3")
        S_wvx.close()


    return nc


_NC_CACHE = None


def _get_nc():
    global _NC_CACHE
    if _NC_CACHE is None:
        _NC_CACHE = build_nc()
    return _NC_CACHE


def kernel(L0_attn_x, L0_attn_Wq, L0_attn_Wk, L0_attn_Wv,
           L0_ffn_w_gate, L0_ffn_w_up, L0_ffn_w_down):
    x = np.asarray(L0_attn_x, dtype=np.float32)
    Wq = np.asarray(L0_attn_Wq, dtype=np.float32)
    Wk = np.asarray(L0_attn_Wk, dtype=np.float32)
    Wv = np.asarray(L0_attn_Wv, dtype=np.float32)
    wg = np.asarray(L0_ffn_w_gate, dtype=np.int32)
    wu = np.asarray(L0_ffn_w_up, dtype=np.int32)
    wd = np.asarray(L0_ffn_w_down, dtype=np.int32)

    wqT = np.ascontiguousarray(np.transpose(Wq, (2, 0, 1)).reshape(D, H * DK))
    wkT = np.ascontiguousarray(np.transpose(Wk, (2, 0, 1)).reshape(D, H * DK))
    wvT = np.ascontiguousarray(np.transpose(Wv, (0, 2, 1)))
    wgT = np.ascontiguousarray(wg.T)
    wuT = np.ascontiguousarray(wu.T)
    wdT = np.ascontiguousarray(wd.T)
    ident = np.eye(128, dtype=ml_dtypes.bfloat16)

    in_maps = []
    metas = []
    for c in range(8):
        b, cc = c // 4, c % 4
        loc2glob = np.array([4 * (lt // 4) + ((lt % 4) + cc + 1) % 4 for lt in range(T)])
        x_loc = np.ascontiguousarray(
            x[b].reshape(T, 128, D)[loc2glob].reshape(N, D))
        kglob = (loc2glob[None, :] * 128 + np.arange(128)[:, None]).astype(np.float32)
        qglob = np.zeros((NPAIR, 128, 256), dtype=np.float32)
        tg = [60 - 4 * i + cc for i in range(NSLOT)]
        for p in range(NPAIR):
            qglob[p, :, 0:128] = tg[2 * p] * 128 + np.arange(128)[None, :]
            qglob[p, :, 128:256] = tg[2 * p + 1] * 128 + np.arange(128)[None, :]
        in_maps.append({
            "x": x_loc, "wqT": wqT, "wkT": wkT, "wvT": wvT,
            "wgT": wgT, "wuT": wuT, "wdT": wdT,
            "ident": ident, "kglob": kglob, "qglob": qglob,
        })
        metas.append((b, tg))

    nc = _get_nc()
    res = run_bass_kernel_spmd(nc, in_maps, core_ids=list(range(8)))

    out = np.zeros((B, N, D), dtype=np.float32)
    for c in range(8):
        b, tg = metas[c]
        yc = np.asarray(res.results[c]["y"], dtype=np.float32)
        for i in range(NSLOT):
            out[b, tg[i] * 128:(tg[i] + 1) * 128, :] = yc[i * 128:(i + 1) * 128, :]
    return out


# version 17
# speedup vs baseline: 1.0057x; 1.0057x over previous
import numpy as np
import ml_dtypes
from contextlib import ExitStack

import concourse.bass as bass
import concourse.mybir as mybir
from concourse.bass_utils import run_bass_kernel_spmd

F32 = mybir.dt.float32
BF16 = mybir.dt.bfloat16
I32 = mybir.dt.int32
AF = mybir.ActivationFunctionType
ALU = mybir.AluOpType
AX = mybir.AxisListType.X

B, N, D, H, DK, HID = 2, 8192, 1024, 8, 128, 3072
T = 64
NSLOT = 16
NPAIR = 8
STRIPES = [(0, 24), (24, 44), (44, 64)]
INV_N = 1.0 / float(N)
INV127 = 1.0 / 127.0
EPS = 1e-5

ENGS = ("sync", "tensor", "scalar", "vector", "gpsimd")


class Sem:
    def __init__(self, ctx, nc, name):
        self.h = ctx.enter_context(nc.semaphore(name))
        self.v = 0


def build_nc():
    nc = bass.Bass("TRN2", target_bir_lowering=False)

    x_in = nc.dram_tensor("x", [N, D], F32, kind="ExternalInput").ap()
    wqT_in = nc.dram_tensor("wqT", [D, H * DK], F32, kind="ExternalInput").ap()
    wkT_in = nc.dram_tensor("wkT", [D, H * DK], F32, kind="ExternalInput").ap()
    wvT_in = nc.dram_tensor("wvT", [H, D, D], F32, kind="ExternalInput").ap()
    wgT_in = nc.dram_tensor("wgT", [D, HID], I32, kind="ExternalInput").ap()
    wuT_in = nc.dram_tensor("wuT", [D, HID], I32, kind="ExternalInput").ap()
    wdT_in = nc.dram_tensor("wdT", [HID, D], I32, kind="ExternalInput").ap()
    ident_in = nc.dram_tensor("ident", [128, 128], BF16, kind="ExternalInput").ap()
    kglob_in = nc.dram_tensor("kglob", [128, T], F32, kind="ExternalInput").ap()
    qglob_in = nc.dram_tensor("qglob", [NPAIR, 128, 256], F32, kind="ExternalInput").ap()
    y_out = nc.dram_tensor("y", [NSLOT * 128, D], F32, kind="ExternalOutput").ap()

    xnkm_d = nc.dram_tensor("xnkm_d", [T, 128, D], BF16, kind="Internal").ap()
    xnfm_d = nc.dram_tensor("xnfm_d", [16, 128, 4096], BF16, kind="Internal").ap()
    xnf32_d = nc.dram_tensor("xnf32_d", [NSLOT, 128, D], F32, kind="Internal").ap()
    apart_d = nc.dram_tensor("apart_d", [NPAIR, H, 128, 2048], F32, kind="Internal").ap()
    a_d = nc.dram_tensor("a_d", [NPAIR, H, 128, 2048], BF16, kind="Internal").ap()
    ht_d = nc.dram_tensor("ht_d", [4, 128, 24 * 512], BF16, kind="Internal").ap()
    g_d = nc.dram_tensor("g_d", [NSLOT, 128, D], F32, kind="Internal").ap()

    x3 = x_in.rearrange("(t p) d -> t p d", p=128)

    cur = {k: [] for k in ENGS}

    def W(eng, sem, val):
        if val > 0:
            cur[eng].append(lambda e, s=sem.h, v=val: e.wait_ge(s, v))

    def E(eng, fn, sem=None, n=1):
        if sem is None:
            cur[eng].append(fn)
            return None
        sem.v += n
        cur[eng].append(lambda e, f=fn, s=sem.h, m=n: f(e).then_inc(s, m))
        return sem.v

    def flush(name):
        nonlocal cur
        bodies = cur
        cur = {k: [] for k in ENGS}
        with nc.Block(name) as block:
            for eng in ENGS:
                ops = bodies[eng]
                if ops:
                    getattr(block, eng)(lambda engine, ops=ops: [f(engine) for f in ops])

    with ExitStack() as S0:
        cbank = S0.enter_context(nc.sbuf_tensor("cbank", [128, 4], F32))
        kglob_sb = S0.enter_context(nc.sbuf_tensor("kglob_sb", [128, T], F32))
        nc.const_aps.register(F32, EPS, cbank[:, 0:1])
        nc.const_aps.register(F32, INV_N, cbank[:, 1:2])
        nc.const_aps.register(F32, INV127, cbank[:, 2:3])

        sem_names = [
            "s_x", "s_x0", "s_x1", "s_x2", "s_n", "s_ng", "s_red", "s_rcp", "s_sq", "s_cols", "s_tr1", "s_fme",
            "s_w10", "s_w11", "s_w12",
            "s_qpld0", "s_qpld1", "s_qmm", "s_qe", "s_wc",
            "s_kmld", "s_fmld0", "s_fmld1", "s_kmm", "s_ke", "s_sc", "s_sim", "s_vm",
            "s_amm", "s_ae", "s_aes", "s_adma0", "s_adma1", "s_admb0", "s_admb1", "s_api0", "s_api1", "s_qgl0", "s_qgl1",
            "s_fw", "s_fwc0", "s_fwc1", "s_fwc2", "s_fwc3", "s_fwc4", "s_fwc5",
            "s_fwc6", "s_fwc7",
            "s_fmtb0", "s_fmtb1", "s_gu", "s_silu", "s_hv", "s_htw",
            "s_htr0", "s_htr1", "s_dmm", "s_gs", "s_gw0", "s_gw1",
            "s_wv0", "s_wv1", "s_wvc", "s_ard0", "s_ard1", "s_ard2", "s_tr3", "s_atc", "s_hom", "s_hoa",
            "s_er0", "s_er1", "s_ev", "s_out0", "s_out1",
        ]
        sm = {n: Sem(S0, nc, n) for n in sem_names}

        with ExitStack() as S_qt:
            qT_sb = S_qt.enter_context(nc.sbuf_tensor("qT_sb", [128, H * NSLOT * 128], BF16))
            with ExitStack() as S_wk:
                wkbf = S_wk.enter_context(nc.sbuf_tensor("wkbf", [128, 8 * H * DK], BF16))
                with ExitStack() as S_wq:
                    wqbf = S_wq.enter_context(nc.sbuf_tensor("wqbf", [128, 8 * H * DK], BF16))
                    ident1 = S_wq.enter_context(nc.sbuf_tensor("ident1", [128, 128], BF16))

                    with ExitStack() as S_i:
                        stage32 = S_i.enter_context(nc.sbuf_tensor("stage32", [128, 8192], F32))

                        E("gpsimd", lambda e: e.memset(cbank[:, 0:1], EPS))
                        E("gpsimd", lambda e: e.memset(cbank[:, 1:2], INV_N))
                        E("gpsimd", lambda e: e.memset(cbank[:, 2:3], INV127))
                        v_id = E("sync", lambda e: e.dma_start(out=ident1[:, :], in_=ident_in[:, :]), sm["s_x"], 16)
                        E("sync", lambda e: e.dma_start(out=kglob_sb[:, :], in_=kglob_in[:, :]), sm["s_x"], 16)
                        wq3 = wqT_in.rearrange("(c p) e -> p c e", p=128)
                        v_wq = E("sync", lambda e, wq3=wq3: e.dma_start(
                            out=stage32.rearrange("p (c e) -> p c e", c=8), in_=wq3), sm["s_x"], 16)
                        W("scalar", sm["s_x"], v_wq)
                        v_cq = E("scalar", lambda e: e.activation(out=wqbf[:, :], in_=stage32[:, :], func=AF.Copy),
                                 sm["s_wc"])
                        W("sync", sm["s_wc"], v_cq)
                        wk3 = wkT_in.rearrange("(c p) e -> p c e", p=128)
                        v_wk = E("sync", lambda e, wk3=wk3: e.dma_start(
                            out=stage32.rearrange("p (c e) -> p c e", c=8), in_=wk3), sm["s_x"], 16)
                        W("scalar", sm["s_x"], v_wk)
                        E("scalar", lambda e: e.activation(out=wkbf[:, :], in_=stage32[:, :], func=AF.Copy),
                          sm["s_wc"])
                        flush("init")

                    with ExitStack() as S_p1:
                        xstage = S_p1.enter_context(nc.sbuf_tensor("xstage", [128, 3 * 4096], F32))
                        xnkm_b = S_p1.enter_context(nc.sbuf_tensor("xnkm_b", [128, 3 * 4096], BF16))
                        xnfm_b = S_p1.enter_context(nc.sbuf_tensor("xnfm_b", [128, 3 * 4096], BF16))
                        xf32_b = S_p1.enter_context(nc.sbuf_tensor("xf32_b", [128, 3 * 1024], F32))
                        cols = S_p1.enter_context(nc.sbuf_tensor("cols", [128, 3 * 24], F32))
                        dump = S_p1.enter_context(nc.sbuf_tensor("dump", [128, 4 * 1024], F32))
                        qp_buf = S_p1.enter_context(nc.sbuf_tensor("qp_buf", [128, 2 * 4096], BF16))
                        ps_t = S_p1.enter_context(nc.psum_tensor("ps_t", [128, 4096], BF16))
                        ps_q = S_p1.enter_context(nc.psum_tensor("ps_q", [128, 1024], F32))

                        vn = [0] * 16
                        vng = [0] * 16
                        vred = [0] * 16
                        vsq = [0] * 16
                        vtr_last = [0] * 16
                        vfme = [0] * 16
                        vw_blk = [0] * 16
                        vfme_slot = {}
                        vqmm_g = [0] * 4
                        vqe_seq = []

                        def emit_qgroup(g):
                            par = g % 2
                            qpb = qp_buf[:, par * 4096:(par + 1) * 4096]
                            qpb3 = qp_buf.rearrange("p (b c q) -> p b c q", b=2, c=8)
                            if g >= 2:
                                W("sync", sm["s_qmm"], vqmm_g[g - 2])
                            sqp = sm["s_qpld0" if par == 0 else "s_qpld1"]
                            for pp in (0, 1, 2):
                                blks = [15 - (4 * g + s) for s in range(4) if (15 - (4 * g + s)) % 3 == pp]
                                if blks:
                                    W("sync", sm[f"s_w1{pp}"], max(vw_blk[b] for b in blks))
                            vld = 0
                            for s in range(4):
                                blkj = 15 - (4 * g + s)
                                srcq = xnfm_d[blkj].rearrange("p (c q) -> p c q", c=8)[:, :, 384:512]
                                vld = E("sync", lambda e, srcq=srcq, qpb3=qpb3, par=par, s=s: e.dma_start(
                                    out=qpb3[:, par, :, s * 128:(s + 1) * 128], in_=srcq), sqp, 16)
                            W("tensor", sqp, vld)
                            for h in range(H):
                                gidx = g * H + h
                                hp = gidx % 2
                                if gidx >= 2:
                                    W("tensor", sm["s_qe"], vqe_seq[gidx - 2])
                                for c in range(8):
                                    r = E("tensor", lambda e, hp=hp, c=c, h=h, qpb=qpb: e.matmul(
                                        ps_q[:, hp * 512:(hp + 1) * 512],
                                        wqbf[:, c * 1024 + h * 128: c * 1024 + (h + 1) * 128],
                                        qpb[:, c * 512:(c + 1) * 512],
                                        start=(c == 0), stop=(c == 7)),
                                        sm["s_qmm"] if c == 7 else None)
                                    if r:
                                        vqmm_g[g] = r
                                W("scalar", sm["s_qmm"], vqmm_g[g])
                                vqe_seq.append(E("scalar", lambda e, hp=hp, h=h, g=g: e.copy(
                                    out=qT_sb[:, h * 2048 + g * 512: h * 2048 + (g + 1) * 512],
                                    in_=ps_q[:, hp * 512:(hp + 1) * 512]), sm["s_qe"]))

                        def emit_A(j):
                            idx = 15 - j
                            par = j % 3
                            xs = xstage[:, par * 4096:(par + 1) * 4096]
                            co = cols[:, par * 24:(par + 1) * 24]
                            if idx >= 3:
                                W("sync", sm["s_n"], vn[j + 3])
                                W("sync", sm["s_ng"], vng[j + 3])
                                W("sync", sm["s_sq"], vsq[j + 3])
                            sx = sm[f"s_x{par}"]
                            vx = E("sync", lambda e, xs=xs, j=j: e.dma_start(
                                out=xs.rearrange("p (t d) -> p t d", t=4),
                                in_=x3[4 * j:4 * j + 4, :, :].transpose([1, 0, 2])), sx, 16)

                            W("vector", sx, vx)
                            for t in range(4):
                                E("vector", lambda e, xs=xs, co=co, t=t: e.reduce_sum(
                                    co[:, t:t + 1], xs[:, t * 1024:(t + 1) * 1024], axis=AX),
                                    sm["s_red"] if t == 3 else None)
                            vred[j] = sm["s_red"].v

                            W("scalar", sx, vx)
                            if idx >= 1:
                                W("scalar", sm["s_sq"], vsq[j + 1])
                            for t in range(4):
                                r = E("scalar", lambda e, xs=xs, co=co, t=t: e.activation(
                                    out=dump[:, t * 1024:(t + 1) * 1024], in_=xs[:, t * 1024:(t + 1) * 1024],
                                    func=AF.Square, accum_out=co[:, 4 + t:5 + t]),
                                    sm["s_sq"] if t == 3 else None)
                                if r:
                                    vsq[j] = r

                        def emit_BC(j):
                            idx = 15 - j
                            par = j % 3
                            bpar = j % 2
                            xs = xstage[:, par * 4096:(par + 1) * 4096]
                            xkb = xnkm_b[:, par * 4096:(par + 1) * 4096]
                            xfb = xnfm_b[:, par * 4096:(par + 1) * 4096]
                            co = cols[:, par * 24:(par + 1) * 24]
                            W("gpsimd", sm["s_sq"], vsq[j])
                            W("gpsimd", sm["s_red"], vred[j])
                            E("gpsimd", lambda e, co=co: e.tensor_scalar_mul(co[:, 8:12], co[:, 0:4], -1.0 / D))
                            v1 = E("gpsimd", lambda e, co=co: e.tensor_scalar_mul(co[:, 12:16], co[:, 4:8], 1.0 / D),
                                   sm["s_cols"])
                            W("gpsimd", sm["s_cols"], v1)
                            v2 = E("gpsimd", lambda e, co=co: e.tensor_mul(
                                co[:, 16:20], co[:, 8:12], co[:, 8:12]), sm["s_cols"])
                            W("gpsimd", sm["s_cols"], v2)
                            v3 = E("gpsimd", lambda e, co=co: e.tensor_sub(co[:, 16:20], co[:, 12:16], co[:, 16:20]),
                                   sm["s_cols"])
                            W("scalar", sm["s_cols"], v3)
                            v4 = E("scalar", lambda e, co=co: e.activation(
                                out=co[:, 20:24], in_=co[:, 16:20], func=AF.Sqrt, bias=EPS), sm["s_sq"])
                            vsq[j] = v4
                            W("vector", sm["s_sq"], v4)
                            v5 = E("vector", lambda e, co=co: e.reciprocal(co[:, 20:24], co[:, 20:24]),
                                   sm["s_rcp"])
                            W("gpsimd", sm["s_rcp"], v5)
                            v6 = E("gpsimd", lambda e, co=co: e.tensor_mul(
                                co[:, 16:20], co[:, 8:12], co[:, 20:24]), sm["s_cols"])
                            W("gpsimd", sm["s_cols"], v6)

                            if idx >= 3:
                                W("vector", sm[f"s_w1{par}"], vw_blk[j + 3])
                                W("vector", sm["s_tr1"], vtr_last[j + 3])
                            W("vector", sm["s_cols"], sm["s_cols"].v)
                            W("vector", sm["s_rcp"], v5)
                            for t in (0, 1, 2, 3):
                                vng[j] = E("vector", lambda e, xs=xs, xkb=xkb, co=co, t=t: e.tensor_scalar(
                                    out=xkb[:, t * 1024:(t + 1) * 1024], in0=xs[:, t * 1024:(t + 1) * 1024],
                                    scalar1=co[:, 20 + t:21 + t], scalar2=co[:, 16 + t:17 + t],
                                    op0=ALU.mult, op1=ALU.add), sm["s_ng"])
                            vn[j] = E("vector", lambda e, xs=xs, co=co, par=par: e.tensor_scalar(
                                out=xf32_b[:, par * 1024:(par + 1) * 1024], in0=xs[:, 3 * 1024:4 * 1024],
                                scalar1=co[:, 23:24], scalar2=co[:, 19:20],
                                op0=ALU.mult, op1=ALU.add), sm["s_n"])

                            W("tensor", sm["s_n"], vn[j])
                            W("tensor", sm["s_ng"], vng[j])
                            if idx == 0:
                                W("tensor", sm["s_x"], 48)
                            for c in range(8):
                                slot = 2 * bpar + (c % 2)
                                if slot in vfme_slot:
                                    W("tensor", sm["s_fme"], vfme_slot[slot])
                                for t in range(4):
                                    r = E("tensor", lambda e, xkb=xkb, c=c, t=t, slot=slot: e.transpose(
                                        ps_t[:, slot * 1024 + t * 128: slot * 1024 + (t + 1) * 128],
                                        xkb[:, t * 1024 + c * 128: t * 1024 + (c + 1) * 128],
                                        ident1[:, :]), sm["s_tr1"] if t == 3 else None)
                                    if r:
                                        vtr_last[j] = r
                                W("scalar", sm["s_tr1"], vtr_last[j])
                                if c == 0 and idx >= 3:
                                    W("scalar", sm[f"s_w1{par}"], vw_blk[j + 3])
                                vfme_slot[slot] = E("scalar", lambda e, xfb=xfb, c=c, slot=slot: e.copy(
                                    out=xfb[:, c * 512:(c + 1) * 512],
                                    in_=ps_t[:, slot * 1024:slot * 1024 + 512]), sm["s_fme"])
                            vfme[j] = sm["s_fme"].v

                            W("gpsimd", sm["s_n"], vn[j])
                            W("gpsimd", sm["s_ng"], vng[j])
                            sw = sm[f"s_w1{par}"]
                            E("gpsimd", lambda e, xkb=xkb, j=j: e.dma_start(
                                out=xnkm_d[4 * j:4 * j + 4, :, :].transpose([1, 0, 2]),
                                in_=xkb.rearrange("p (t d) -> p t d", t=4)), sw, 16)
                            E("gpsimd", lambda e, par=par, so=15 - j: e.dma_start(
                                out=xnf32_d[so, :, :],
                                in_=xf32_b[:, par * 1024:(par + 1) * 1024]), sw, 16)
                            W("gpsimd", sm["s_fme"], vfme[j])
                            vw_blk[j] = E("gpsimd", lambda e, xfb=xfb, j=j: e.dma_start(
                                out=xnfm_d[j, :, :], in_=xfb[:, :]), sw, 16)

                            if j % 4 == 0:
                                emit_qgroup((12 - j) // 4)

                        for k in range(17):
                            if k < 16:
                                emit_A(15 - k)
                            if k >= 1:
                                emit_BC(15 - (k - 1))
                        W("gpsimd", sm["s_w10"], sm["s_w10"].v)
                        W("gpsimd", sm["s_w11"], sm["s_w11"].v)
                        W("gpsimd", sm["s_w12"], sm["s_w12"].v)
                        flush("p1")

                with ExitStack() as S_p2:
                    xnkm_s = S_p2.enter_context(nc.sbuf_tensor("xnkm_s", [128, 24 * 1024], BF16))
                    kT_s = S_p2.enter_context(nc.sbuf_tensor("kT_s", [128, H * 24 * 128], BF16))
                    xnfm2 = S_p2.enter_context(nc.sbuf_tensor("xnfm2", [128, 2 * 4096], BF16))
                    sim_sb = S_p2.enter_context(nc.sbuf_tensor("sim_sb", [128, 4 * 512], BF16))
                    mask_sb = S_p2.enter_context(nc.sbuf_tensor("mask_sb", [128, 8 * 256], BF16))
                    qgl_sb = S_p2.enter_context(nc.sbuf_tensor("qgl_sb", [128, 2 * 256], F32))
                    abf_sb = S_p2.enter_context(nc.sbuf_tensor("abf_sb", [128, 2 * 1024], BF16))
                    apo_sb = S_p2.enter_context(nc.sbuf_tensor("apo_sb", [128, 2 * 1024], F32))
                    api_sb = S_p2.enter_context(nc.sbuf_tensor("api_sb", [128, 4 * 1024], F32))
                    stg_sb = S_p2.enter_context(nc.sbuf_tensor("stg_sb", [128, 2 * 1024], F32))
                    ps_a = S_p2.enter_context(nc.psum_tensor("ps_a", [128, 2048], F32))
                    ps_s = S_p2.enter_context(nc.psum_tensor("ps_s", [128, 2048], F32))
                    ps_k = ps_s[:, 1024:2048]

                    def pairs_in(si):
                        return [p for p in range(NPAIR) if 64 - 8 * p > STRIPES[si][0]]

                    last_stripe_of = {}
                    for p in range(NPAIR):
                        for si in range(3):
                            if p in pairs_in(si):
                                last_stripe_of[p] = si

                    vA_free = 0
                    vA_free_s = 0
                    vstg_free = 0
                    vabf_free = [0, 0]
                    vapo_free = [0, 0]
                    vapi_used = [0, 0, 0, 0]
                    ci_uid = 0
                    vqgl_used = [0, 0]
                    prev_vsim = [None, None, None, None]
                    vfm_pre = {}

                    for si in range(3):
                        v0, v1 = STRIPES[si]
                        nblk = (v1 - v0) // 4
                        prs = pairs_in(si)

                        kmm_base = sm["s_kmm"].v
                        vkm_last = 0
                        vkm_blk = [0] * nblk
                        vfm_blk = [0] * nblk
                        for bb in range(nblk):
                            j = v0 // 4 + bb
                            vkm_last = E("sync", lambda e, bb=bb, j=j: e.dma_start(
                                out=xnkm_s.rearrange("p (t d) -> p t d", t=24)[:, 4 * bb:4 * bb + 4, :],
                                in_=xnkm_d[4 * j:4 * j + 4, :, :].transpose([1, 0, 2])), sm["s_kmld"], 16)
                            vkm_blk[bb] = vkm_last
                        qgl_pre = {}
                        order_pairs = sorted(prs, key=lambda q: min(64 - 8 * q, v1))
                        seen_par = set()
                        for p in order_pairs:
                            NKAp = 64 - 8 * p
                            u1p = min(NKAp, v1)
                            tl = [v for v in range(v0, u1p) if v >= NKAp - 8]
                            if tl and (p % 2) not in seen_par:
                                seen_par.add(p % 2)
                                sqg = sm["s_qgl0" if p % 2 == 0 else "s_qgl1"]
                                W("sync", sm["s_vm"], vqgl_used[p % 2])
                                qgl_pre[p] = E("sync", lambda e, p=p: e.dma_start(
                                    out=qgl_sb[:, (p % 2) * 256:(p % 2 + 1) * 256],
                                    in_=qglob_in[p, :, :]), sqg, 16)
                        for bb in range(min(nblk, 2)):
                            j = v0 // 4 + bb
                            if (si, bb) in vfm_pre:
                                vfm_blk[bb] = vfm_pre[(si, bb)]
                                continue
                            vfm_blk[bb] = E("sync", lambda e, bb=bb, j=j: e.dma_start(
                                out=xnfm2[:, (bb % 2) * 4096:(bb % 2 + 1) * 4096],
                                in_=xnfm_d[j, :, :]), sm["s_fmld0" if bb % 2 == 0 else "s_fmld1"], 16)

                        vke_seq = []

                        def emit_kblock(bb):
                            W("tensor", sm["s_fmld0" if bb % 2 == 0 else "s_fmld1"], vfm_blk[bb])
                            for pv in (2, 3):
                                if prev_vsim[pv] is not None:
                                    sh, sv = prev_vsim[pv]
                                    W("tensor", sh, sv)
                            for h in range(H):
                                idx = bb * H + h
                                bank = idx % 2
                                if idx >= 2:
                                    W("tensor", sm["s_ke"], vke_seq[idx - 2])
                                vkmm = 0
                                for c in range(8):
                                    r = E("tensor", lambda e, bank=bank, c=c, h=h, bb=bb: e.matmul(
                                        ps_k[:, bank * 512:(bank + 1) * 512],
                                        wkbf[:, c * 1024 + h * 128: c * 1024 + (h + 1) * 128],
                                        xnfm2[:, (bb % 2) * 4096 + c * 512:(bb % 2) * 4096 + (c + 1) * 512],
                                        start=(c == 0), stop=(c == 7)),
                                        sm["s_kmm"] if c == 7 else None)
                                    if r:
                                        vkmm = r
                                W("scalar", sm["s_kmm"], vkmm)
                                vke_seq.append(E("scalar", lambda e, bank=bank, h=h, bb=bb: e.copy(
                                    out=kT_s[:, h * 3072 + bb * 512: h * 3072 + (bb + 1) * 512],
                                    in_=ps_k[:, bank * 512:(bank + 1) * 512]), sm["s_ke"]))
                            if bb + 2 < nblk:
                                j2 = v0 // 4 + bb + 2
                                W("sync", sm["s_kmm"], sm["s_kmm"].v)
                                vfm_blk[bb + 2] = E("sync", lambda e, bb=bb, j2=j2: e.dma_start(
                                    out=xnfm2[:, (bb % 2) * 4096:(bb % 2 + 1) * 4096],
                                    in_=xnfm_d[j2, :, :]), sm["s_fmld0" if bb % 2 == 0 else "s_fmld1"], 16)

                        bb_done = 0
                        for p in sorted(prs, key=lambda q: min(64 - 8 * q, v1)):
                            NKA = 64 - 8 * p
                            NKB = NKA - 4
                            u0, u1 = v0, min(NKA, v1)
                            is_last = (si == last_stripe_of[p])
                            has_ci = (si > 0)

                            need_blocks = (u1 - v0 + 3) // 4
                            while bb_done < need_blocks:
                                emit_kblock(bb_done)
                                bb_done += 1
                            vke_pair = vke_seq[need_blocks * H - 1]
                            vkm_pair = vkm_last
                            pair_first = True

                            tail_vis = [v for v in range(u0, u1) if v >= NKB - 4]
                            vmask_p = 0
                            if tail_vis:
                                sqg = sm["s_qgl0" if p % 2 == 0 else "s_qgl1"]
                                if p in qgl_pre:
                                    vqgl_p = qgl_pre[p]
                                else:
                                    W("sync", sm["s_vm"], vqgl_used[p % 2])
                                    vqgl_p = E("sync", lambda e, p=p: e.dma_start(
                                        out=qgl_sb[:, (p % 2) * 256:(p % 2 + 1) * 256],
                                        in_=qglob_in[p, :, :]), sqg, 16)
                                W("vector", sqg, vqgl_p)
                                for idx, v in enumerate(tail_vis):
                                    wid = 256 if v < NKB else 128
                                    vmask_p = E("vector", lambda e, idx=idx, v=v, p=p, wid=wid: e.tensor_scalar(
                                        out=mask_sb[:, idx * 256: idx * 256 + wid],
                                        in0=qgl_sb[:, (p % 2) * 256:(p % 2) * 256 + wid],
                                        scalar1=kglob_sb[:, v:v + 1], scalar2=INV_N,
                                        op0=ALU.is_ge, op1=ALU.mult), sm["s_vm"])
                                vqgl_used[p % 2] = vmask_p

                            head_cache = None
                            for h in range(H):
                                nv = u1 - u0
                                groups = [(u0 + 2 * k, min(u0 + 2 * k + 2, u1))
                                          for k in range((nv + 1) // 2)]
                                ng = len(groups)
                                vsc_g = [0] * ng
                                vsim_g = [None] * ng

                                if pair_first:
                                    W("tensor", sm["s_ke"], vke_pair)
                                    W("tensor", sm["s_kmld"], vkm_pair)
                                    pair_first = False
                                vA_free_u = vA_free
                                vA_free_su = vA_free_s
                                vke_floor = vke_seq[-1] if vke_seq else 0

                                def emit_scores(gi, hh, vsc_a, vsim_a, p=p, NKB=NKB,
                                                groups=groups, v0=v0):
                                    glo, ghi = groups[gi]
                                    gpar = gi % 4
                                    if gi >= 4:
                                        sh, sv = vsim_a[gi - 4]
                                        W("tensor", sh, sv)
                                    else:
                                        if prev_vsim[gpar] is not None:
                                            sh, sv = prev_vsim[gpar]
                                            W("tensor", sh, sv)
                                        if gpar >= 2:
                                            W("tensor", sm["s_ke"], vke_floor)
                                    for vv in range(glo, ghi):
                                        wid = 256 if vv < NKB else 128
                                        lt = vv - v0
                                        half = vv - glo
                                        r = E("tensor", lambda e, gpar=gpar, half=half, wid=wid, lt=lt, hh=hh: e.matmul(
                                            ps_s[:, gpar * 512 + half * 256: gpar * 512 + half * 256 + wid],
                                            kT_s[:, hh * 3072 + lt * 128: hh * 3072 + (lt + 1) * 128],
                                            qT_sb[:, hh * 2048 + p * 256: hh * 2048 + p * 256 + wid],
                                            start=True, stop=True),
                                            sm["s_sc"] if vv == ghi - 1 else None)
                                        if r:
                                            vsc_a[gi] = r

                                def emit_relu(gi, vsc_a, vsim_a, p=p, NKB=NKB, groups=groups,
                                              tail_vis=tail_vis, vmask_p=vmask_p):
                                    glo, ghi = groups[gi]
                                    gpar = gi % 4
                                    wid = (ghi - glo - 1) * 256 + (256 if ghi - 1 < NKB else 128)
                                    masked = (glo >= NKB - 4)
                                    W("scalar", sm["s_sc"], vsc_a[gi])
                                    if not masked:
                                        v = E("scalar", lambda e, gpar=gpar, wid=wid: e.activation(
                                            out=sim_sb[:, gpar * 512: gpar * 512 + wid],
                                            in_=ps_s[:, gpar * 512: gpar * 512 + wid],
                                            func=AF.Relu, scale=INV_N), sm["s_sim"])
                                        vsim_a[gi] = (sm["s_sim"], v)
                                    else:
                                        vr = E("scalar", lambda e, gpar=gpar, wid=wid: e.activation(
                                            out=sim_sb[:, gpar * 512: gpar * 512 + wid],
                                            in_=ps_s[:, gpar * 512: gpar * 512 + wid],
                                            func=AF.Relu, scale=1.0), sm["s_sim"])
                                        W("vector", sm["s_sim"], vr)
                                        W("vector", sm["s_vm"], vmask_p)
                                        vlast = 0
                                        for vv in range(glo, ghi):
                                            midx = tail_vis.index(vv)
                                            wid_v = 256 if vv < NKB else 128
                                            half = vv - glo
                                            vlast = E("vector", lambda e, gpar=gpar, half=half, midx=midx, wid_v=wid_v: e.tensor_mul(
                                                sim_sb[:, gpar * 512 + half * 256: gpar * 512 + half * 256 + wid_v],
                                                sim_sb[:, gpar * 512 + half * 256: gpar * 512 + half * 256 + wid_v],
                                                mask_sb[:, midx * 256: midx * 256 + wid_v]), sm["s_vm"])
                                        vsim_a[gi] = (sm["s_vm"], vlast)

                                last_vis = {0: u1 - 1, 1: min(u1, NKB) - 1}

                                def emit_amm(gi, NKB=NKB, groups=groups, ng=ng,
                                             vsim_g=vsim_g, u0=u0, v0=v0, last_vis=last_vis):
                                    glo, ghi = groups[gi]
                                    gpar = gi % 4
                                    sh, sv = vsim_g[gi]
                                    if gi == 0:
                                        W("tensor", sm["s_ae"], vA_free_u)
                                        W("tensor", sm["s_aes"], vA_free_su)
                                    W("tensor", sh, sv)
                                    for vv in range(glo, ghi):
                                        halfcol = (vv - glo) * 256
                                        lt = vv - v0
                                        tiles = (0, 1) if vv < NKB else (0,)
                                        for tt in tiles:
                                            for dh in range(2):
                                                first = (vv == u0)
                                                stop = (vv == last_vis[tt])
                                                is_final = (gi == ng - 1 and vv == ghi - 1
                                                            and tt == tiles[-1] and dh == 1)
                                                E("tensor", lambda e, gpar=gpar, halfcol=halfcol, lt=lt, tt=tt, dh=dh, first=first, stop=stop: e.matmul(
                                                    ps_a[:, tt * 1024 + dh * 512: tt * 1024 + (dh + 1) * 512],
                                                    sim_sb[:, gpar * 512 + halfcol + tt * 128: gpar * 512 + halfcol + (tt + 1) * 128],
                                                    xnkm_s[:, lt * 1024 + dh * 512: lt * 1024 + (dh + 1) * 512],
                                                    start=first, stop=stop),
                                                    sm["s_amm"] if is_final else None)

                                vapi_ready = 0
                                api_base = 0
                                sapi = None
                                if has_ci:
                                    api_base = (ci_uid % 2) * 2
                                    sapi = sm["s_api0" if ci_uid % 2 == 0 else "s_api1"]
                                    ci_uid += 1
                                    for tt in range(2):
                                        W("sync", sm["s_ae"], vapi_used[api_base + tt])
                                        vapi_ready = E("sync", lambda e, p=p, h=h, tt=tt, api_base=api_base: e.dma_start(
                                            out=api_sb[:, (api_base + tt) * 1024:(api_base + tt + 1) * 1024],
                                            in_=apart_d[p, h, :, tt * 1024:(tt + 1) * 1024]), sapi, 16)

                                if head_cache is not None:
                                    vsc_g[0], vsim_g[0] = head_cache[0]
                                    if ng > 1:
                                        vsc_g[1], vsim_g[1] = head_cache[1]
                                    head_cache = None
                                else:
                                    emit_scores(0, h, vsc_g, vsim_g)
                                    emit_relu(0, vsc_g, vsim_g)
                                    if ng > 1:
                                        emit_scores(1, h, vsc_g, vsim_g)
                                        emit_relu(1, vsc_g, vsim_g)

                                last_b01 = max(k for k in range(ng) if k % 4 in (0, 1))
                                for gi in range(ng):
                                    if gi + 2 < ng:
                                        emit_scores(gi + 2, h, vsc_g, vsim_g)
                                        emit_relu(gi + 2, vsc_g, vsim_g)
                                    emit_amm(gi)
                                    if gi == last_b01 and h + 1 < H:
                                        for g4 in range(4):
                                            cand = [vsim_g[k] for k in range(ng) if k % 4 == g4]
                                            if cand:
                                                prev_vsim[g4] = cand[-1]
                                        nvsc = [0] * ng
                                        nvsim = [None] * ng
                                        emit_scores(0, h + 1, nvsc, nvsim)
                                        emit_relu(0, nvsc, nvsim)
                                        hc = [(nvsc[0], nvsim[0])]
                                        if ng > 1:
                                            emit_scores(1, h + 1, nvsc, nvsim)
                                            emit_relu(1, nvsc, nvsim)
                                            hc.append((nvsc[1], nvsim[1]))
                                        head_cache = hc
                                vamm = sm["s_amm"].v
                                if h + 1 >= H:
                                    for gpar2 in range(4):
                                        cand = [vsim_g[k] for k in range(ng) if k % 4 == gpar2]
                                        if cand:
                                            prev_vsim[gpar2] = cand[-1]

                                tileB_has = (u0 < min(NKB, u1))

                                if is_last:
                                    sad0, sad1 = sm["s_adma0"], sm["s_adma1"]
                                    dst, freev = abf_sb, vabf_free
                                else:
                                    sad0, sad1 = sm["s_admb0"], sm["s_admb1"]
                                    dst, freev = apo_sb, vapo_free
                                W("scalar", sm["s_amm"], vamm)
                                W("vector", sm["s_amm"], vamm)
                                if not has_ci:
                                    W("scalar", sad0, freev[0])
                                    v_e0 = E("scalar", lambda e, dst=dst: e.copy(
                                        out=dst[:, 0:1024], in_=ps_a[:, 0:1024]), sm["s_aes"])
                                    W("vector", sad1, freev[1])
                                    v_e1 = E("vector", lambda e, dst=dst: e.tensor_copy(
                                        dst[:, 1024:2048], ps_a[:, 1024:2048]), sm["s_ae"])
                                    W("gpsimd", sm["s_aes"], v_e0)
                                    tgt = a_d if is_last else apart_d
                                    freev[0] = E("gpsimd", lambda e, tgt=tgt, dst=dst, p=p, h=h: e.dma_start(
                                        out=tgt[p, h, :, 0:1024], in_=dst[:, 0:1024]), sad0, 16)
                                    W("gpsimd", sm["s_ae"], v_e1)
                                    freev[1] = E("gpsimd", lambda e, tgt=tgt, dst=dst, p=p, h=h: e.dma_start(
                                        out=tgt[p, h, :, 1024:2048], in_=dst[:, 1024:2048]), sad1, 16)
                                else:
                                    W("scalar", sm["s_ae"], vstg_free)
                                    v_e0 = E("scalar", lambda e: e.copy(
                                        out=stg_sb[:, 0:1024], in_=ps_a[:, 0:1024]), sm["s_aes"])
                                    if tileB_has:
                                        W("vector", sm["s_ae"], vstg_free)
                                        v_e1 = E("vector", lambda e: e.tensor_copy(
                                            stg_sb[:, 1024:2048], ps_a[:, 1024:2048]), sm["s_ae"])
                                    else:
                                        v_e1 = sm["s_ae"].v
                                    W("vector", sapi, vapi_ready)
                                    W("vector", sm["s_aes"], v_e0)
                                    W("vector", sad0, freev[0])
                                    va0 = E("vector", lambda e, dst=dst, api_base=api_base: e.tensor_add(
                                        dst[:, 0:1024], stg_sb[:, 0:1024],
                                        api_sb[:, api_base * 1024:(api_base + 1) * 1024]), sm["s_ae"])
                                    vapi_used[api_base] = va0
                                    W("vector", sad1, freev[1])
                                    if tileB_has:
                                        W("vector", sm["s_ae"], v_e1)
                                        va1 = E("vector", lambda e, dst=dst, api_base=api_base: e.tensor_add(
                                            dst[:, 1024:2048], stg_sb[:, 1024:2048],
                                            api_sb[:, (api_base + 1) * 1024:(api_base + 2) * 1024]), sm["s_ae"])
                                    else:
                                        va1 = E("vector", lambda e, dst=dst, api_base=api_base: e.tensor_copy(
                                            dst[:, 1024:2048],
                                            api_sb[:, (api_base + 1) * 1024:(api_base + 2) * 1024]), sm["s_ae"])
                                    vapi_used[api_base + 1] = va1
                                    vstg_free = va1
                                    W("gpsimd", sm["s_ae"], va0)
                                    tgt = a_d if is_last else apart_d
                                    freev[0] = E("gpsimd", lambda e, tgt=tgt, dst=dst, p=p, h=h: e.dma_start(
                                        out=tgt[p, h, :, 0:1024], in_=dst[:, 0:1024]), sad0, 16)
                                    W("gpsimd", sm["s_ae"], va1)
                                    freev[1] = E("gpsimd", lambda e, tgt=tgt, dst=dst, p=p, h=h: e.dma_start(
                                        out=tgt[p, h, :, 1024:2048], in_=dst[:, 1024:2048]), sad1, 16)
                                vA_free = v_e1 if not has_ci else max(v_e1, 0)
                                vA_free = sm["s_ae"].v if False else vA_free
                                vA_free_s = sm["s_aes"].v
                                vA_free = v_e1
                        if si < 2:
                            kmm_end = sm["s_kmm"].v
                            nv0 = STRIPES[si + 1][0]
                            for bb in range(2):
                                j = nv0 // 4 + bb
                                W("sync", sm["s_kmm"], kmm_end)
                                vfm_pre[(si + 1, bb)] = E("sync", lambda e, bb=bb, j=j: e.dma_start(
                                    out=xnfm2[:, (bb % 2) * 4096:(bb % 2 + 1) * 4096],
                                    in_=xnfm_d[j, :, :]), sm["s_fmld0" if bb % 2 == 0 else "s_fmld1"], 16)
                        for snm in ("s_adma0", "s_adma1", "s_admb0", "s_admb1"):
                            W("gpsimd", sm[snm], sm[snm].v)
                        flush(f"p2_s{si}")

        with ExitStack() as S_f1:
            wgbf = S_f1.enter_context(nc.sbuf_tensor("wgbf", [128, 8 * HID], BF16))
            wubf = S_f1.enter_context(nc.sbuf_tensor("wubf", [128, 8 * HID], BF16))
            xnfm_tb = S_f1.enter_context(nc.sbuf_tensor("xnfm_tb", [128, 2 * 4096], BF16))
            hT_buf = S_f1.enter_context(nc.sbuf_tensor("hT_buf", [128, 24 * 512], BF16))
            silu_sb = S_f1.enter_context(nc.sbuf_tensor("silu_sb", [128, 4 * 512], F32))
            ps_g = S_f1.enter_context(nc.psum_tensor("ps_g", [128, 2048], F32))
            ps_u = S_f1.enter_context(nc.psum_tensor("ps_u", [128, 2048], F32))

            vfwc = [0] * 8
            wg3 = wgT_in.rearrange("(c p) j -> p c j", p=128)
            wu3 = wuT_in.rearrange("(c p) j -> p c j", p=128)
            wgo = wgbf.rearrange("p (c j) -> p c j", c=8)
            wuo = wubf.rearrange("p (c j) -> p c j", c=8)
            for r in range(8):
                E("gpsimd", lambda e, r=r: e.dma_start(
                    out=wgo[:, :, r * 384:(r + 1) * 384],
                    in_=wg3[:, :, r * 384:(r + 1) * 384]), sm[f"s_fwc{r}"], 16)
                vfwc[r] = E("gpsimd", lambda e, r=r: e.dma_start(
                    out=wuo[:, :, r * 384:(r + 1) * 384],
                    in_=wu3[:, :, r * 384:(r + 1) * 384]), sm[f"s_fwc{r}"], 16)

            vhv_seq = []
            vsil_seq = []
            vhtw = [0] * 4
            vgu_tb = [0] * 4
            for tb in range(4):
                par = tb % 2
                xft = xnfm_tb[:, par * 4096:(par + 1) * 4096]
                xft3 = xnfm_tb.rearrange("p (b c q) -> p b c q", b=2, c=8)
                if tb >= 2:
                    W("sync", sm["s_gu"], vgu_tb[tb - 2])
                sft = sm["s_fmtb0" if par == 0 else "s_fmtb1"]
                vld = 0
                for s in range(4):
                    blkj = 15 - (4 * tb + s)
                    src = xnfm_d[blkj].rearrange("p (c q) -> p c q", c=8)[:, :, 384:512]
                    vld = E("sync", lambda e, src=src, xft3=xft3, par=par, s=s: e.dma_start(
                        out=xft3[:, par, :, s * 128:(s + 1) * 128], in_=src), sft, 16)
                W("tensor", sft, vld)
                for jc in range(24):
                    if tb == 0 and jc % 3 == 0:
                        W("tensor", sm[f"s_fwc{jc // 3}"], vfwc[jc // 3])
                    idx = tb * 24 + jc
                    bank = idx % 4
                    if idx >= 4:
                        W("tensor", sm["s_silu"], vsil_seq[idx - 4])
                        W("tensor", sm["s_hv"], vhv_seq[idx - 4])
                    for c in range(8):
                        E("tensor", lambda e, bank=bank, c=c, jc=jc, xft=xft: e.matmul(
                            ps_g[:, bank * 512:(bank + 1) * 512],
                            wgbf[:, c * HID + jc * 128: c * HID + (jc + 1) * 128],
                            xft[:, c * 512:(c + 1) * 512],
                            start=(c == 0), stop=(c == 7)))
                    for c in range(8):
                        r = E("tensor", lambda e, bank=bank, c=c, jc=jc, xft=xft: e.matmul(
                            ps_u[:, bank * 512:(bank + 1) * 512],
                            wubf[:, c * HID + jc * 128: c * HID + (jc + 1) * 128],
                            xft[:, c * 512:(c + 1) * 512],
                            start=(c == 0), stop=(c == 7)),
                            sm["s_gu"] if c == 7 else None)
                        if r:
                            vgu_tb[tb] = r
                    W("scalar", sm["s_gu"], vgu_tb[tb])
                    vsil_seq.append(E("scalar", lambda e, bank=bank: e.activation(
                        out=silu_sb[:, bank * 512:(bank + 1) * 512],
                        in_=ps_g[:, bank * 512:(bank + 1) * 512],
                        func=AF.Silu, scale=INV127), sm["s_silu"]))
                    W("vector", sm["s_silu"], vsil_seq[-1])
                    W("vector", sm["s_gu"], vgu_tb[tb])
                    if jc == 0 and tb >= 1:
                        W("vector", sm["s_htw"], vhtw[tb - 1])
                    vhv_seq.append(E("vector", lambda e, bank=bank, jc=jc: e.scalar_tensor_tensor(
                        out=hT_buf[:, jc * 512:(jc + 1) * 512],
                        in0=ps_u[:, bank * 512:(bank + 1) * 512], scalar=INV127,
                        in1=silu_sb[:, bank * 512:(bank + 1) * 512],
                        op0=ALU.mult, op1=ALU.mult), sm["s_hv"]))
                W("gpsimd", sm["s_hv"], vhv_seq[-1])
                vhtw[tb] = E("gpsimd", lambda e, tb=tb: e.dma_start(
                    out=ht_d[tb, :, :], in_=hT_buf[:, :]), sm["s_htw"], 16)
            W("gpsimd", sm["s_htw"], sm["s_htw"].v)
            flush("f1")

        S_wvx = ExitStack()
        wv32 = S_wvx.enter_context(nc.sbuf_tensor("wv32", [128, 8192], F32))
        wvbf = S_wvx.enter_context(nc.sbuf_tensor("wvbf", [128, 2 * 8192], BF16))
        ident2 = S_wvx.enter_context(nc.sbuf_tensor("ident2", [128, 128], BF16))
        vwvc = [0] * H
        with ExitStack() as S_f2:
            wdbf = S_f2.enter_context(nc.sbuf_tensor("wdbf", [128, 24 * 1024], BF16))
            hT_rd = S_f2.enter_context(nc.sbuf_tensor("hT_rd", [128, 2 * 12288], BF16))
            g_sb = S_f2.enter_context(nc.sbuf_tensor("g_sb", [128, 2 * 1024], F32))
            ps_d = S_f2.enter_context(nc.psum_tensor("ps_d", [128, 2048], F32))

            v_id2 = E("sync", lambda e: e.dma_start(out=ident2[:, :], in_=ident_in[:, :]), sm["s_wvc"], 16)
            vwv0 = E("sync", lambda e: e.dma_start(
                out=wv32.rearrange("p (c v) -> p c v", c=8),
                in_=wvT_in[0].rearrange("(c p) v -> p c v", p=128)), sm["s_wv0"], 16)
            W("vector", sm["s_wv0"], vwv0)
            E("vector", lambda e: e.tensor_copy(wvbf[:, 0:4096], wv32[:, 0:4096]), sm["s_wv1"])
            vwvc[0] = E("vector", lambda e: e.tensor_copy(wvbf[:, 4096:8192], wv32[:, 4096:8192]),
                        sm["s_wv1"])

            vfwd = [0] * 8
            wd3 = wdT_in.rearrange("(c p) d -> p c d", p=128)
            wdo = wdbf.rearrange("p (c d) -> p c d", c=24)
            for r in range(8):
                vfwd[r] = E("gpsimd", lambda e, r=r: e.dma_start(
                    out=wdo[:, r * 3:(r + 1) * 3, :],
                    in_=wd3[:, r * 3:(r + 1) * 3, :]), sm[f"s_fwc{r}"], 16)

            vgs_seq = []
            vgw_seq = []
            vdm_seq = []
            for tb in range(4):
                par = tb % 2
                hrt = hT_rd[:, par * 12288:(par + 1) * 12288]
                if tb >= 2:
                    W("sync", sm["s_dmm"], vdm_seq[4 * (tb - 2) + 3])
                sht = sm["s_htr0" if par == 0 else "s_htr1"]
                vht = E("sync", lambda e, hrt=hrt, tb=tb: e.dma_start(
                    out=hrt[:, :], in_=ht_d[tb, :, :]), sht, 16)
                for s in range(4):
                    u = 4 * tb + s
                    bank = u % 2
                    W("tensor", sht, vht)
                    if u >= 2:
                        W("tensor", sm["s_gs"], vgs_seq[u - 2])
                    vdm = 0
                    for jc in range(24):
                        if u == 0 and jc % 3 == 0:
                            W("tensor", sm[f"s_fwc{jc // 3}"], vfwd[jc // 3])
                        for dh in range(2):
                            r = E("tensor", lambda e, bank=bank, jc=jc, dh=dh, hrt=hrt, s=s: e.matmul(
                                ps_d[:, bank * 1024 + dh * 512: bank * 1024 + (dh + 1) * 512],
                                hrt[:, jc * 512 + s * 128: jc * 512 + (s + 1) * 128],
                                wdbf[:, jc * 1024 + dh * 512: jc * 1024 + (dh + 1) * 512],
                                start=(jc == 0), stop=(jc == 23)),
                                sm["s_dmm"] if (jc == 23 and dh == 1) else None)
                            if r:
                                vdm = r
                    vdm_seq.append(vdm)
                    W("scalar", sm["s_dmm"], vdm)
                    if u >= 2:
                        W("scalar", sm["s_gw0" if u % 2 == 0 else "s_gw1"], vgw_seq[u - 2])
                    vgs = 0
                    for dh in range(2):
                        vgs = E("scalar", lambda e, bank=bank, dh=dh: e.activation(
                            out=g_sb[:, bank * 1024 + dh * 512: bank * 1024 + (dh + 1) * 512],
                            in_=ps_d[:, bank * 1024 + dh * 512: bank * 1024 + (dh + 1) * 512],
                            func=AF.Silu, scale=INV127), sm["s_gs"])
                    vgs_seq.append(vgs)
                    W("gpsimd", sm["s_gs"], vgs)
                    vgw_seq.append(E("gpsimd", lambda e, bank=bank, u=u: e.dma_start(
                        out=g_d[u, :, :], in_=g_sb[:, bank * 1024:(bank + 1) * 1024]),
                        sm["s_gw0" if u % 2 == 0 else "s_gw1"], 16))
            W("gpsimd", sm["s_gw0"], sm["s_gw0"].v)
            W("gpsimd", sm["s_gw1"], sm["s_gw1"].v)
            flush("f2")

        with ExitStack() as S_p3o:
            ho_sb = S_p3o.enter_context(nc.sbuf_tensor("ho_sb", [128, NSLOT * 1024], F32))
            a_rd = S_p3o.enter_context(nc.sbuf_tensor("a_rd", [128, 3 * 1024], BF16))
            at_sb = S_p3o.enter_context(nc.sbuf_tensor("at_sb", [128, 3 * 1024], BF16))
            xn_b = S_p3o.enter_context(nc.sbuf_tensor("xn_b", [128, 2 * 1024], F32))
            g_b = S_p3o.enter_context(nc.sbuf_tensor("g_b", [128, 2 * 1024], F32))
            x_b = S_p3o.enter_context(nc.sbuf_tensor("x_b", [128, 2 * 1024], F32))
            s_b = S_p3o.enter_context(nc.sbuf_tensor("s_b", [128, 2 * 1024], F32))
            o_b = S_p3o.enter_context(nc.sbuf_tensor("o_b", [128, 2 * 1024], F32))
            ps_tr3 = S_p3o.enter_context(nc.psum_tensor("ps_tr3", [128, 3 * 1024], BF16))
            ps_ho = S_p3o.enter_context(nc.psum_tensor("ps_ho", [128, 2048], F32))

            if True:
                UTOT = H * NSLOT
                vard = [0] * UTOT
                vtr3 = [0] * UTOT
                vatc = [0] * UTOT
                vhom = [0] * UTOT
                vhoa = [0] * UTOT
                vev = [0] * NSLOT
                vout = [0] * NSLOT

                vwvl = [vwv0] + [0] * (H - 1)

                def plan_wv(h):
                    W("sync", sm["s_wv1"], vwvc[h - 1])
                    vwvl[h] = E("sync", lambda e, h=h: e.dma_start(
                        out=wv32.rearrange("p (c v) -> p c v", c=8),
                        in_=wvT_in[h].rearrange("(c p) v -> p c v", p=128)),
                        sm["s_wv0"], 16)
                    W("vector", sm["s_wv0"], vwvl[h])
                    if h >= 2:
                        W("vector", sm["s_hom"], vhom[(h - 1) * NSLOT - 1])
                    base = (h % 2) * 8192
                    E("vector", lambda e, base=base: e.tensor_copy(
                        wvbf[:, base:base + 4096], wv32[:, 0:4096]), sm["s_wv1"])
                    vwvc[h] = E("vector", lambda e, base=base: e.tensor_copy(
                        wvbf[:, base + 4096:base + 8192], wv32[:, 4096:8192]), sm["s_wv1"])

                def plan_ard(u):
                    h, i = u // NSLOT, u % NSLOT
                    p, tt = i // 2, i % 2
                    if u >= 3:
                        W("sync", sm["s_tr3"], vtr3[u - 3])
                    vard[u] = E("sync", lambda e, p=p, h=h, tt=tt, u=u: e.dma_start(
                        out=a_rd[:, (u % 3) * 1024:(u % 3 + 1) * 1024],
                        in_=a_d[p, h, :, tt * 1024:(tt + 1) * 1024]),
                        sm[f"s_ard{u % 3}"], 16)

                def plan_tr(u):
                    W("tensor", sm[f"s_ard{u % 3}"], vard[u])
                    if u >= 3:
                        W("tensor", sm["s_atc"], vatc[u - 3])
                    if u == 0:
                        W("tensor", sm["s_wvc"], 16)
                    for c in range(8):
                        r = E("tensor", lambda e, c=c, u=u: e.transpose(
                            ps_tr3[:, (u % 3) * 1024 + c * 128: (u % 3) * 1024 + (c + 1) * 128],
                            a_rd[:, (u % 3) * 1024 + c * 128: (u % 3) * 1024 + (c + 1) * 128],
                            ident2[:, :]), sm["s_tr3"] if c == 7 else None)
                        if r:
                            vtr3[u] = r

                def plan_atc(u):
                    W("scalar", sm["s_tr3"], vtr3[u])
                    vatc[u] = E("scalar", lambda e, u=u: e.copy(
                        out=at_sb[:, (u % 3) * 1024:(u % 3 + 1) * 1024],
                        in_=ps_tr3[:, (u % 3) * 1024:(u % 3 + 1) * 1024]), sm["s_atc"])

                def plan_mm(u):
                    h = u // NSLOT
                    W("tensor", sm["s_atc"], vatc[u])
                    if (u % NSLOT) == 0:
                        W("tensor", sm["s_wv1"], vwvc[h])
                    if u >= 2:
                        W("tensor", sm["s_hoa"], vhoa[u - 2])
                    for c in range(8):
                        for vh in range(2):
                            r = E("tensor", lambda e, c=c, vh=vh, u=u, h=h: e.matmul(
                                ps_ho[:, (u % 2) * 1024 + vh * 512: (u % 2) * 1024 + vh * 512 + 512],
                                at_sb[:, (u % 3) * 1024 + c * 128: (u % 3) * 1024 + (c + 1) * 128],
                                wvbf[:, (h % 2) * 8192 + c * 1024 + vh * 512: (h % 2) * 8192 + c * 1024 + vh * 512 + 512],
                                start=(c == 0), stop=(c == 7)),
                                sm["s_hom"] if (c == 7 and vh == 1) else None)
                            if r:
                                vhom[u] = r

                def plan_add(u):
                    h, i = u // NSLOT, u % NSLOT
                    W("vector", sm["s_hom"], vhom[u])
                    if h == 0:
                        vhoa[u] = E("vector", lambda e, u=u, i=i: e.tensor_copy(
                            ho_sb[:, i * 1024:(i + 1) * 1024],
                            ps_ho[:, (u % 2) * 1024:(u % 2 + 1) * 1024]), sm["s_hoa"])
                    else:
                        vhoa[u] = E("vector", lambda e, u=u, i=i: e.tensor_add(
                            ho_sb[:, i * 1024:(i + 1) * 1024],
                            ps_ho[:, (u % 2) * 1024:(u % 2 + 1) * 1024],
                            ho_sb[:, i * 1024:(i + 1) * 1024]), sm["s_hoa"])

                def plan_epi(i):
                    par = i % 2
                    ser = sm["s_er0" if par == 0 else "s_er1"]
                    sou = sm["s_out0" if par == 0 else "s_out1"]
                    if i >= 2:
                        W("sync", sm["s_ev"], vev[i - 2])
                    E("sync", lambda e, i=i, par=par: e.dma_start(
                        out=xn_b[:, par * 1024:(par + 1) * 1024], in_=xnf32_d[i, :, :]), ser, 16)
                    E("sync", lambda e, i=i, par=par: e.dma_start(
                        out=g_b[:, par * 1024:(par + 1) * 1024], in_=g_d[i, :, :]), ser, 16)
                    ver = E("sync", lambda e, i=i, par=par: e.dma_start(
                        out=x_b[:, par * 1024:(par + 1) * 1024], in_=x3[63 - 4 * i, :, :]), ser, 16)

                    W("vector", ser, ver)
                    W("vector", sm["s_hoa"], vhoa_slot_final[i])
                    if i >= 2:
                        W("vector", sou, vout[i - 2])
                    v1 = E("vector", lambda e, i=i, par=par: e.tensor_add(
                        s_b[:, par * 1024:(par + 1) * 1024],
                        xn_b[:, par * 1024:(par + 1) * 1024],
                        ho_sb[:, i * 1024:(i + 1) * 1024]), sm["s_ev"])
                    W("vector", sm["s_ev"], v1)
                    v2 = E("vector", lambda e, par=par: e.tensor_mul(
                        s_b[:, par * 1024:(par + 1) * 1024],
                        s_b[:, par * 1024:(par + 1) * 1024],
                        g_b[:, par * 1024:(par + 1) * 1024]), sm["s_ev"])
                    W("vector", sm["s_ev"], v2)
                    vev[i] = E("vector", lambda e, par=par: e.tensor_add(
                        o_b[:, par * 1024:(par + 1) * 1024],
                        s_b[:, par * 1024:(par + 1) * 1024],
                        x_b[:, par * 1024:(par + 1) * 1024]), sm["s_ev"])
                    W("gpsimd", sm["s_ev"], vev[i])
                    vout[i] = E("gpsimd", lambda e, i=i, par=par: e.dma_start(
                        out=y_out[i * 128:(i + 1) * 128, :],
                        in_=o_b[:, par * 1024:(par + 1) * 1024]), sou, 16)

                vhoa_slot_final = [0] * NSLOT

                def maybe_epi(u):
                    h, i = u // NSLOT, u % NSLOT
                    if h == H - 1:
                        vhoa_slot_final[i] = vhoa[u]
                        plan_epi(i)

                plan_ard(0)
                plan_tr(0)
                plan_ard(1)
                plan_tr(1)
                for u in range(UTOT):
                    if u % NSLOT == 0 and u > 0:
                        plan_wv(u // NSLOT)
                    if u + 2 < UTOT:
                        plan_ard(u + 2)
                        plan_tr(u + 2)
                    plan_atc(u)
                    plan_mm(u)
                    plan_add(u)
                    maybe_epi(u)
                W("gpsimd", sm["s_out0"], sm["s_out0"].v)
                W("gpsimd", sm["s_out1"], sm["s_out1"].v)
                flush("p3")
        S_wvx.close()


    return nc


_NC_CACHE = None


def _get_nc():
    global _NC_CACHE
    if _NC_CACHE is None:
        _NC_CACHE = build_nc()
    return _NC_CACHE


def kernel(L0_attn_x, L0_attn_Wq, L0_attn_Wk, L0_attn_Wv,
           L0_ffn_w_gate, L0_ffn_w_up, L0_ffn_w_down):
    x = np.asarray(L0_attn_x, dtype=np.float32)
    Wq = np.asarray(L0_attn_Wq, dtype=np.float32)
    Wk = np.asarray(L0_attn_Wk, dtype=np.float32)
    Wv = np.asarray(L0_attn_Wv, dtype=np.float32)
    wg = np.asarray(L0_ffn_w_gate, dtype=np.int32)
    wu = np.asarray(L0_ffn_w_up, dtype=np.int32)
    wd = np.asarray(L0_ffn_w_down, dtype=np.int32)

    wqT = np.ascontiguousarray(np.transpose(Wq, (2, 0, 1)).reshape(D, H * DK))
    wkT = np.ascontiguousarray(np.transpose(Wk, (2, 0, 1)).reshape(D, H * DK))
    wvT = np.ascontiguousarray(np.transpose(Wv, (0, 2, 1)))
    wgT = np.ascontiguousarray(wg.T)
    wuT = np.ascontiguousarray(wu.T)
    wdT = np.ascontiguousarray(wd.T)
    ident = np.eye(128, dtype=ml_dtypes.bfloat16)

    in_maps = []
    metas = []
    for c in range(8):
        b, cc = c // 4, c % 4
        loc2glob = np.array([4 * (lt // 4) + ((lt % 4) + cc + 1) % 4 for lt in range(T)])
        x_loc = np.ascontiguousarray(
            x[b].reshape(T, 128, D)[loc2glob].reshape(N, D))
        kglob = (loc2glob[None, :] * 128 + np.arange(128)[:, None]).astype(np.float32)
        qglob = np.zeros((NPAIR, 128, 256), dtype=np.float32)
        tg = [60 - 4 * i + cc for i in range(NSLOT)]
        for p in range(NPAIR):
            qglob[p, :, 0:128] = tg[2 * p] * 128 + np.arange(128)[None, :]
            qglob[p, :, 128:256] = tg[2 * p + 1] * 128 + np.arange(128)[None, :]
        in_maps.append({
            "x": x_loc, "wqT": wqT, "wkT": wkT, "wvT": wvT,
            "wgT": wgT, "wuT": wuT, "wdT": wdT,
            "ident": ident, "kglob": kglob, "qglob": qglob,
        })
        metas.append((b, tg))

    nc = _get_nc()
    res = run_bass_kernel_spmd(nc, in_maps, core_ids=list(range(8)))

    out = np.zeros((B, N, D), dtype=np.float32)
    for c in range(8):
        b, tg = metas[c]
        yc = np.asarray(res.results[c]["y"], dtype=np.float32)
        for i in range(NSLOT):
            out[b, tg[i] * 128:(tg[i] + 1) * 128, :] = yc[i * 128:(i + 1) * 128, :]
    return out


# version 18
# speedup vs baseline: 1.0157x; 1.0100x over previous
import numpy as np
import ml_dtypes
from contextlib import ExitStack

import concourse.bass as bass
import concourse.mybir as mybir
from concourse.bass_utils import run_bass_kernel_spmd

F32 = mybir.dt.float32
BF16 = mybir.dt.bfloat16
I32 = mybir.dt.int32
AF = mybir.ActivationFunctionType
ALU = mybir.AluOpType
AX = mybir.AxisListType.X

B, N, D, H, DK, HID = 2, 8192, 1024, 8, 128, 3072
T = 64
NSLOT = 16
NPAIR = 8
STRIPES = [(0, 24), (24, 44), (44, 64)]
INV_N = 1.0 / float(N)
INV127 = 1.0 / 127.0
EPS = 1e-5

ENGS = ("sync", "tensor", "scalar", "vector", "gpsimd")


class Sem:
    def __init__(self, ctx, nc, name):
        self.h = ctx.enter_context(nc.semaphore(name))
        self.v = 0


def build_nc():
    nc = bass.Bass("TRN2", target_bir_lowering=False)

    x_in = nc.dram_tensor("x", [N, D], F32, kind="ExternalInput").ap()
    wqT_in = nc.dram_tensor("wqT", [D, H * DK], F32, kind="ExternalInput").ap()
    wkT_in = nc.dram_tensor("wkT", [D, H * DK], F32, kind="ExternalInput").ap()
    wvT_in = nc.dram_tensor("wvT", [H, D, D], F32, kind="ExternalInput").ap()
    wgT_in = nc.dram_tensor("wgT", [D, HID], I32, kind="ExternalInput").ap()
    wuT_in = nc.dram_tensor("wuT", [D, HID], I32, kind="ExternalInput").ap()
    wdT_in = nc.dram_tensor("wdT", [HID, D], I32, kind="ExternalInput").ap()
    ident_in = nc.dram_tensor("ident", [128, 128], BF16, kind="ExternalInput").ap()
    kglob_in = nc.dram_tensor("kglob", [128, T], F32, kind="ExternalInput").ap()
    qglob_in = nc.dram_tensor("qglob", [NPAIR, 128, 256], F32, kind="ExternalInput").ap()
    y_out = nc.dram_tensor("y", [NSLOT * 128, D], F32, kind="ExternalOutput").ap()

    xnkm_d = nc.dram_tensor("xnkm_d", [T, 128, D], BF16, kind="Internal").ap()
    xnfm_d = nc.dram_tensor("xnfm_d", [16, 128, 4096], BF16, kind="Internal").ap()
    xnf32_d = nc.dram_tensor("xnf32_d", [NSLOT, 128, D], F32, kind="Internal").ap()
    apart_d = nc.dram_tensor("apart_d", [NPAIR, H, 128, 2048], F32, kind="Internal").ap()
    a_d = nc.dram_tensor("a_d", [NPAIR, H, 128, 2048], BF16, kind="Internal").ap()
    ht_d = nc.dram_tensor("ht_d", [4, 128, 24 * 512], BF16, kind="Internal").ap()
    g_d = nc.dram_tensor("g_d", [NSLOT, 128, D], F32, kind="Internal").ap()

    x3 = x_in.rearrange("(t p) d -> t p d", p=128)

    cur = {k: [] for k in ENGS}

    def W(eng, sem, val):
        if val > 0:
            cur[eng].append(lambda e, s=sem.h, v=val: e.wait_ge(s, v))

    def E(eng, fn, sem=None, n=1):
        if sem is None:
            cur[eng].append(fn)
            return None
        sem.v += n
        cur[eng].append(lambda e, f=fn, s=sem.h, m=n: f(e).then_inc(s, m))
        return sem.v

    def flush(name):
        nonlocal cur
        bodies = cur
        cur = {k: [] for k in ENGS}
        with nc.Block(name) as block:
            for eng in ENGS:
                ops = bodies[eng]
                if ops:
                    getattr(block, eng)(lambda engine, ops=ops: [f(engine) for f in ops])

    with ExitStack() as S0:
        cbank = S0.enter_context(nc.sbuf_tensor("cbank", [128, 4], F32))
        kglob_sb = S0.enter_context(nc.sbuf_tensor("kglob_sb", [128, T], F32))
        nc.const_aps.register(F32, EPS, cbank[:, 0:1])
        nc.const_aps.register(F32, INV_N, cbank[:, 1:2])
        nc.const_aps.register(F32, INV127, cbank[:, 2:3])

        sem_names = [
            "s_x", "s_x0", "s_x1", "s_x2", "s_n", "s_ng", "s_red", "s_rcp", "s_sq", "s_cols", "s_tr1", "s_fme",
            "s_w10", "s_w11", "s_w12",
            "s_qpld0", "s_qpld1", "s_qmm", "s_qe", "s_wc",
            "s_kmld", "s_fmld0", "s_fmld1", "s_kmm", "s_ke", "s_sc", "s_sim", "s_vm",
            "s_amm", "s_ae", "s_aes", "s_adma0", "s_adma1", "s_admb0", "s_admb1", "s_api0", "s_api1", "s_qgl0", "s_qgl1",
            "s_fw", "s_fwc0", "s_fwc1", "s_fwc2", "s_fwc3", "s_fwc4", "s_fwc5",
            "s_fwc6", "s_fwc7",
            "s_fmtb0", "s_fmtb1", "s_gu", "s_silu", "s_hv", "s_htw",
            "s_htr0", "s_htr1", "s_dmm", "s_gs", "s_gw0", "s_gw1",
            "s_wv0", "s_wv1", "s_wvc", "s_ard0", "s_ard1", "s_ard2", "s_tr3", "s_atc", "s_atcv", "s_hom", "s_hoa",
            "s_er0", "s_er1", "s_ev", "s_out0", "s_out1",
        ]
        sm = {n: Sem(S0, nc, n) for n in sem_names}

        with ExitStack() as S_qt:
            qT_sb = S_qt.enter_context(nc.sbuf_tensor("qT_sb", [128, H * NSLOT * 128], BF16))
            with ExitStack() as S_wk:
                wkbf = S_wk.enter_context(nc.sbuf_tensor("wkbf", [128, 8 * H * DK], BF16))
                with ExitStack() as S_wq:
                    wqbf = S_wq.enter_context(nc.sbuf_tensor("wqbf", [128, 8 * H * DK], BF16))
                    ident1 = S_wq.enter_context(nc.sbuf_tensor("ident1", [128, 128], BF16))

                    with ExitStack() as S_i:
                        stage32 = S_i.enter_context(nc.sbuf_tensor("stage32", [128, 8192], F32))

                        E("gpsimd", lambda e: e.memset(cbank[:, 0:1], EPS))
                        E("gpsimd", lambda e: e.memset(cbank[:, 1:2], INV_N))
                        E("gpsimd", lambda e: e.memset(cbank[:, 2:3], INV127))
                        v_id = E("sync", lambda e: e.dma_start(out=ident1[:, :], in_=ident_in[:, :]), sm["s_x"], 16)
                        E("sync", lambda e: e.dma_start(out=kglob_sb[:, :], in_=kglob_in[:, :]), sm["s_x"], 16)
                        wq3 = wqT_in.rearrange("(c p) e -> p c e", p=128)
                        v_wq = E("sync", lambda e, wq3=wq3: e.dma_start(
                            out=stage32.rearrange("p (c e) -> p c e", c=8), in_=wq3), sm["s_x"], 16)
                        W("scalar", sm["s_x"], v_wq)
                        v_cq = E("scalar", lambda e: e.activation(out=wqbf[:, :], in_=stage32[:, :], func=AF.Copy),
                                 sm["s_wc"])
                        W("sync", sm["s_wc"], v_cq)
                        wk3 = wkT_in.rearrange("(c p) e -> p c e", p=128)
                        v_wk = E("sync", lambda e, wk3=wk3: e.dma_start(
                            out=stage32.rearrange("p (c e) -> p c e", c=8), in_=wk3), sm["s_x"], 16)
                        W("scalar", sm["s_x"], v_wk)
                        E("scalar", lambda e: e.activation(out=wkbf[:, :], in_=stage32[:, :], func=AF.Copy),
                          sm["s_wc"])
                        flush("init")

                    with ExitStack() as S_p1:
                        xstage = S_p1.enter_context(nc.sbuf_tensor("xstage", [128, 3 * 4096], F32))
                        xnkm_b = S_p1.enter_context(nc.sbuf_tensor("xnkm_b", [128, 3 * 4096], BF16))
                        xnfm_b = S_p1.enter_context(nc.sbuf_tensor("xnfm_b", [128, 3 * 4096], BF16))
                        xf32_b = S_p1.enter_context(nc.sbuf_tensor("xf32_b", [128, 3 * 1024], F32))
                        cols = S_p1.enter_context(nc.sbuf_tensor("cols", [128, 3 * 24], F32))
                        dump = S_p1.enter_context(nc.sbuf_tensor("dump", [128, 4 * 1024], F32))
                        qp_buf = S_p1.enter_context(nc.sbuf_tensor("qp_buf", [128, 2 * 4096], BF16))
                        ps_t = S_p1.enter_context(nc.psum_tensor("ps_t", [128, 4096], BF16))
                        ps_q = S_p1.enter_context(nc.psum_tensor("ps_q", [128, 1024], F32))

                        vn = [0] * 16
                        vng = [0] * 16
                        vred = [0] * 16
                        vsq = [0] * 16
                        vtr_last = [0] * 16
                        vfme = [0] * 16
                        vw_blk = [0] * 16
                        vfme_slot = {}
                        vqmm_g = [0] * 4
                        vqe_seq = []

                        def emit_qgroup(g):
                            par = g % 2
                            qpb = qp_buf[:, par * 4096:(par + 1) * 4096]
                            qpb3 = qp_buf.rearrange("p (b c q) -> p b c q", b=2, c=8)
                            if g >= 2:
                                W("sync", sm["s_qmm"], vqmm_g[g - 2])
                            sqp = sm["s_qpld0" if par == 0 else "s_qpld1"]
                            for pp in (0, 1, 2):
                                blks = [15 - (4 * g + s) for s in range(4) if (15 - (4 * g + s)) % 3 == pp]
                                if blks:
                                    W("sync", sm[f"s_w1{pp}"], max(vw_blk[b] for b in blks))
                            vld = 0
                            for s in range(4):
                                blkj = 15 - (4 * g + s)
                                srcq = xnfm_d[blkj].rearrange("p (c q) -> p c q", c=8)[:, :, 384:512]
                                vld = E("sync", lambda e, srcq=srcq, qpb3=qpb3, par=par, s=s: e.dma_start(
                                    out=qpb3[:, par, :, s * 128:(s + 1) * 128], in_=srcq), sqp, 16)
                            W("tensor", sqp, vld)
                            for h in range(H):
                                gidx = g * H + h
                                hp = gidx % 2
                                if gidx >= 2:
                                    W("tensor", sm["s_qe"], vqe_seq[gidx - 2])
                                for c in range(8):
                                    r = E("tensor", lambda e, hp=hp, c=c, h=h, qpb=qpb: e.matmul(
                                        ps_q[:, hp * 512:(hp + 1) * 512],
                                        wqbf[:, c * 1024 + h * 128: c * 1024 + (h + 1) * 128],
                                        qpb[:, c * 512:(c + 1) * 512],
                                        start=(c == 0), stop=(c == 7)),
                                        sm["s_qmm"] if c == 7 else None)
                                    if r:
                                        vqmm_g[g] = r
                                W("scalar", sm["s_qmm"], vqmm_g[g])
                                vqe_seq.append(E("scalar", lambda e, hp=hp, h=h, g=g: e.copy(
                                    out=qT_sb[:, h * 2048 + g * 512: h * 2048 + (g + 1) * 512],
                                    in_=ps_q[:, hp * 512:(hp + 1) * 512]), sm["s_qe"]))

                        def emit_A(j):
                            idx = 15 - j
                            par = j % 3
                            xs = xstage[:, par * 4096:(par + 1) * 4096]
                            co = cols[:, par * 24:(par + 1) * 24]
                            if idx >= 3:
                                W("sync", sm["s_n"], vn[j + 3])
                                W("sync", sm["s_ng"], vng[j + 3])
                                W("sync", sm["s_sq"], vsq[j + 3])
                            sx = sm[f"s_x{par}"]
                            vx = E("sync", lambda e, xs=xs, j=j: e.dma_start(
                                out=xs.rearrange("p (t d) -> p t d", t=4),
                                in_=x3[4 * j:4 * j + 4, :, :].transpose([1, 0, 2])), sx, 16)

                            W("vector", sx, vx)
                            for t in range(4):
                                E("vector", lambda e, xs=xs, co=co, t=t: e.reduce_sum(
                                    co[:, t:t + 1], xs[:, t * 1024:(t + 1) * 1024], axis=AX),
                                    sm["s_red"] if t == 3 else None)
                            vred[j] = sm["s_red"].v

                            W("scalar", sx, vx)
                            if idx >= 1:
                                W("scalar", sm["s_sq"], vsq[j + 1])
                            for t in range(4):
                                r = E("scalar", lambda e, xs=xs, co=co, t=t: e.activation(
                                    out=dump[:, t * 1024:(t + 1) * 1024], in_=xs[:, t * 1024:(t + 1) * 1024],
                                    func=AF.Square, accum_out=co[:, 4 + t:5 + t]),
                                    sm["s_sq"] if t == 3 else None)
                                if r:
                                    vsq[j] = r

                        def emit_BC(j):
                            idx = 15 - j
                            par = j % 3
                            bpar = j % 2
                            xs = xstage[:, par * 4096:(par + 1) * 4096]
                            xkb = xnkm_b[:, par * 4096:(par + 1) * 4096]
                            xfb = xnfm_b[:, par * 4096:(par + 1) * 4096]
                            co = cols[:, par * 24:(par + 1) * 24]
                            W("gpsimd", sm["s_sq"], vsq[j])
                            W("gpsimd", sm["s_red"], vred[j])
                            E("gpsimd", lambda e, co=co: e.tensor_scalar_mul(co[:, 8:12], co[:, 0:4], -1.0 / D))
                            v1 = E("gpsimd", lambda e, co=co: e.tensor_scalar_mul(co[:, 12:16], co[:, 4:8], 1.0 / D),
                                   sm["s_cols"])
                            W("gpsimd", sm["s_cols"], v1)
                            v2 = E("gpsimd", lambda e, co=co: e.tensor_mul(
                                co[:, 16:20], co[:, 8:12], co[:, 8:12]), sm["s_cols"])
                            W("gpsimd", sm["s_cols"], v2)
                            v3 = E("gpsimd", lambda e, co=co: e.tensor_sub(co[:, 16:20], co[:, 12:16], co[:, 16:20]),
                                   sm["s_cols"])
                            W("scalar", sm["s_cols"], v3)
                            v4 = E("scalar", lambda e, co=co: e.activation(
                                out=co[:, 20:24], in_=co[:, 16:20], func=AF.Sqrt, bias=EPS), sm["s_sq"])
                            vsq[j] = v4
                            W("vector", sm["s_sq"], v4)
                            v5 = E("vector", lambda e, co=co: e.reciprocal(co[:, 20:24], co[:, 20:24]),
                                   sm["s_rcp"])
                            W("gpsimd", sm["s_rcp"], v5)
                            v6 = E("gpsimd", lambda e, co=co: e.tensor_mul(
                                co[:, 16:20], co[:, 8:12], co[:, 20:24]), sm["s_cols"])
                            W("gpsimd", sm["s_cols"], v6)

                            if idx >= 3:
                                W("vector", sm[f"s_w1{par}"], vw_blk[j + 3])
                                W("vector", sm["s_tr1"], vtr_last[j + 3])
                            W("vector", sm["s_cols"], sm["s_cols"].v)
                            W("vector", sm["s_rcp"], v5)
                            for t in (0, 1, 2, 3):
                                vng[j] = E("vector", lambda e, xs=xs, xkb=xkb, co=co, t=t: e.tensor_scalar(
                                    out=xkb[:, t * 1024:(t + 1) * 1024], in0=xs[:, t * 1024:(t + 1) * 1024],
                                    scalar1=co[:, 20 + t:21 + t], scalar2=co[:, 16 + t:17 + t],
                                    op0=ALU.mult, op1=ALU.add), sm["s_ng"])
                            vn[j] = E("vector", lambda e, xs=xs, co=co, par=par: e.tensor_scalar(
                                out=xf32_b[:, par * 1024:(par + 1) * 1024], in0=xs[:, 3 * 1024:4 * 1024],
                                scalar1=co[:, 23:24], scalar2=co[:, 19:20],
                                op0=ALU.mult, op1=ALU.add), sm["s_n"])

                            W("tensor", sm["s_n"], vn[j])
                            W("tensor", sm["s_ng"], vng[j])
                            if idx == 0:
                                W("tensor", sm["s_x"], 48)
                            for c in range(8):
                                slot = 2 * bpar + (c % 2)
                                if slot in vfme_slot:
                                    W("tensor", sm["s_fme"], vfme_slot[slot])
                                for t in range(4):
                                    r = E("tensor", lambda e, xkb=xkb, c=c, t=t, slot=slot: e.transpose(
                                        ps_t[:, slot * 1024 + t * 128: slot * 1024 + (t + 1) * 128],
                                        xkb[:, t * 1024 + c * 128: t * 1024 + (c + 1) * 128],
                                        ident1[:, :]), sm["s_tr1"] if t == 3 else None)
                                    if r:
                                        vtr_last[j] = r
                                W("scalar", sm["s_tr1"], vtr_last[j])
                                if c == 0 and idx >= 3:
                                    W("scalar", sm[f"s_w1{par}"], vw_blk[j + 3])
                                vfme_slot[slot] = E("scalar", lambda e, xfb=xfb, c=c, slot=slot: e.copy(
                                    out=xfb[:, c * 512:(c + 1) * 512],
                                    in_=ps_t[:, slot * 1024:slot * 1024 + 512]), sm["s_fme"])
                            vfme[j] = sm["s_fme"].v

                            W("gpsimd", sm["s_n"], vn[j])
                            W("gpsimd", sm["s_ng"], vng[j])
                            sw = sm[f"s_w1{par}"]
                            E("gpsimd", lambda e, xkb=xkb, j=j: e.dma_start(
                                out=xnkm_d[4 * j:4 * j + 4, :, :].transpose([1, 0, 2]),
                                in_=xkb.rearrange("p (t d) -> p t d", t=4)), sw, 16)
                            E("gpsimd", lambda e, par=par, so=15 - j: e.dma_start(
                                out=xnf32_d[so, :, :],
                                in_=xf32_b[:, par * 1024:(par + 1) * 1024]), sw, 16)
                            W("gpsimd", sm["s_fme"], vfme[j])
                            vw_blk[j] = E("gpsimd", lambda e, xfb=xfb, j=j: e.dma_start(
                                out=xnfm_d[j, :, :], in_=xfb[:, :]), sw, 16)

                            if j % 4 == 0:
                                emit_qgroup((12 - j) // 4)

                        for k in range(17):
                            if k < 16:
                                emit_A(15 - k)
                            if k >= 1:
                                emit_BC(15 - (k - 1))
                        W("gpsimd", sm["s_w10"], sm["s_w10"].v)
                        W("gpsimd", sm["s_w11"], sm["s_w11"].v)
                        W("gpsimd", sm["s_w12"], sm["s_w12"].v)
                        flush("p1")

                with ExitStack() as S_p2:
                    xnkm_s = S_p2.enter_context(nc.sbuf_tensor("xnkm_s", [128, 24 * 1024], BF16))
                    kT_s = S_p2.enter_context(nc.sbuf_tensor("kT_s", [128, H * 24 * 128], BF16))
                    xnfm2 = S_p2.enter_context(nc.sbuf_tensor("xnfm2", [128, 2 * 4096], BF16))
                    sim_sb = S_p2.enter_context(nc.sbuf_tensor("sim_sb", [128, 4 * 512], BF16))
                    mask_sb = S_p2.enter_context(nc.sbuf_tensor("mask_sb", [128, 8 * 256], BF16))
                    qgl_sb = S_p2.enter_context(nc.sbuf_tensor("qgl_sb", [128, 2 * 256], F32))
                    abf_sb = S_p2.enter_context(nc.sbuf_tensor("abf_sb", [128, 2 * 1024], BF16))
                    apo_sb = S_p2.enter_context(nc.sbuf_tensor("apo_sb", [128, 2 * 1024], F32))
                    api_sb = S_p2.enter_context(nc.sbuf_tensor("api_sb", [128, 4 * 1024], F32))
                    stg_sb = S_p2.enter_context(nc.sbuf_tensor("stg_sb", [128, 2 * 1024], F32))
                    ps_a = S_p2.enter_context(nc.psum_tensor("ps_a", [128, 2048], F32))
                    ps_s = S_p2.enter_context(nc.psum_tensor("ps_s", [128, 2048], F32))
                    ps_k = ps_s[:, 1024:2048]

                    def pairs_in(si):
                        return [p for p in range(NPAIR) if 64 - 8 * p > STRIPES[si][0]]

                    last_stripe_of = {}
                    for p in range(NPAIR):
                        for si in range(3):
                            if p in pairs_in(si):
                                last_stripe_of[p] = si

                    vA_free = 0
                    vA_free_s = 0
                    vstg_free = 0
                    vbank_free = [None, None, None, None]
                    vabf_free = [0, 0]
                    vapo_free = [0, 0]
                    vapi_used = [0, 0, 0, 0]
                    ci_uid = 0
                    vqgl_used = [0, 0]
                    prev_vsim = [None, None, None, None]
                    vfm_pre = {}

                    for si in range(3):
                        v0, v1 = STRIPES[si]
                        nblk = (v1 - v0) // 4
                        prs = pairs_in(si)

                        kmm_base = sm["s_kmm"].v
                        vkm_last = 0
                        vkm_blk = [0] * nblk
                        vfm_blk = [0] * nblk
                        for bb in range(nblk):
                            j = v0 // 4 + bb
                            vkm_last = E("sync", lambda e, bb=bb, j=j: e.dma_start(
                                out=xnkm_s.rearrange("p (t d) -> p t d", t=24)[:, 4 * bb:4 * bb + 4, :],
                                in_=xnkm_d[4 * j:4 * j + 4, :, :].transpose([1, 0, 2])), sm["s_kmld"], 16)
                            vkm_blk[bb] = vkm_last
                        qgl_pre = {}
                        order_pairs = sorted(prs, key=lambda q: min(64 - 8 * q, v1))
                        seen_par = set()
                        for p in order_pairs:
                            NKAp = 64 - 8 * p
                            u1p = min(NKAp, v1)
                            tl = [v for v in range(v0, u1p) if v >= NKAp - 8]
                            if tl and (p % 2) not in seen_par:
                                seen_par.add(p % 2)
                                sqg = sm["s_qgl0" if p % 2 == 0 else "s_qgl1"]
                                W("sync", sm["s_vm"], vqgl_used[p % 2])
                                qgl_pre[p] = E("sync", lambda e, p=p: e.dma_start(
                                    out=qgl_sb[:, (p % 2) * 256:(p % 2 + 1) * 256],
                                    in_=qglob_in[p, :, :]), sqg, 16)
                        for bb in range(min(nblk, 2)):
                            j = v0 // 4 + bb
                            if (si, bb) in vfm_pre:
                                vfm_blk[bb] = vfm_pre[(si, bb)]
                                continue
                            vfm_blk[bb] = E("sync", lambda e, bb=bb, j=j: e.dma_start(
                                out=xnfm2[:, (bb % 2) * 4096:(bb % 2 + 1) * 4096],
                                in_=xnfm_d[j, :, :]), sm["s_fmld0" if bb % 2 == 0 else "s_fmld1"], 16)

                        vke_seq = []

                        def emit_kblock(bb):
                            W("tensor", sm["s_fmld0" if bb % 2 == 0 else "s_fmld1"], vfm_blk[bb])
                            for pv in (2, 3):
                                if prev_vsim[pv] is not None:
                                    sh, sv = prev_vsim[pv]
                                    W("tensor", sh, sv)
                            for h in range(H):
                                idx = bb * H + h
                                bank = idx % 2
                                if idx >= 2:
                                    W("tensor", sm["s_ke"], vke_seq[idx - 2])
                                vkmm = 0
                                for c in range(8):
                                    r = E("tensor", lambda e, bank=bank, c=c, h=h, bb=bb: e.matmul(
                                        ps_k[:, bank * 512:(bank + 1) * 512],
                                        wkbf[:, c * 1024 + h * 128: c * 1024 + (h + 1) * 128],
                                        xnfm2[:, (bb % 2) * 4096 + c * 512:(bb % 2) * 4096 + (c + 1) * 512],
                                        start=(c == 0), stop=(c == 7)),
                                        sm["s_kmm"] if c == 7 else None)
                                    if r:
                                        vkmm = r
                                W("scalar", sm["s_kmm"], vkmm)
                                vke_seq.append(E("scalar", lambda e, bank=bank, h=h, bb=bb: e.copy(
                                    out=kT_s[:, h * 3072 + bb * 512: h * 3072 + (bb + 1) * 512],
                                    in_=ps_k[:, bank * 512:(bank + 1) * 512]), sm["s_ke"]))
                            if bb + 2 < nblk:
                                j2 = v0 // 4 + bb + 2
                                W("sync", sm["s_kmm"], sm["s_kmm"].v)
                                vfm_blk[bb + 2] = E("sync", lambda e, bb=bb, j2=j2: e.dma_start(
                                    out=xnfm2[:, (bb % 2) * 4096:(bb % 2 + 1) * 4096],
                                    in_=xnfm_d[j2, :, :]), sm["s_fmld0" if bb % 2 == 0 else "s_fmld1"], 16)

                        bb_done = 0
                        for p in sorted(prs, key=lambda q: min(64 - 8 * q, v1)):
                            NKA = 64 - 8 * p
                            NKB = NKA - 4
                            u0, u1 = v0, min(NKA, v1)
                            is_last = (si == last_stripe_of[p])
                            has_ci = (si > 0)

                            need_blocks = (u1 - v0 + 3) // 4
                            while bb_done < need_blocks:
                                emit_kblock(bb_done)
                                bb_done += 1
                            vke_pair = vke_seq[need_blocks * H - 1]
                            vkm_pair = vkm_last
                            pair_first = True

                            tail_vis = [v for v in range(u0, u1) if v >= NKB - 4]
                            vmask_p = 0
                            if tail_vis:
                                sqg = sm["s_qgl0" if p % 2 == 0 else "s_qgl1"]
                                if p in qgl_pre:
                                    vqgl_p = qgl_pre[p]
                                else:
                                    W("sync", sm["s_vm"], vqgl_used[p % 2])
                                    vqgl_p = E("sync", lambda e, p=p: e.dma_start(
                                        out=qgl_sb[:, (p % 2) * 256:(p % 2 + 1) * 256],
                                        in_=qglob_in[p, :, :]), sqg, 16)
                                W("vector", sqg, vqgl_p)
                                for idx, v in enumerate(tail_vis):
                                    wid = 256 if v < NKB else 128
                                    vmask_p = E("vector", lambda e, idx=idx, v=v, p=p, wid=wid: e.tensor_scalar(
                                        out=mask_sb[:, idx * 256: idx * 256 + wid],
                                        in0=qgl_sb[:, (p % 2) * 256:(p % 2) * 256 + wid],
                                        scalar1=kglob_sb[:, v:v + 1], scalar2=INV_N,
                                        op0=ALU.is_ge, op1=ALU.mult), sm["s_vm"])
                                vqgl_used[p % 2] = vmask_p

                            head_cache = None
                            for h in range(H):
                                nv = u1 - u0
                                groups = [(u0 + 2 * k, min(u0 + 2 * k + 2, u1))
                                          for k in range((nv + 1) // 2)]
                                ng = len(groups)
                                vsc_g = [0] * ng
                                vsim_g = [None] * ng

                                if pair_first:
                                    W("tensor", sm["s_ke"], vke_pair)
                                    W("tensor", sm["s_kmld"], vkm_pair)
                                    pair_first = False
                                vA_free_u = vA_free
                                vA_free_su = vA_free_s
                                vbank_u = list(vbank_free)
                                vke_floor = vke_seq[-1] if vke_seq else 0

                                def emit_scores(gi, hh, vsc_a, vsim_a, p=p, NKB=NKB,
                                                groups=groups, v0=v0):
                                    glo, ghi = groups[gi]
                                    gpar = gi % 4
                                    if gi >= 4:
                                        sh, sv = vsim_a[gi - 4]
                                        W("tensor", sh, sv)
                                    else:
                                        if prev_vsim[gpar] is not None:
                                            sh, sv = prev_vsim[gpar]
                                            W("tensor", sh, sv)
                                        if gpar >= 2:
                                            W("tensor", sm["s_ke"], vke_floor)
                                    for vv in range(glo, ghi):
                                        wid = 256 if vv < NKB else 128
                                        lt = vv - v0
                                        half = vv - glo
                                        r = E("tensor", lambda e, gpar=gpar, half=half, wid=wid, lt=lt, hh=hh: e.matmul(
                                            ps_s[:, gpar * 512 + half * 256: gpar * 512 + half * 256 + wid],
                                            kT_s[:, hh * 3072 + lt * 128: hh * 3072 + (lt + 1) * 128],
                                            qT_sb[:, hh * 2048 + p * 256: hh * 2048 + p * 256 + wid],
                                            start=True, stop=True),
                                            sm["s_sc"] if vv == ghi - 1 else None)
                                        if r:
                                            vsc_a[gi] = r

                                def emit_relu(gi, vsc_a, vsim_a, p=p, NKB=NKB, groups=groups,
                                              tail_vis=tail_vis, vmask_p=vmask_p):
                                    glo, ghi = groups[gi]
                                    gpar = gi % 4
                                    wid = (ghi - glo - 1) * 256 + (256 if ghi - 1 < NKB else 128)
                                    masked = (glo >= NKB - 4)
                                    W("scalar", sm["s_sc"], vsc_a[gi])
                                    if not masked:
                                        v = E("scalar", lambda e, gpar=gpar, wid=wid: e.activation(
                                            out=sim_sb[:, gpar * 512: gpar * 512 + wid],
                                            in_=ps_s[:, gpar * 512: gpar * 512 + wid],
                                            func=AF.Relu, scale=INV_N), sm["s_sim"])
                                        vsim_a[gi] = (sm["s_sim"], v)
                                    else:
                                        vr = E("scalar", lambda e, gpar=gpar, wid=wid: e.activation(
                                            out=sim_sb[:, gpar * 512: gpar * 512 + wid],
                                            in_=ps_s[:, gpar * 512: gpar * 512 + wid],
                                            func=AF.Relu, scale=1.0), sm["s_sim"])
                                        W("vector", sm["s_sim"], vr)
                                        W("vector", sm["s_vm"], vmask_p)
                                        vlast = 0
                                        for vv in range(glo, ghi):
                                            midx = tail_vis.index(vv)
                                            wid_v = 256 if vv < NKB else 128
                                            half = vv - glo
                                            vlast = E("vector", lambda e, gpar=gpar, half=half, midx=midx, wid_v=wid_v: e.tensor_mul(
                                                sim_sb[:, gpar * 512 + half * 256: gpar * 512 + half * 256 + wid_v],
                                                sim_sb[:, gpar * 512 + half * 256: gpar * 512 + half * 256 + wid_v],
                                                mask_sb[:, midx * 256: midx * 256 + wid_v]), sm["s_vm"])
                                        vsim_a[gi] = (sm["s_vm"], vlast)

                                last_vis = {0: u1 - 1, 1: min(u1, NKB) - 1}

                                def emit_amm(gi, NKB=NKB, groups=groups, ng=ng,
                                             vsim_g=vsim_g, u0=u0, v0=v0, last_vis=last_vis):
                                    glo, ghi = groups[gi]
                                    gpar = gi % 4
                                    sh, sv = vsim_g[gi]
                                    W("tensor", sh, sv)
                                    for vv in range(glo, ghi):
                                        halfcol = (vv - glo) * 256
                                        lt = vv - v0
                                        tiles = (0, 1) if vv < NKB else (0,)
                                        for tt in tiles:
                                            for dh in range(2):
                                                first = (vv == u0)
                                                if first and vbank_u[2 * tt + dh] is not None:
                                                    bs, bv = vbank_u[2 * tt + dh]
                                                    W("tensor", bs, bv)
                                                stop = (vv == last_vis[tt])
                                                is_final = (gi == ng - 1 and vv == ghi - 1
                                                            and tt == tiles[-1] and dh == 1)
                                                E("tensor", lambda e, gpar=gpar, halfcol=halfcol, lt=lt, tt=tt, dh=dh, first=first, stop=stop: e.matmul(
                                                    ps_a[:, tt * 1024 + dh * 512: tt * 1024 + (dh + 1) * 512],
                                                    sim_sb[:, gpar * 512 + halfcol + tt * 128: gpar * 512 + halfcol + (tt + 1) * 128],
                                                    xnkm_s[:, lt * 1024 + dh * 512: lt * 1024 + (dh + 1) * 512],
                                                    start=first, stop=stop),
                                                    sm["s_amm"] if is_final else None)

                                vapi_ready = 0
                                api_base = 0
                                sapi = None
                                if has_ci:
                                    api_base = (ci_uid % 2) * 2
                                    sapi = sm["s_api0" if ci_uid % 2 == 0 else "s_api1"]
                                    ci_uid += 1
                                    for tt in range(2):
                                        W("sync", sm["s_ae"], vapi_used[api_base + tt])
                                        vapi_ready = E("sync", lambda e, p=p, h=h, tt=tt, api_base=api_base: e.dma_start(
                                            out=api_sb[:, (api_base + tt) * 1024:(api_base + tt + 1) * 1024],
                                            in_=apart_d[p, h, :, tt * 1024:(tt + 1) * 1024]), sapi, 16)

                                if head_cache is not None:
                                    vsc_g[0], vsim_g[0] = head_cache[0]
                                    if ng > 1:
                                        vsc_g[1], vsim_g[1] = head_cache[1]
                                    head_cache = None
                                else:
                                    emit_scores(0, h, vsc_g, vsim_g)
                                    emit_relu(0, vsc_g, vsim_g)
                                    if ng > 1:
                                        emit_scores(1, h, vsc_g, vsim_g)
                                        emit_relu(1, vsc_g, vsim_g)

                                last_b01 = max(k for k in range(ng) if k % 4 in (0, 1))
                                for gi in range(ng):
                                    if gi + 2 < ng:
                                        emit_scores(gi + 2, h, vsc_g, vsim_g)
                                        emit_relu(gi + 2, vsc_g, vsim_g)
                                    emit_amm(gi)
                                    if gi == last_b01 and h + 1 < H:
                                        for g4 in range(4):
                                            cand = [vsim_g[k] for k in range(ng) if k % 4 == g4]
                                            if cand:
                                                prev_vsim[g4] = cand[-1]
                                        nvsc = [0] * ng
                                        nvsim = [None] * ng
                                        emit_scores(0, h + 1, nvsc, nvsim)
                                        emit_relu(0, nvsc, nvsim)
                                        hc = [(nvsc[0], nvsim[0])]
                                        if ng > 1:
                                            emit_scores(1, h + 1, nvsc, nvsim)
                                            emit_relu(1, nvsc, nvsim)
                                            hc.append((nvsc[1], nvsim[1]))
                                        head_cache = hc
                                vamm = sm["s_amm"].v
                                if h + 1 >= H:
                                    for gpar2 in range(4):
                                        cand = [vsim_g[k] for k in range(ng) if k % 4 == gpar2]
                                        if cand:
                                            prev_vsim[gpar2] = cand[-1]

                                tileB_has = (u0 < min(NKB, u1))

                                if is_last:
                                    sad0, sad1 = sm["s_adma0"], sm["s_adma1"]
                                    dst, freev = abf_sb, vabf_free
                                else:
                                    sad0, sad1 = sm["s_admb0"], sm["s_admb1"]
                                    dst, freev = apo_sb, vapo_free
                                W("scalar", sm["s_amm"], vamm)
                                W("vector", sm["s_amm"], vamm)
                                if not has_ci:
                                    W("scalar", sad0, freev[0])
                                    vb0 = E("scalar", lambda e, dst=dst: e.copy(
                                        out=dst[:, 0:512], in_=ps_a[:, 0:512]), sm["s_aes"])
                                    W("vector", sad0, freev[0])
                                    vb1 = E("vector", lambda e, dst=dst: e.tensor_copy(
                                        dst[:, 512:1024], ps_a[:, 512:1024]), sm["s_ae"])
                                    W("scalar", sad1, freev[1])
                                    v_e0 = vb2 = E("scalar", lambda e, dst=dst: e.copy(
                                        out=dst[:, 1024:1536], in_=ps_a[:, 1024:1536]), sm["s_aes"])
                                    W("vector", sad1, freev[1])
                                    v_e1 = vb3 = E("vector", lambda e, dst=dst: e.tensor_copy(
                                        dst[:, 1536:2048], ps_a[:, 1536:2048]), sm["s_ae"])
                                    vbank_free = [(sm["s_aes"], vb0), (sm["s_ae"], vb1),
                                                  (sm["s_aes"], vb2), (sm["s_ae"], vb3)]
                                    W("gpsimd", sm["s_aes"], v_e0)
                                    W("gpsimd", sm["s_ae"], v_e1)
                                    tgt = a_d if is_last else apart_d
                                    freev[0] = E("gpsimd", lambda e, tgt=tgt, dst=dst, p=p, h=h: e.dma_start(
                                        out=tgt[p, h, :, 0:1024], in_=dst[:, 0:1024]), sad0, 16)
                                    freev[1] = E("gpsimd", lambda e, tgt=tgt, dst=dst, p=p, h=h: e.dma_start(
                                        out=tgt[p, h, :, 1024:2048], in_=dst[:, 1024:2048]), sad1, 16)
                                else:
                                    W("scalar", sm["s_ae"], vstg_free)
                                    W("vector", sm["s_ae"], vstg_free)
                                    vb0 = E("scalar", lambda e: e.copy(
                                        out=stg_sb[:, 0:512], in_=ps_a[:, 0:512]), sm["s_aes"])
                                    vb1 = E("vector", lambda e: e.tensor_copy(
                                        stg_sb[:, 512:1024], ps_a[:, 512:1024]), sm["s_ae"])
                                    if tileB_has:
                                        v_e0 = vb2 = E("scalar", lambda e: e.copy(
                                            out=stg_sb[:, 1024:1536], in_=ps_a[:, 1024:1536]), sm["s_aes"])
                                        v_e1 = vb3 = E("vector", lambda e: e.tensor_copy(
                                            stg_sb[:, 1536:2048], ps_a[:, 1536:2048]), sm["s_ae"])
                                    else:
                                        v_e0 = vb2 = sm["s_aes"].v
                                        v_e1 = vb3 = sm["s_ae"].v
                                    vbank_free = [(sm["s_aes"], vb0), (sm["s_ae"], vb1),
                                                  (sm["s_aes"], vb2), (sm["s_ae"], vb3)]
                                    W("vector", sapi, vapi_ready)
                                    W("vector", sm["s_aes"], v_e0)
                                    W("vector", sm["s_ae"], v_e1)
                                    W("vector", sad0, freev[0])
                                    va0 = E("vector", lambda e, dst=dst, api_base=api_base: e.tensor_add(
                                        dst[:, 0:1024], stg_sb[:, 0:1024],
                                        api_sb[:, api_base * 1024:(api_base + 1) * 1024]), sm["s_ae"])
                                    vapi_used[api_base] = va0
                                    W("vector", sad1, freev[1])
                                    if tileB_has:
                                        va1 = E("vector", lambda e, dst=dst, api_base=api_base: e.tensor_add(
                                            dst[:, 1024:2048], stg_sb[:, 1024:2048],
                                            api_sb[:, (api_base + 1) * 1024:(api_base + 2) * 1024]), sm["s_ae"])
                                    else:
                                        va1 = E("vector", lambda e, dst=dst, api_base=api_base: e.tensor_copy(
                                            dst[:, 1024:2048],
                                            api_sb[:, (api_base + 1) * 1024:(api_base + 2) * 1024]), sm["s_ae"])
                                    vapi_used[api_base + 1] = va1
                                    vstg_free = va1
                                    W("gpsimd", sm["s_ae"], va0)
                                    tgt = a_d if is_last else apart_d
                                    freev[0] = E("gpsimd", lambda e, tgt=tgt, dst=dst, p=p, h=h: e.dma_start(
                                        out=tgt[p, h, :, 0:1024], in_=dst[:, 0:1024]), sad0, 16)
                                    W("gpsimd", sm["s_ae"], va1)
                                    freev[1] = E("gpsimd", lambda e, tgt=tgt, dst=dst, p=p, h=h: e.dma_start(
                                        out=tgt[p, h, :, 1024:2048], in_=dst[:, 1024:2048]), sad1, 16)
                                vA_free_s = sm["s_aes"].v
                                vA_free = sm["s_ae"].v if has_ci else v_e1
                                if has_ci:
                                    vA_free = v_e1
                        if si < 2:
                            kmm_end = sm["s_kmm"].v
                            nv0 = STRIPES[si + 1][0]
                            for bb in range(2):
                                j = nv0 // 4 + bb
                                W("sync", sm["s_kmm"], kmm_end)
                                vfm_pre[(si + 1, bb)] = E("sync", lambda e, bb=bb, j=j: e.dma_start(
                                    out=xnfm2[:, (bb % 2) * 4096:(bb % 2 + 1) * 4096],
                                    in_=xnfm_d[j, :, :]), sm["s_fmld0" if bb % 2 == 0 else "s_fmld1"], 16)
                        for snm in ("s_adma0", "s_adma1", "s_admb0", "s_admb1"):
                            W("gpsimd", sm[snm], sm[snm].v)
                        flush(f"p2_s{si}")

        with ExitStack() as S_f1:
            wgbf = S_f1.enter_context(nc.sbuf_tensor("wgbf", [128, 8 * HID], BF16))
            wubf = S_f1.enter_context(nc.sbuf_tensor("wubf", [128, 8 * HID], BF16))
            xnfm_tb = S_f1.enter_context(nc.sbuf_tensor("xnfm_tb", [128, 2 * 4096], BF16))
            hT_buf = S_f1.enter_context(nc.sbuf_tensor("hT_buf", [128, 24 * 512], BF16))
            silu_sb = S_f1.enter_context(nc.sbuf_tensor("silu_sb", [128, 4 * 512], F32))
            ps_g = S_f1.enter_context(nc.psum_tensor("ps_g", [128, 2048], F32))
            ps_u = S_f1.enter_context(nc.psum_tensor("ps_u", [128, 2048], F32))

            vfwc = [0] * 8
            wg3 = wgT_in.rearrange("(c p) j -> p c j", p=128)
            wu3 = wuT_in.rearrange("(c p) j -> p c j", p=128)
            wgo = wgbf.rearrange("p (c j) -> p c j", c=8)
            wuo = wubf.rearrange("p (c j) -> p c j", c=8)
            for r in range(8):
                E("gpsimd", lambda e, r=r: e.dma_start(
                    out=wgo[:, :, r * 384:(r + 1) * 384],
                    in_=wg3[:, :, r * 384:(r + 1) * 384]), sm[f"s_fwc{r}"], 16)
                vfwc[r] = E("gpsimd", lambda e, r=r: e.dma_start(
                    out=wuo[:, :, r * 384:(r + 1) * 384],
                    in_=wu3[:, :, r * 384:(r + 1) * 384]), sm[f"s_fwc{r}"], 16)

            vhv_seq = []
            vsil_seq = []
            vhtw = [0] * 4
            vgu_tb = [0] * 4
            for tb in range(4):
                par = tb % 2
                xft = xnfm_tb[:, par * 4096:(par + 1) * 4096]
                xft3 = xnfm_tb.rearrange("p (b c q) -> p b c q", b=2, c=8)
                if tb >= 2:
                    W("sync", sm["s_gu"], vgu_tb[tb - 2])
                sft = sm["s_fmtb0" if par == 0 else "s_fmtb1"]
                vld = 0
                for s in range(4):
                    blkj = 15 - (4 * tb + s)
                    src = xnfm_d[blkj].rearrange("p (c q) -> p c q", c=8)[:, :, 384:512]
                    vld = E("sync", lambda e, src=src, xft3=xft3, par=par, s=s: e.dma_start(
                        out=xft3[:, par, :, s * 128:(s + 1) * 128], in_=src), sft, 16)
                W("tensor", sft, vld)
                for jc in range(24):
                    if tb == 0 and jc % 3 == 0:
                        W("tensor", sm[f"s_fwc{jc // 3}"], vfwc[jc // 3])
                    idx = tb * 24 + jc
                    bank = idx % 4
                    if idx >= 4:
                        W("tensor", sm["s_silu"], vsil_seq[idx - 4])
                        W("tensor", sm["s_hv"], vhv_seq[idx - 4])
                    for c in range(8):
                        E("tensor", lambda e, bank=bank, c=c, jc=jc, xft=xft: e.matmul(
                            ps_g[:, bank * 512:(bank + 1) * 512],
                            wgbf[:, c * HID + jc * 128: c * HID + (jc + 1) * 128],
                            xft[:, c * 512:(c + 1) * 512],
                            start=(c == 0), stop=(c == 7)))
                    for c in range(8):
                        r = E("tensor", lambda e, bank=bank, c=c, jc=jc, xft=xft: e.matmul(
                            ps_u[:, bank * 512:(bank + 1) * 512],
                            wubf[:, c * HID + jc * 128: c * HID + (jc + 1) * 128],
                            xft[:, c * 512:(c + 1) * 512],
                            start=(c == 0), stop=(c == 7)),
                            sm["s_gu"] if c == 7 else None)
                        if r:
                            vgu_tb[tb] = r
                    W("scalar", sm["s_gu"], vgu_tb[tb])
                    vsil_seq.append(E("scalar", lambda e, bank=bank: e.activation(
                        out=silu_sb[:, bank * 512:(bank + 1) * 512],
                        in_=ps_g[:, bank * 512:(bank + 1) * 512],
                        func=AF.Silu, scale=INV127), sm["s_silu"]))
                    W("vector", sm["s_silu"], vsil_seq[-1])
                    W("vector", sm["s_gu"], vgu_tb[tb])
                    if jc == 0 and tb >= 1:
                        W("vector", sm["s_htw"], vhtw[tb - 1])
                    vhv_seq.append(E("vector", lambda e, bank=bank, jc=jc: e.scalar_tensor_tensor(
                        out=hT_buf[:, jc * 512:(jc + 1) * 512],
                        in0=ps_u[:, bank * 512:(bank + 1) * 512], scalar=INV127,
                        in1=silu_sb[:, bank * 512:(bank + 1) * 512],
                        op0=ALU.mult, op1=ALU.mult), sm["s_hv"]))
                W("gpsimd", sm["s_hv"], vhv_seq[-1])
                vhtw[tb] = E("gpsimd", lambda e, tb=tb: e.dma_start(
                    out=ht_d[tb, :, :], in_=hT_buf[:, :]), sm["s_htw"], 16)
            W("gpsimd", sm["s_htw"], sm["s_htw"].v)
            flush("f1")

        S_wvx = ExitStack()
        wv32 = S_wvx.enter_context(nc.sbuf_tensor("wv32", [128, 8192], F32))
        wvbf = S_wvx.enter_context(nc.sbuf_tensor("wvbf", [128, 2 * 8192], BF16))
        ident2 = S_wvx.enter_context(nc.sbuf_tensor("ident2", [128, 128], BF16))
        vwvc = [0] * H
        with ExitStack() as S_f2:
            wdbf = S_f2.enter_context(nc.sbuf_tensor("wdbf", [128, 24 * 1024], BF16))
            hT_rd = S_f2.enter_context(nc.sbuf_tensor("hT_rd", [128, 2 * 12288], BF16))
            g_sb = S_f2.enter_context(nc.sbuf_tensor("g_sb", [128, 2 * 1024], F32))
            ps_d = S_f2.enter_context(nc.psum_tensor("ps_d", [128, 2048], F32))

            v_id2 = E("sync", lambda e: e.dma_start(out=ident2[:, :], in_=ident_in[:, :]), sm["s_wvc"], 16)
            vwv0 = E("sync", lambda e: e.dma_start(
                out=wv32.rearrange("p (c v) -> p c v", c=8),
                in_=wvT_in[0].rearrange("(c p) v -> p c v", p=128)), sm["s_wv0"], 16)
            W("vector", sm["s_wv0"], vwv0)
            E("vector", lambda e: e.tensor_copy(wvbf[:, 0:4096], wv32[:, 0:4096]), sm["s_wv1"])
            vwvc[0] = E("vector", lambda e: e.tensor_copy(wvbf[:, 4096:8192], wv32[:, 4096:8192]),
                        sm["s_wv1"])

            vfwd = [0] * 8
            wd3 = wdT_in.rearrange("(c p) d -> p c d", p=128)
            wdo = wdbf.rearrange("p (c d) -> p c d", c=24)
            for r in range(8):
                vfwd[r] = E("gpsimd", lambda e, r=r: e.dma_start(
                    out=wdo[:, r * 3:(r + 1) * 3, :],
                    in_=wd3[:, r * 3:(r + 1) * 3, :]), sm[f"s_fwc{r}"], 16)

            vgs_seq = []
            vgw_seq = []
            vdm_seq = []
            for tb in range(4):
                par = tb % 2
                hrt = hT_rd[:, par * 12288:(par + 1) * 12288]
                if tb >= 2:
                    W("sync", sm["s_dmm"], vdm_seq[4 * (tb - 2) + 3])
                sht = sm["s_htr0" if par == 0 else "s_htr1"]
                vht = E("sync", lambda e, hrt=hrt, tb=tb: e.dma_start(
                    out=hrt[:, :], in_=ht_d[tb, :, :]), sht, 16)
                for s in range(4):
                    u = 4 * tb + s
                    bank = u % 2
                    W("tensor", sht, vht)
                    if u >= 2:
                        W("tensor", sm["s_gs"], vgs_seq[u - 2])
                    vdm = 0
                    for jc in range(24):
                        if u == 0 and jc % 3 == 0:
                            W("tensor", sm[f"s_fwc{jc // 3}"], vfwd[jc // 3])
                        for dh in range(2):
                            r = E("tensor", lambda e, bank=bank, jc=jc, dh=dh, hrt=hrt, s=s: e.matmul(
                                ps_d[:, bank * 1024 + dh * 512: bank * 1024 + (dh + 1) * 512],
                                hrt[:, jc * 512 + s * 128: jc * 512 + (s + 1) * 128],
                                wdbf[:, jc * 1024 + dh * 512: jc * 1024 + (dh + 1) * 512],
                                start=(jc == 0), stop=(jc == 23)),
                                sm["s_dmm"] if (jc == 23 and dh == 1) else None)
                            if r:
                                vdm = r
                    vdm_seq.append(vdm)
                    W("scalar", sm["s_dmm"], vdm)
                    if u >= 2:
                        W("scalar", sm["s_gw0" if u % 2 == 0 else "s_gw1"], vgw_seq[u - 2])
                    vgs = 0
                    for dh in range(2):
                        vgs = E("scalar", lambda e, bank=bank, dh=dh: e.activation(
                            out=g_sb[:, bank * 1024 + dh * 512: bank * 1024 + (dh + 1) * 512],
                            in_=ps_d[:, bank * 1024 + dh * 512: bank * 1024 + (dh + 1) * 512],
                            func=AF.Silu, scale=INV127), sm["s_gs"])
                    vgs_seq.append(vgs)
                    W("gpsimd", sm["s_gs"], vgs)
                    vgw_seq.append(E("gpsimd", lambda e, bank=bank, u=u: e.dma_start(
                        out=g_d[u, :, :], in_=g_sb[:, bank * 1024:(bank + 1) * 1024]),
                        sm["s_gw0" if u % 2 == 0 else "s_gw1"], 16))
            W("gpsimd", sm["s_gw0"], sm["s_gw0"].v)
            W("gpsimd", sm["s_gw1"], sm["s_gw1"].v)
            flush("f2")

        with ExitStack() as S_p3o:
            ho_sb = S_p3o.enter_context(nc.sbuf_tensor("ho_sb", [128, NSLOT * 1024], F32))
            a_rd = S_p3o.enter_context(nc.sbuf_tensor("a_rd", [128, 3 * 1024], BF16))
            at_sb = S_p3o.enter_context(nc.sbuf_tensor("at_sb", [128, 3 * 1024], BF16))
            xn_b = S_p3o.enter_context(nc.sbuf_tensor("xn_b", [128, 2 * 1024], F32))
            g_b = S_p3o.enter_context(nc.sbuf_tensor("g_b", [128, 2 * 1024], F32))
            x_b = S_p3o.enter_context(nc.sbuf_tensor("x_b", [128, 2 * 1024], F32))
            s_b = S_p3o.enter_context(nc.sbuf_tensor("s_b", [128, 2 * 1024], F32))
            o_b = S_p3o.enter_context(nc.sbuf_tensor("o_b", [128, 2 * 1024], F32))
            ps_tr3 = S_p3o.enter_context(nc.psum_tensor("ps_tr3", [128, 3 * 1024], BF16))
            ps_ho = S_p3o.enter_context(nc.psum_tensor("ps_ho", [128, 2048], F32))

            if True:
                UTOT = H * NSLOT
                vard = [0] * UTOT
                vtr3 = [0] * UTOT
                vatc = [0] * UTOT
                vhom = [0] * UTOT
                vhoa = [0] * UTOT
                vev = [0] * NSLOT
                vout = [0] * NSLOT

                vwvl = [vwv0] + [0] * (H - 1)

                def plan_wv(h):
                    W("sync", sm["s_wv1"], vwvc[h - 1])
                    vwvl[h] = E("sync", lambda e, h=h: e.dma_start(
                        out=wv32.rearrange("p (c v) -> p c v", c=8),
                        in_=wvT_in[h].rearrange("(c p) v -> p c v", p=128)),
                        sm["s_wv0"], 16)
                    W("vector", sm["s_wv0"], vwvl[h])
                    if h >= 2:
                        W("vector", sm["s_hom"], vhom[(h - 1) * NSLOT - 1])
                    base = (h % 2) * 8192
                    E("vector", lambda e, base=base: e.tensor_copy(
                        wvbf[:, base:base + 4096], wv32[:, 0:4096]), sm["s_wv1"])
                    vwvc[h] = E("vector", lambda e, base=base: e.tensor_copy(
                        wvbf[:, base + 4096:base + 8192], wv32[:, 4096:8192]), sm["s_wv1"])

                def plan_ard(u):
                    h, i = u // NSLOT, u % NSLOT
                    p, tt = i // 2, i % 2
                    if u >= 3:
                        W("sync", sm["s_tr3"], vtr3[u - 3])
                    vard[u] = E("sync", lambda e, p=p, h=h, tt=tt, u=u: e.dma_start(
                        out=a_rd[:, (u % 3) * 1024:(u % 3 + 1) * 1024],
                        in_=a_d[p, h, :, tt * 1024:(tt + 1) * 1024]),
                        sm[f"s_ard{u % 3}"], 16)

                def plan_tr(u):
                    W("tensor", sm[f"s_ard{u % 3}"], vard[u])
                    if u >= 3:
                        W("tensor", sm["s_atc"], vatc[u - 3][0])
                        W("tensor", sm["s_atcv"], vatc[u - 3][1])
                    if u == 0:
                        W("tensor", sm["s_wvc"], 16)
                    for c in range(8):
                        r = E("tensor", lambda e, c=c, u=u: e.transpose(
                            ps_tr3[:, (u % 3) * 1024 + c * 128: (u % 3) * 1024 + (c + 1) * 128],
                            a_rd[:, (u % 3) * 1024 + c * 128: (u % 3) * 1024 + (c + 1) * 128],
                            ident2[:, :]), sm["s_tr3"] if c == 7 else None)
                        if r:
                            vtr3[u] = r

                def plan_atc(u):
                    W("scalar", sm["s_tr3"], vtr3[u])
                    W("vector", sm["s_tr3"], vtr3[u])
                    va = E("scalar", lambda e, u=u: e.copy(
                        out=at_sb[:, (u % 3) * 1024:(u % 3) * 1024 + 512],
                        in_=ps_tr3[:, (u % 3) * 1024:(u % 3) * 1024 + 512]), sm["s_atc"])
                    vb = E("vector", lambda e, u=u: e.tensor_copy(
                        at_sb[:, (u % 3) * 1024 + 512:(u % 3 + 1) * 1024],
                        ps_tr3[:, (u % 3) * 1024 + 512:(u % 3 + 1) * 1024]), sm["s_atcv"])
                    vatc[u] = (va, vb)

                def plan_mm(u):
                    h = u // NSLOT
                    W("tensor", sm["s_atc"], vatc[u][0])
                    if (u % NSLOT) == 0:
                        W("tensor", sm["s_wv1"], vwvc[h])
                    if u >= 2:
                        W("tensor", sm["s_hoa"], vhoa[u - 2])
                    for c in range(8):
                        if c == 4:
                            W("tensor", sm["s_atcv"], vatc[u][1])
                        for vh in range(2):
                            r = E("tensor", lambda e, c=c, vh=vh, u=u, h=h: e.matmul(
                                ps_ho[:, (u % 2) * 1024 + vh * 512: (u % 2) * 1024 + vh * 512 + 512],
                                at_sb[:, (u % 3) * 1024 + c * 128: (u % 3) * 1024 + (c + 1) * 128],
                                wvbf[:, (h % 2) * 8192 + c * 1024 + vh * 512: (h % 2) * 8192 + c * 1024 + vh * 512 + 512],
                                start=(c == 0), stop=(c == 7)),
                                sm["s_hom"] if (c == 7 and vh == 1) else None)
                            if r:
                                vhom[u] = r

                def plan_add(u):
                    h, i = u // NSLOT, u % NSLOT
                    W("vector", sm["s_hom"], vhom[u])
                    if h == 0:
                        vhoa[u] = E("vector", lambda e, u=u, i=i: e.tensor_copy(
                            ho_sb[:, i * 1024:(i + 1) * 1024],
                            ps_ho[:, (u % 2) * 1024:(u % 2 + 1) * 1024]), sm["s_hoa"])
                    else:
                        vhoa[u] = E("vector", lambda e, u=u, i=i: e.tensor_add(
                            ho_sb[:, i * 1024:(i + 1) * 1024],
                            ps_ho[:, (u % 2) * 1024:(u % 2 + 1) * 1024],
                            ho_sb[:, i * 1024:(i + 1) * 1024]), sm["s_hoa"])

                def plan_epi(i):
                    par = i % 2
                    ser = sm["s_er0" if par == 0 else "s_er1"]
                    sou = sm["s_out0" if par == 0 else "s_out1"]
                    if i >= 2:
                        W("sync", sm["s_ev"], vev[i - 2])
                    E("sync", lambda e, i=i, par=par: e.dma_start(
                        out=xn_b[:, par * 1024:(par + 1) * 1024], in_=xnf32_d[i, :, :]), ser, 16)
                    E("sync", lambda e, i=i, par=par: e.dma_start(
                        out=g_b[:, par * 1024:(par + 1) * 1024], in_=g_d[i, :, :]), ser, 16)
                    ver = E("sync", lambda e, i=i, par=par: e.dma_start(
                        out=x_b[:, par * 1024:(par + 1) * 1024], in_=x3[63 - 4 * i, :, :]), ser, 16)

                    W("vector", ser, ver)
                    W("vector", sm["s_hoa"], vhoa_slot_final[i])
                    if i >= 2:
                        W("vector", sou, vout[i - 2])
                    v1 = E("vector", lambda e, i=i, par=par: e.tensor_add(
                        s_b[:, par * 1024:(par + 1) * 1024],
                        xn_b[:, par * 1024:(par + 1) * 1024],
                        ho_sb[:, i * 1024:(i + 1) * 1024]), sm["s_ev"])
                    W("vector", sm["s_ev"], v1)
                    v2 = E("vector", lambda e, par=par: e.tensor_mul(
                        s_b[:, par * 1024:(par + 1) * 1024],
                        s_b[:, par * 1024:(par + 1) * 1024],
                        g_b[:, par * 1024:(par + 1) * 1024]), sm["s_ev"])
                    W("vector", sm["s_ev"], v2)
                    vev[i] = E("vector", lambda e, par=par: e.tensor_add(
                        o_b[:, par * 1024:(par + 1) * 1024],
                        s_b[:, par * 1024:(par + 1) * 1024],
                        x_b[:, par * 1024:(par + 1) * 1024]), sm["s_ev"])
                    W("gpsimd", sm["s_ev"], vev[i])
                    vout[i] = E("gpsimd", lambda e, i=i, par=par: e.dma_start(
                        out=y_out[i * 128:(i + 1) * 128, :],
                        in_=o_b[:, par * 1024:(par + 1) * 1024]), sou, 16)

                vhoa_slot_final = [0] * NSLOT

                def maybe_epi(u):
                    h, i = u // NSLOT, u % NSLOT
                    if h == H - 1:
                        vhoa_slot_final[i] = vhoa[u]
                        plan_epi(i)

                plan_ard(0)
                plan_tr(0)
                plan_ard(1)
                plan_tr(1)
                for u in range(UTOT):
                    if u % NSLOT == 0 and u > 0:
                        plan_wv(u // NSLOT)
                    if u + 2 < UTOT:
                        plan_ard(u + 2)
                        plan_tr(u + 2)
                    plan_atc(u)
                    plan_mm(u)
                    plan_add(u)
                    maybe_epi(u)
                W("gpsimd", sm["s_out0"], sm["s_out0"].v)
                W("gpsimd", sm["s_out1"], sm["s_out1"].v)
                flush("p3")
        S_wvx.close()


    return nc


_NC_CACHE = None


def _get_nc():
    global _NC_CACHE
    if _NC_CACHE is None:
        _NC_CACHE = build_nc()
    return _NC_CACHE


def kernel(L0_attn_x, L0_attn_Wq, L0_attn_Wk, L0_attn_Wv,
           L0_ffn_w_gate, L0_ffn_w_up, L0_ffn_w_down):
    x = np.asarray(L0_attn_x, dtype=np.float32)
    Wq = np.asarray(L0_attn_Wq, dtype=np.float32)
    Wk = np.asarray(L0_attn_Wk, dtype=np.float32)
    Wv = np.asarray(L0_attn_Wv, dtype=np.float32)
    wg = np.asarray(L0_ffn_w_gate, dtype=np.int32)
    wu = np.asarray(L0_ffn_w_up, dtype=np.int32)
    wd = np.asarray(L0_ffn_w_down, dtype=np.int32)

    wqT = np.ascontiguousarray(np.transpose(Wq, (2, 0, 1)).reshape(D, H * DK))
    wkT = np.ascontiguousarray(np.transpose(Wk, (2, 0, 1)).reshape(D, H * DK))
    wvT = np.ascontiguousarray(np.transpose(Wv, (0, 2, 1)))
    wgT = np.ascontiguousarray(wg.T)
    wuT = np.ascontiguousarray(wu.T)
    wdT = np.ascontiguousarray(wd.T)
    ident = np.eye(128, dtype=ml_dtypes.bfloat16)

    in_maps = []
    metas = []
    for c in range(8):
        b, cc = c // 4, c % 4
        loc2glob = np.array([4 * (lt // 4) + ((lt % 4) + cc + 1) % 4 for lt in range(T)])
        x_loc = np.ascontiguousarray(
            x[b].reshape(T, 128, D)[loc2glob].reshape(N, D))
        kglob = (loc2glob[None, :] * 128 + np.arange(128)[:, None]).astype(np.float32)
        qglob = np.zeros((NPAIR, 128, 256), dtype=np.float32)
        tg = [60 - 4 * i + cc for i in range(NSLOT)]
        for p in range(NPAIR):
            qglob[p, :, 0:128] = tg[2 * p] * 128 + np.arange(128)[None, :]
            qglob[p, :, 128:256] = tg[2 * p + 1] * 128 + np.arange(128)[None, :]
        in_maps.append({
            "x": x_loc, "wqT": wqT, "wkT": wkT, "wvT": wvT,
            "wgT": wgT, "wuT": wuT, "wdT": wdT,
            "ident": ident, "kglob": kglob, "qglob": qglob,
        })
        metas.append((b, tg))

    nc = _get_nc()
    res = run_bass_kernel_spmd(nc, in_maps, core_ids=list(range(8)))

    out = np.zeros((B, N, D), dtype=np.float32)
    for c in range(8):
        b, tg = metas[c]
        yc = np.asarray(res.results[c]["y"], dtype=np.float32)
        for i in range(NSLOT):
            out[b, tg[i] * 128:(tg[i] + 1) * 128, :] = yc[i * 128:(i + 1) * 128, :]
    return out


# version 19
# speedup vs baseline: 1.0277x; 1.0118x over previous
import numpy as np
import ml_dtypes
from contextlib import ExitStack

import concourse.bass as bass
import concourse.mybir as mybir
from concourse.bass_utils import run_bass_kernel_spmd

F32 = mybir.dt.float32
BF16 = mybir.dt.bfloat16
I32 = mybir.dt.int32
AF = mybir.ActivationFunctionType
ALU = mybir.AluOpType
AX = mybir.AxisListType.X

B, N, D, H, DK, HID = 2, 8192, 1024, 8, 128, 3072
T = 64
NSLOT = 16
NPAIR = 8
STRIPES = [(0, 24), (24, 44), (44, 64)]
INV_N = 1.0 / float(N)
INV127 = 1.0 / 127.0
EPS = 1e-5

ENGS = ("sync", "tensor", "scalar", "vector", "gpsimd")


class Sem:
    def __init__(self, ctx, nc, name):
        self.h = ctx.enter_context(nc.semaphore(name))
        self.v = 0


def build_nc():
    nc = bass.Bass("TRN2", target_bir_lowering=False)

    x_in = nc.dram_tensor("x", [N, D], F32, kind="ExternalInput").ap()
    wqT_in = nc.dram_tensor("wqT", [D, H * DK], F32, kind="ExternalInput").ap()
    wkT_in = nc.dram_tensor("wkT", [D, H * DK], F32, kind="ExternalInput").ap()
    wvT_in = nc.dram_tensor("wvT", [H, D, D], F32, kind="ExternalInput").ap()
    wgT_in = nc.dram_tensor("wgT", [D, HID], I32, kind="ExternalInput").ap()
    wuT_in = nc.dram_tensor("wuT", [D, HID], I32, kind="ExternalInput").ap()
    wdT_in = nc.dram_tensor("wdT", [HID, D], I32, kind="ExternalInput").ap()
    ident_in = nc.dram_tensor("ident", [128, 128], BF16, kind="ExternalInput").ap()
    kglob_in = nc.dram_tensor("kglob", [128, T], F32, kind="ExternalInput").ap()
    qglob_in = nc.dram_tensor("qglob", [NPAIR, 128, 256], F32, kind="ExternalInput").ap()
    y_out = nc.dram_tensor("y", [NSLOT * 128, D], F32, kind="ExternalOutput").ap()

    xnkm_d = nc.dram_tensor("xnkm_d", [T, 128, D], BF16, kind="Internal").ap()
    xnfm_d = nc.dram_tensor("xnfm_d", [16, 128, 4096], BF16, kind="Internal").ap()
    xnf32_d = nc.dram_tensor("xnf32_d", [NSLOT, 128, D], F32, kind="Internal").ap()
    apart_d = nc.dram_tensor("apart_d", [NPAIR, H, 128, 2048], F32, kind="Internal").ap()
    a_d = nc.dram_tensor("a_d", [NPAIR, H, 128, 2048], BF16, kind="Internal").ap()
    ht_d = nc.dram_tensor("ht_d", [4, 128, 24 * 512], BF16, kind="Internal").ap()
    g_d = nc.dram_tensor("g_d", [NSLOT, 128, D], F32, kind="Internal").ap()

    x3 = x_in.rearrange("(t p) d -> t p d", p=128)

    cur = {k: [] for k in ENGS}

    def W(eng, sem, val):
        if val > 0:
            cur[eng].append(lambda e, s=sem.h, v=val: e.wait_ge(s, v))

    def E(eng, fn, sem=None, n=1):
        if sem is None:
            cur[eng].append(fn)
            return None
        sem.v += n
        cur[eng].append(lambda e, f=fn, s=sem.h, m=n: f(e).then_inc(s, m))
        return sem.v

    def flush(name):
        nonlocal cur
        bodies = cur
        cur = {k: [] for k in ENGS}
        with nc.Block(name) as block:
            for eng in ENGS:
                ops = bodies[eng]
                if ops:
                    getattr(block, eng)(lambda engine, ops=ops: [f(engine) for f in ops])

    with ExitStack() as S0:
        cbank = S0.enter_context(nc.sbuf_tensor("cbank", [128, 4], F32))
        kglob_sb = S0.enter_context(nc.sbuf_tensor("kglob_sb", [128, T], F32))
        nc.const_aps.register(F32, EPS, cbank[:, 0:1])
        nc.const_aps.register(F32, INV_N, cbank[:, 1:2])
        nc.const_aps.register(F32, INV127, cbank[:, 2:3])
        nc.const_aps.register(F32, -0.5, cbank[:, 3:4])

        sem_names = [
            "s_x", "s_x0", "s_x1", "s_x2", "s_n", "s_ng", "s_red", "s_rcp", "s_sq", "s_cols", "s_tr1", "s_fme",
            "s_w10", "s_w11", "s_w12",
            "s_qpld0", "s_qpld1", "s_qmm", "s_qe", "s_wc",
            "s_kmld", "s_fmld0", "s_fmld1", "s_kmm", "s_ke", "s_sc", "s_sim", "s_vm",
            "s_amm", "s_ae", "s_aes", "s_adma0", "s_adma1", "s_admb0", "s_admb1", "s_api0", "s_api1", "s_qgl0", "s_qgl1",
            "s_fw", "s_fwc0", "s_fwc1", "s_fwc2", "s_fwc3", "s_fwc4", "s_fwc5",
            "s_fwc6", "s_fwc7",
            "s_fmtb0", "s_fmtb1", "s_gu", "s_silu", "s_hv", "s_htw",
            "s_htr0", "s_htr1", "s_dmm", "s_gs", "s_gw0", "s_gw1",
            "s_wv0", "s_wv1", "s_wvc", "s_ard0", "s_ard1", "s_ard2", "s_tr3", "s_atc", "s_atcv", "s_hom", "s_hoa",
            "s_er0", "s_er1", "s_ev", "s_out0", "s_out1",
        ]
        sm = {n: Sem(S0, nc, n) for n in sem_names}

        with ExitStack() as S_qt:
            qT_sb = S_qt.enter_context(nc.sbuf_tensor("qT_sb", [128, H * NSLOT * 128], BF16))
            with ExitStack() as S_wk:
                wkbf = S_wk.enter_context(nc.sbuf_tensor("wkbf", [128, 8 * H * DK], BF16))
                with ExitStack() as S_wq:
                    wqbf = S_wq.enter_context(nc.sbuf_tensor("wqbf", [128, 8 * H * DK], BF16))
                    ident1 = S_wq.enter_context(nc.sbuf_tensor("ident1", [128, 128], BF16))

                    with ExitStack() as S_i:
                        stage32 = S_i.enter_context(nc.sbuf_tensor("stage32", [128, 8192], F32))

                        E("gpsimd", lambda e: e.memset(cbank[:, 0:1], EPS))
                        E("gpsimd", lambda e: e.memset(cbank[:, 1:2], INV_N))
                        E("gpsimd", lambda e: e.memset(cbank[:, 2:3], INV127))
                        E("gpsimd", lambda e: e.memset(cbank[:, 3:4], -0.5))
                        v_id = E("sync", lambda e: e.dma_start(out=ident1[:, :], in_=ident_in[:, :]), sm["s_x"], 16)
                        E("sync", lambda e: e.dma_start(out=kglob_sb[:, :], in_=kglob_in[:, :]), sm["s_x"], 16)
                        wq3 = wqT_in.rearrange("(c p) e -> p c e", p=128)
                        v_wq = E("sync", lambda e, wq3=wq3: e.dma_start(
                            out=stage32.rearrange("p (c e) -> p c e", c=8), in_=wq3), sm["s_x"], 16)
                        W("scalar", sm["s_x"], v_wq)
                        v_cq = E("scalar", lambda e: e.activation(out=wqbf[:, :], in_=stage32[:, :], func=AF.Copy),
                                 sm["s_wc"])
                        W("sync", sm["s_wc"], v_cq)
                        wk3 = wkT_in.rearrange("(c p) e -> p c e", p=128)
                        v_wk = E("sync", lambda e, wk3=wk3: e.dma_start(
                            out=stage32.rearrange("p (c e) -> p c e", c=8), in_=wk3), sm["s_x"], 16)
                        W("scalar", sm["s_x"], v_wk)
                        E("scalar", lambda e: e.activation(out=wkbf[:, :], in_=stage32[:, :], func=AF.Copy),
                          sm["s_wc"])
                        flush("init")

                    with ExitStack() as S_p1:
                        xstage = S_p1.enter_context(nc.sbuf_tensor("xstage", [128, 3 * 4096], F32))
                        xnkm_b = S_p1.enter_context(nc.sbuf_tensor("xnkm_b", [128, 3 * 4096], BF16))
                        xnfm_b = S_p1.enter_context(nc.sbuf_tensor("xnfm_b", [128, 3 * 4096], BF16))
                        xf32_b = S_p1.enter_context(nc.sbuf_tensor("xf32_b", [128, 3 * 1024], F32))
                        cols = S_p1.enter_context(nc.sbuf_tensor("cols", [128, 3 * 24], F32))
                        dump = S_p1.enter_context(nc.sbuf_tensor("dump", [128, 4 * 1024], F32))
                        qp_buf = S_p1.enter_context(nc.sbuf_tensor("qp_buf", [128, 2 * 4096], BF16))
                        ps_t = S_p1.enter_context(nc.psum_tensor("ps_t", [128, 4096], BF16))
                        ps_q = S_p1.enter_context(nc.psum_tensor("ps_q", [128, 1024], F32))

                        vn = [0] * 16
                        vng = [0] * 16
                        vred = [0] * 16
                        vsq = [0] * 16
                        vtr_last = [0] * 16
                        vfme = [0] * 16
                        vw_blk = [0] * 16
                        vfme_slot = {}
                        vqmm_g = [0] * 4
                        vqe_seq = []

                        def emit_qgroup(g):
                            par = g % 2
                            qpb = qp_buf[:, par * 4096:(par + 1) * 4096]
                            qpb3 = qp_buf.rearrange("p (b c q) -> p b c q", b=2, c=8)
                            if g >= 2:
                                W("sync", sm["s_qmm"], vqmm_g[g - 2])
                            sqp = sm["s_qpld0" if par == 0 else "s_qpld1"]
                            for pp in (0, 1, 2):
                                blks = [15 - (4 * g + s) for s in range(4) if (15 - (4 * g + s)) % 3 == pp]
                                if blks:
                                    W("sync", sm[f"s_w1{pp}"], max(vw_blk[b] for b in blks))
                            vld = 0
                            for s in range(4):
                                blkj = 15 - (4 * g + s)
                                srcq = xnfm_d[blkj].rearrange("p (c q) -> p c q", c=8)[:, :, 384:512]
                                vld = E("sync", lambda e, srcq=srcq, qpb3=qpb3, par=par, s=s: e.dma_start(
                                    out=qpb3[:, par, :, s * 128:(s + 1) * 128], in_=srcq), sqp, 16)
                            W("tensor", sqp, vld)
                            for h in range(H):
                                gidx = g * H + h
                                hp = gidx % 2
                                if gidx >= 2:
                                    W("tensor", sm["s_qe"], vqe_seq[gidx - 2])
                                for c in range(8):
                                    r = E("tensor", lambda e, hp=hp, c=c, h=h, qpb=qpb: e.matmul(
                                        ps_q[:, hp * 512:(hp + 1) * 512],
                                        wqbf[:, c * 1024 + h * 128: c * 1024 + (h + 1) * 128],
                                        qpb[:, c * 512:(c + 1) * 512],
                                        start=(c == 0), stop=(c == 7)),
                                        sm["s_qmm"] if c == 7 else None)
                                    if r:
                                        vqmm_g[g] = r
                                W("scalar", sm["s_qmm"], vqmm_g[g])
                                vqe_seq.append(E("scalar", lambda e, hp=hp, h=h, g=g: e.copy(
                                    out=qT_sb[:, h * 2048 + g * 512: h * 2048 + (g + 1) * 512],
                                    in_=ps_q[:, hp * 512:(hp + 1) * 512]), sm["s_qe"]))

                        def emit_A(j):
                            idx = 15 - j
                            par = j % 3
                            xs = xstage[:, par * 4096:(par + 1) * 4096]
                            co = cols[:, par * 24:(par + 1) * 24]
                            if idx >= 3:
                                W("sync", sm["s_n"], vn[j + 3])
                                W("sync", sm["s_ng"], vng[j + 3])
                                W("sync", sm["s_sq"], vsq[j + 3])
                            sx = sm[f"s_x{par}"]
                            vx = E("sync", lambda e, xs=xs, j=j: e.dma_start(
                                out=xs.rearrange("p (t d) -> p t d", t=4),
                                in_=x3[4 * j:4 * j + 4, :, :].transpose([1, 0, 2])), sx, 16)

                            W("vector", sx, vx)
                            for t in range(4):
                                E("vector", lambda e, xs=xs, co=co, t=t: e.reduce_sum(
                                    co[:, t:t + 1], xs[:, t * 1024:(t + 1) * 1024], axis=AX),
                                    sm["s_red"] if t == 3 else None)
                            vred[j] = sm["s_red"].v

                            W("scalar", sx, vx)
                            if idx >= 1:
                                W("scalar", sm["s_sq"], vsq[j + 1])
                            for t in range(4):
                                r = E("scalar", lambda e, xs=xs, co=co, t=t: e.activation(
                                    out=dump[:, t * 1024:(t + 1) * 1024], in_=xs[:, t * 1024:(t + 1) * 1024],
                                    func=AF.Square, accum_out=co[:, 4 + t:5 + t]),
                                    sm["s_sq"] if t == 3 else None)
                                if r:
                                    vsq[j] = r

                        def emit_BC(j):
                            idx = 15 - j
                            par = j % 3
                            bpar = j % 2
                            xs = xstage[:, par * 4096:(par + 1) * 4096]
                            xkb = xnkm_b[:, par * 4096:(par + 1) * 4096]
                            xfb = xnfm_b[:, par * 4096:(par + 1) * 4096]
                            co = cols[:, par * 24:(par + 1) * 24]
                            W("gpsimd", sm["s_sq"], vsq[j])
                            W("gpsimd", sm["s_red"], vred[j])
                            E("gpsimd", lambda e, co=co: e.tensor_scalar_mul(co[:, 8:12], co[:, 0:4], -1.0 / D))
                            v1 = E("gpsimd", lambda e, co=co: e.tensor_scalar_mul(co[:, 12:16], co[:, 4:8], 1.0 / D),
                                   sm["s_cols"])
                            W("gpsimd", sm["s_cols"], v1)
                            v2 = E("gpsimd", lambda e, co=co: e.tensor_mul(
                                co[:, 16:20], co[:, 8:12], co[:, 8:12]), sm["s_cols"])
                            W("gpsimd", sm["s_cols"], v2)
                            v3 = E("gpsimd", lambda e, co=co: e.tensor_sub(co[:, 16:20], co[:, 12:16], co[:, 16:20]),
                                   sm["s_cols"])
                            W("scalar", sm["s_cols"], v3)
                            v4a = E("scalar", lambda e, co=co: e.activation(
                                out=co[:, 20:24], in_=co[:, 16:20], func=AF.Ln, bias=EPS), sm["s_sq"])
                            W("scalar", sm["s_sq"], v4a)
                            v4 = E("scalar", lambda e, co=co: e.activation(
                                out=co[:, 20:24], in_=co[:, 20:24], func=AF.Exp, scale=-0.5), sm["s_sq"])
                            vsq[j] = v4
                            W("gpsimd", sm["s_sq"], v4)
                            v6 = E("gpsimd", lambda e, co=co: e.tensor_mul(
                                co[:, 16:20], co[:, 8:12], co[:, 20:24]), sm["s_cols"])
                            W("gpsimd", sm["s_cols"], v6)

                            if idx >= 3:
                                W("vector", sm[f"s_w1{par}"], vw_blk[j + 3])
                                W("vector", sm["s_tr1"], vtr_last[j + 3])
                            W("vector", sm["s_cols"], sm["s_cols"].v)
                            W("vector", sm["s_sq"], v4)
                            for t in (0, 1, 2, 3):
                                vng[j] = E("vector", lambda e, xs=xs, xkb=xkb, co=co, t=t: e.tensor_scalar(
                                    out=xkb[:, t * 1024:(t + 1) * 1024], in0=xs[:, t * 1024:(t + 1) * 1024],
                                    scalar1=co[:, 20 + t:21 + t], scalar2=co[:, 16 + t:17 + t],
                                    op0=ALU.mult, op1=ALU.add), sm["s_ng"])
                            vn[j] = E("vector", lambda e, xs=xs, co=co, par=par: e.tensor_scalar(
                                out=xf32_b[:, par * 1024:(par + 1) * 1024], in0=xs[:, 3 * 1024:4 * 1024],
                                scalar1=co[:, 23:24], scalar2=co[:, 19:20],
                                op0=ALU.mult, op1=ALU.add), sm["s_n"])

                            W("tensor", sm["s_n"], vn[j])
                            W("tensor", sm["s_ng"], vng[j])
                            if idx == 0:
                                W("tensor", sm["s_x"], 48)
                            for c in range(8):
                                slot = 2 * bpar + (c % 2)
                                if slot in vfme_slot:
                                    W("tensor", sm["s_fme"], vfme_slot[slot])
                                for t in range(4):
                                    r = E("tensor", lambda e, xkb=xkb, c=c, t=t, slot=slot: e.transpose(
                                        ps_t[:, slot * 1024 + t * 128: slot * 1024 + (t + 1) * 128],
                                        xkb[:, t * 1024 + c * 128: t * 1024 + (c + 1) * 128],
                                        ident1[:, :]), sm["s_tr1"] if t == 3 else None)
                                    if r:
                                        vtr_last[j] = r
                                W("scalar", sm["s_tr1"], vtr_last[j])
                                if c == 0 and idx >= 3:
                                    W("scalar", sm[f"s_w1{par}"], vw_blk[j + 3])
                                vfme_slot[slot] = E("scalar", lambda e, xfb=xfb, c=c, slot=slot: e.copy(
                                    out=xfb[:, c * 512:(c + 1) * 512],
                                    in_=ps_t[:, slot * 1024:slot * 1024 + 512]), sm["s_fme"])
                            vfme[j] = sm["s_fme"].v

                            W("gpsimd", sm["s_n"], vn[j])
                            W("gpsimd", sm["s_ng"], vng[j])
                            sw = sm[f"s_w1{par}"]
                            E("gpsimd", lambda e, xkb=xkb, j=j: e.dma_start(
                                out=xnkm_d[4 * j:4 * j + 4, :, :].transpose([1, 0, 2]),
                                in_=xkb.rearrange("p (t d) -> p t d", t=4)), sw, 16)
                            E("gpsimd", lambda e, par=par, so=15 - j: e.dma_start(
                                out=xnf32_d[so, :, :],
                                in_=xf32_b[:, par * 1024:(par + 1) * 1024]), sw, 16)
                            W("gpsimd", sm["s_fme"], vfme[j])
                            vw_blk[j] = E("gpsimd", lambda e, xfb=xfb, j=j: e.dma_start(
                                out=xnfm_d[j, :, :], in_=xfb[:, :]), sw, 16)

                            if j % 4 == 0:
                                emit_qgroup((12 - j) // 4)

                        for k in range(17):
                            if k < 16:
                                emit_A(15 - k)
                            if k >= 1:
                                emit_BC(15 - (k - 1))
                        W("gpsimd", sm["s_w10"], sm["s_w10"].v)
                        W("gpsimd", sm["s_w11"], sm["s_w11"].v)
                        W("gpsimd", sm["s_w12"], sm["s_w12"].v)
                        flush("p1")

                with ExitStack() as S_p2:
                    xnkm_s = S_p2.enter_context(nc.sbuf_tensor("xnkm_s", [128, 24 * 1024], BF16))
                    kT_s = S_p2.enter_context(nc.sbuf_tensor("kT_s", [128, H * 24 * 128], BF16))
                    xnfm2 = S_p2.enter_context(nc.sbuf_tensor("xnfm2", [128, 2 * 4096], BF16))
                    sim_sb = S_p2.enter_context(nc.sbuf_tensor("sim_sb", [128, 4 * 512], BF16))
                    mask_sb = S_p2.enter_context(nc.sbuf_tensor("mask_sb", [128, 8 * 256], BF16))
                    qgl_sb = S_p2.enter_context(nc.sbuf_tensor("qgl_sb", [128, 2 * 256], F32))
                    abf_sb = S_p2.enter_context(nc.sbuf_tensor("abf_sb", [128, 2 * 1024], BF16))
                    apo_sb = S_p2.enter_context(nc.sbuf_tensor("apo_sb", [128, 2 * 1024], F32))
                    api_sb = S_p2.enter_context(nc.sbuf_tensor("api_sb", [128, 4 * 1024], F32))
                    stg_sb = S_p2.enter_context(nc.sbuf_tensor("stg_sb", [128, 2 * 1024], F32))
                    ps_a = S_p2.enter_context(nc.psum_tensor("ps_a", [128, 2048], F32))
                    ps_s = S_p2.enter_context(nc.psum_tensor("ps_s", [128, 2048], F32))
                    ps_k = ps_s[:, 1024:2048]

                    def pairs_in(si):
                        return [p for p in range(NPAIR) if 64 - 8 * p > STRIPES[si][0]]

                    last_stripe_of = {}
                    for p in range(NPAIR):
                        for si in range(3):
                            if p in pairs_in(si):
                                last_stripe_of[p] = si

                    vA_free = 0
                    vA_free_s = 0
                    vstg_free = 0
                    vbank_free = [None, None, None, None]
                    vabf_free = [0, 0]
                    vapo_free = [0, 0]
                    vapi_used = [0, 0, 0, 0]
                    ci_uid = 0
                    vqgl_used = [0, 0]
                    prev_vsim = [None, None, None, None]
                    vfm_pre = {}

                    for si in range(3):
                        v0, v1 = STRIPES[si]
                        nblk = (v1 - v0) // 4
                        prs = pairs_in(si)

                        kmm_base = sm["s_kmm"].v
                        vkm_last = 0
                        vkm_blk = [0] * nblk
                        vfm_blk = [0] * nblk
                        for bb in range(nblk):
                            j = v0 // 4 + bb
                            vkm_last = E("sync", lambda e, bb=bb, j=j: e.dma_start(
                                out=xnkm_s.rearrange("p (t d) -> p t d", t=24)[:, 4 * bb:4 * bb + 4, :],
                                in_=xnkm_d[4 * j:4 * j + 4, :, :].transpose([1, 0, 2])), sm["s_kmld"], 16)
                            vkm_blk[bb] = vkm_last
                        qgl_pre = {}
                        order_pairs = sorted(prs, key=lambda q: min(64 - 8 * q, v1))
                        seen_par = set()
                        for p in order_pairs:
                            NKAp = 64 - 8 * p
                            u1p = min(NKAp, v1)
                            tl = [v for v in range(v0, u1p) if v >= NKAp - 8]
                            if tl and (p % 2) not in seen_par:
                                seen_par.add(p % 2)
                                sqg = sm["s_qgl0" if p % 2 == 0 else "s_qgl1"]
                                W("sync", sm["s_vm"], vqgl_used[p % 2])
                                qgl_pre[p] = E("sync", lambda e, p=p: e.dma_start(
                                    out=qgl_sb[:, (p % 2) * 256:(p % 2 + 1) * 256],
                                    in_=qglob_in[p, :, :]), sqg, 16)
                        for bb in range(min(nblk, 2)):
                            j = v0 // 4 + bb
                            if (si, bb) in vfm_pre:
                                vfm_blk[bb] = vfm_pre[(si, bb)]
                                continue
                            vfm_blk[bb] = E("sync", lambda e, bb=bb, j=j: e.dma_start(
                                out=xnfm2[:, (bb % 2) * 4096:(bb % 2 + 1) * 4096],
                                in_=xnfm_d[j, :, :]), sm["s_fmld0" if bb % 2 == 0 else "s_fmld1"], 16)

                        vke_seq = []

                        def emit_kblock(bb):
                            W("tensor", sm["s_fmld0" if bb % 2 == 0 else "s_fmld1"], vfm_blk[bb])
                            for pv in (2, 3):
                                if prev_vsim[pv] is not None:
                                    sh, sv = prev_vsim[pv]
                                    W("tensor", sh, sv)
                            for h in range(H):
                                idx = bb * H + h
                                bank = idx % 2
                                if idx >= 2:
                                    W("tensor", sm["s_ke"], vke_seq[idx - 2])
                                vkmm = 0
                                for c in range(8):
                                    r = E("tensor", lambda e, bank=bank, c=c, h=h, bb=bb: e.matmul(
                                        ps_k[:, bank * 512:(bank + 1) * 512],
                                        wkbf[:, c * 1024 + h * 128: c * 1024 + (h + 1) * 128],
                                        xnfm2[:, (bb % 2) * 4096 + c * 512:(bb % 2) * 4096 + (c + 1) * 512],
                                        start=(c == 0), stop=(c == 7)),
                                        sm["s_kmm"] if c == 7 else None)
                                    if r:
                                        vkmm = r
                                W("scalar", sm["s_kmm"], vkmm)
                                vke_seq.append(E("scalar", lambda e, bank=bank, h=h, bb=bb: e.copy(
                                    out=kT_s[:, h * 3072 + bb * 512: h * 3072 + (bb + 1) * 512],
                                    in_=ps_k[:, bank * 512:(bank + 1) * 512]), sm["s_ke"]))
                            if bb + 2 < nblk:
                                j2 = v0 // 4 + bb + 2
                                W("sync", sm["s_kmm"], sm["s_kmm"].v)
                                vfm_blk[bb + 2] = E("sync", lambda e, bb=bb, j2=j2: e.dma_start(
                                    out=xnfm2[:, (bb % 2) * 4096:(bb % 2 + 1) * 4096],
                                    in_=xnfm_d[j2, :, :]), sm["s_fmld0" if bb % 2 == 0 else "s_fmld1"], 16)

                        bb_done = 0
                        for p in sorted(prs, key=lambda q: min(64 - 8 * q, v1)):
                            NKA = 64 - 8 * p
                            NKB = NKA - 4
                            u0, u1 = v0, min(NKA, v1)
                            is_last = (si == last_stripe_of[p])
                            has_ci = (si > 0)

                            need_blocks = (u1 - v0 + 3) // 4
                            while bb_done < need_blocks:
                                emit_kblock(bb_done)
                                bb_done += 1
                            vke_pair = vke_seq[need_blocks * H - 1]
                            vkm_pair = vkm_last
                            pair_first = True

                            tail_vis = [v for v in range(u0, u1) if v >= NKB - 4]
                            vmask_p = 0
                            if tail_vis:
                                sqg = sm["s_qgl0" if p % 2 == 0 else "s_qgl1"]
                                if p in qgl_pre:
                                    vqgl_p = qgl_pre[p]
                                else:
                                    W("sync", sm["s_vm"], vqgl_used[p % 2])
                                    vqgl_p = E("sync", lambda e, p=p: e.dma_start(
                                        out=qgl_sb[:, (p % 2) * 256:(p % 2 + 1) * 256],
                                        in_=qglob_in[p, :, :]), sqg, 16)
                                W("vector", sqg, vqgl_p)
                                for idx, v in enumerate(tail_vis):
                                    wid = 256 if v < NKB else 128
                                    vmask_p = E("vector", lambda e, idx=idx, v=v, p=p, wid=wid: e.tensor_scalar(
                                        out=mask_sb[:, idx * 256: idx * 256 + wid],
                                        in0=qgl_sb[:, (p % 2) * 256:(p % 2) * 256 + wid],
                                        scalar1=kglob_sb[:, v:v + 1], scalar2=INV_N,
                                        op0=ALU.is_ge, op1=ALU.mult), sm["s_vm"])
                                vqgl_used[p % 2] = vmask_p

                            head_cache = None
                            for h in range(H):
                                nv = u1 - u0
                                groups = [(u0 + 2 * k, min(u0 + 2 * k + 2, u1))
                                          for k in range((nv + 1) // 2)]
                                ng = len(groups)
                                vsc_g = [0] * ng
                                vsim_g = [None] * ng

                                if pair_first:
                                    W("tensor", sm["s_ke"], vke_pair)
                                    W("tensor", sm["s_kmld"], vkm_pair)
                                    pair_first = False
                                vA_free_u = vA_free
                                vA_free_su = vA_free_s
                                vbank_u = list(vbank_free)
                                vke_floor = vke_seq[-1] if vke_seq else 0

                                def emit_scores(gi, hh, vsc_a, vsim_a, p=p, NKB=NKB,
                                                groups=groups, v0=v0):
                                    glo, ghi = groups[gi]
                                    gpar = gi % 4
                                    if gi >= 4:
                                        sh, sv = vsim_a[gi - 4]
                                        W("tensor", sh, sv)
                                    else:
                                        if prev_vsim[gpar] is not None:
                                            sh, sv = prev_vsim[gpar]
                                            W("tensor", sh, sv)
                                        if gpar >= 2:
                                            W("tensor", sm["s_ke"], vke_floor)
                                    for vv in range(glo, ghi):
                                        wid = 256 if vv < NKB else 128
                                        lt = vv - v0
                                        half = vv - glo
                                        r = E("tensor", lambda e, gpar=gpar, half=half, wid=wid, lt=lt, hh=hh: e.matmul(
                                            ps_s[:, gpar * 512 + half * 256: gpar * 512 + half * 256 + wid],
                                            kT_s[:, hh * 3072 + lt * 128: hh * 3072 + (lt + 1) * 128],
                                            qT_sb[:, hh * 2048 + p * 256: hh * 2048 + p * 256 + wid],
                                            start=True, stop=True),
                                            sm["s_sc"] if vv == ghi - 1 else None)
                                        if r:
                                            vsc_a[gi] = r

                                def emit_relu(gi, vsc_a, vsim_a, p=p, NKB=NKB, groups=groups,
                                              tail_vis=tail_vis, vmask_p=vmask_p):
                                    glo, ghi = groups[gi]
                                    gpar = gi % 4
                                    wid = (ghi - glo - 1) * 256 + (256 if ghi - 1 < NKB else 128)
                                    masked = (glo >= NKB - 4)
                                    W("scalar", sm["s_sc"], vsc_a[gi])
                                    if not masked:
                                        v = E("scalar", lambda e, gpar=gpar, wid=wid: e.activation(
                                            out=sim_sb[:, gpar * 512: gpar * 512 + wid],
                                            in_=ps_s[:, gpar * 512: gpar * 512 + wid],
                                            func=AF.Relu, scale=INV_N), sm["s_sim"])
                                        vsim_a[gi] = (sm["s_sim"], v)
                                    else:
                                        vr = E("scalar", lambda e, gpar=gpar, wid=wid: e.activation(
                                            out=sim_sb[:, gpar * 512: gpar * 512 + wid],
                                            in_=ps_s[:, gpar * 512: gpar * 512 + wid],
                                            func=AF.Relu, scale=1.0), sm["s_sim"])
                                        W("vector", sm["s_sim"], vr)
                                        W("vector", sm["s_vm"], vmask_p)
                                        vlast = 0
                                        for vv in range(glo, ghi):
                                            midx = tail_vis.index(vv)
                                            wid_v = 256 if vv < NKB else 128
                                            half = vv - glo
                                            vlast = E("vector", lambda e, gpar=gpar, half=half, midx=midx, wid_v=wid_v: e.tensor_mul(
                                                sim_sb[:, gpar * 512 + half * 256: gpar * 512 + half * 256 + wid_v],
                                                sim_sb[:, gpar * 512 + half * 256: gpar * 512 + half * 256 + wid_v],
                                                mask_sb[:, midx * 256: midx * 256 + wid_v]), sm["s_vm"])
                                        vsim_a[gi] = (sm["s_vm"], vlast)

                                last_vis = {0: u1 - 1, 1: min(u1, NKB) - 1}

                                def emit_amm(gi, NKB=NKB, groups=groups, ng=ng,
                                             vsim_g=vsim_g, u0=u0, v0=v0, last_vis=last_vis):
                                    glo, ghi = groups[gi]
                                    gpar = gi % 4
                                    sh, sv = vsim_g[gi]
                                    W("tensor", sh, sv)
                                    for vv in range(glo, ghi):
                                        halfcol = (vv - glo) * 256
                                        lt = vv - v0
                                        tiles = (0, 1) if vv < NKB else (0,)
                                        for tt in tiles:
                                            for dh in range(2):
                                                first = (vv == u0)
                                                if first and vbank_u[2 * tt + dh] is not None:
                                                    bs, bv = vbank_u[2 * tt + dh]
                                                    W("tensor", bs, bv)
                                                stop = (vv == last_vis[tt])
                                                is_final = (gi == ng - 1 and vv == ghi - 1
                                                            and tt == tiles[-1] and dh == 1)
                                                E("tensor", lambda e, gpar=gpar, halfcol=halfcol, lt=lt, tt=tt, dh=dh, first=first, stop=stop: e.matmul(
                                                    ps_a[:, tt * 1024 + dh * 512: tt * 1024 + (dh + 1) * 512],
                                                    sim_sb[:, gpar * 512 + halfcol + tt * 128: gpar * 512 + halfcol + (tt + 1) * 128],
                                                    xnkm_s[:, lt * 1024 + dh * 512: lt * 1024 + (dh + 1) * 512],
                                                    start=first, stop=stop),
                                                    sm["s_amm"] if is_final else None)

                                vapi_ready = 0
                                api_base = 0
                                sapi = None
                                if has_ci:
                                    api_base = (ci_uid % 2) * 2
                                    sapi = sm["s_api0" if ci_uid % 2 == 0 else "s_api1"]
                                    ci_uid += 1
                                    for tt in range(2):
                                        W("sync", sm["s_ae"], vapi_used[api_base + tt])
                                        vapi_ready = E("sync", lambda e, p=p, h=h, tt=tt, api_base=api_base: e.dma_start(
                                            out=api_sb[:, (api_base + tt) * 1024:(api_base + tt + 1) * 1024],
                                            in_=apart_d[p, h, :, tt * 1024:(tt + 1) * 1024]), sapi, 16)

                                if head_cache is not None:
                                    vsc_g[0], vsim_g[0] = head_cache[0]
                                    if ng > 1:
                                        vsc_g[1], vsim_g[1] = head_cache[1]
                                    head_cache = None
                                else:
                                    emit_scores(0, h, vsc_g, vsim_g)
                                    emit_relu(0, vsc_g, vsim_g)
                                    if ng > 1:
                                        emit_scores(1, h, vsc_g, vsim_g)
                                        emit_relu(1, vsc_g, vsim_g)

                                last_b01 = max(k for k in range(ng) if k % 4 in (0, 1))
                                for gi in range(ng):
                                    if gi + 2 < ng:
                                        emit_scores(gi + 2, h, vsc_g, vsim_g)
                                        emit_relu(gi + 2, vsc_g, vsim_g)
                                    emit_amm(gi)
                                    if gi == last_b01 and h + 1 < H:
                                        for g4 in range(4):
                                            cand = [vsim_g[k] for k in range(ng) if k % 4 == g4]
                                            if cand:
                                                prev_vsim[g4] = cand[-1]
                                        nvsc = [0] * ng
                                        nvsim = [None] * ng
                                        emit_scores(0, h + 1, nvsc, nvsim)
                                        emit_relu(0, nvsc, nvsim)
                                        hc = [(nvsc[0], nvsim[0])]
                                        if ng > 1:
                                            emit_scores(1, h + 1, nvsc, nvsim)
                                            emit_relu(1, nvsc, nvsim)
                                            hc.append((nvsc[1], nvsim[1]))
                                        head_cache = hc
                                vamm = sm["s_amm"].v
                                if h + 1 >= H:
                                    for gpar2 in range(4):
                                        cand = [vsim_g[k] for k in range(ng) if k % 4 == gpar2]
                                        if cand:
                                            prev_vsim[gpar2] = cand[-1]

                                tileB_has = (u0 < min(NKB, u1))

                                if is_last:
                                    sad0, sad1 = sm["s_adma0"], sm["s_adma1"]
                                    dst, freev = abf_sb, vabf_free
                                else:
                                    sad0, sad1 = sm["s_admb0"], sm["s_admb1"]
                                    dst, freev = apo_sb, vapo_free
                                W("scalar", sm["s_amm"], vamm)
                                W("vector", sm["s_amm"], vamm)
                                if not has_ci:
                                    W("scalar", sad0, freev[0])
                                    vb0 = E("scalar", lambda e, dst=dst: e.copy(
                                        out=dst[:, 0:512], in_=ps_a[:, 0:512]), sm["s_aes"])
                                    W("vector", sad0, freev[0])
                                    vb1 = E("vector", lambda e, dst=dst: e.tensor_copy(
                                        dst[:, 512:1024], ps_a[:, 512:1024]), sm["s_ae"])
                                    W("scalar", sad1, freev[1])
                                    v_e0 = vb2 = E("scalar", lambda e, dst=dst: e.copy(
                                        out=dst[:, 1024:1536], in_=ps_a[:, 1024:1536]), sm["s_aes"])
                                    W("vector", sad1, freev[1])
                                    v_e1 = vb3 = E("vector", lambda e, dst=dst: e.tensor_copy(
                                        dst[:, 1536:2048], ps_a[:, 1536:2048]), sm["s_ae"])
                                    vbank_free = [(sm["s_aes"], vb0), (sm["s_ae"], vb1),
                                                  (sm["s_aes"], vb2), (sm["s_ae"], vb3)]
                                    W("gpsimd", sm["s_aes"], v_e0)
                                    W("gpsimd", sm["s_ae"], v_e1)
                                    tgt = a_d if is_last else apart_d
                                    freev[0] = E("gpsimd", lambda e, tgt=tgt, dst=dst, p=p, h=h: e.dma_start(
                                        out=tgt[p, h, :, 0:1024], in_=dst[:, 0:1024]), sad0, 16)
                                    freev[1] = E("gpsimd", lambda e, tgt=tgt, dst=dst, p=p, h=h: e.dma_start(
                                        out=tgt[p, h, :, 1024:2048], in_=dst[:, 1024:2048]), sad1, 16)
                                else:
                                    W("scalar", sm["s_ae"], vstg_free)
                                    W("vector", sm["s_ae"], vstg_free)
                                    vb0 = E("scalar", lambda e: e.copy(
                                        out=stg_sb[:, 0:512], in_=ps_a[:, 0:512]), sm["s_aes"])
                                    vb1 = E("vector", lambda e: e.tensor_copy(
                                        stg_sb[:, 512:1024], ps_a[:, 512:1024]), sm["s_ae"])
                                    if tileB_has:
                                        v_e0 = vb2 = E("scalar", lambda e: e.copy(
                                            out=stg_sb[:, 1024:1536], in_=ps_a[:, 1024:1536]), sm["s_aes"])
                                        v_e1 = vb3 = E("vector", lambda e: e.tensor_copy(
                                            stg_sb[:, 1536:2048], ps_a[:, 1536:2048]), sm["s_ae"])
                                    else:
                                        v_e0 = vb2 = sm["s_aes"].v
                                        v_e1 = vb3 = sm["s_ae"].v
                                    vbank_free = [(sm["s_aes"], vb0), (sm["s_ae"], vb1),
                                                  (sm["s_aes"], vb2), (sm["s_ae"], vb3)]
                                    W("vector", sapi, vapi_ready)
                                    W("vector", sm["s_aes"], v_e0)
                                    W("vector", sm["s_ae"], v_e1)
                                    W("vector", sad0, freev[0])
                                    va0 = E("vector", lambda e, dst=dst, api_base=api_base: e.tensor_add(
                                        dst[:, 0:1024], stg_sb[:, 0:1024],
                                        api_sb[:, api_base * 1024:(api_base + 1) * 1024]), sm["s_ae"])
                                    vapi_used[api_base] = va0
                                    W("vector", sad1, freev[1])
                                    if tileB_has:
                                        va1 = E("vector", lambda e, dst=dst, api_base=api_base: e.tensor_add(
                                            dst[:, 1024:2048], stg_sb[:, 1024:2048],
                                            api_sb[:, (api_base + 1) * 1024:(api_base + 2) * 1024]), sm["s_ae"])
                                    else:
                                        va1 = E("vector", lambda e, dst=dst, api_base=api_base: e.tensor_copy(
                                            dst[:, 1024:2048],
                                            api_sb[:, (api_base + 1) * 1024:(api_base + 2) * 1024]), sm["s_ae"])
                                    vapi_used[api_base + 1] = va1
                                    vstg_free = va1
                                    W("gpsimd", sm["s_ae"], va0)
                                    tgt = a_d if is_last else apart_d
                                    freev[0] = E("gpsimd", lambda e, tgt=tgt, dst=dst, p=p, h=h: e.dma_start(
                                        out=tgt[p, h, :, 0:1024], in_=dst[:, 0:1024]), sad0, 16)
                                    W("gpsimd", sm["s_ae"], va1)
                                    freev[1] = E("gpsimd", lambda e, tgt=tgt, dst=dst, p=p, h=h: e.dma_start(
                                        out=tgt[p, h, :, 1024:2048], in_=dst[:, 1024:2048]), sad1, 16)
                                vA_free_s = sm["s_aes"].v
                                vA_free = sm["s_ae"].v if has_ci else v_e1
                                if has_ci:
                                    vA_free = v_e1
                        if si < 2:
                            kmm_end = sm["s_kmm"].v
                            nv0 = STRIPES[si + 1][0]
                            for bb in range(2):
                                j = nv0 // 4 + bb
                                W("sync", sm["s_kmm"], kmm_end)
                                vfm_pre[(si + 1, bb)] = E("sync", lambda e, bb=bb, j=j: e.dma_start(
                                    out=xnfm2[:, (bb % 2) * 4096:(bb % 2 + 1) * 4096],
                                    in_=xnfm_d[j, :, :]), sm["s_fmld0" if bb % 2 == 0 else "s_fmld1"], 16)
                        for snm in ("s_adma0", "s_adma1", "s_admb0", "s_admb1"):
                            W("gpsimd", sm[snm], sm[snm].v)
                        flush(f"p2_s{si}")

        with ExitStack() as S_f1:
            wgbf = S_f1.enter_context(nc.sbuf_tensor("wgbf", [128, 8 * HID], BF16))
            wubf = S_f1.enter_context(nc.sbuf_tensor("wubf", [128, 8 * HID], BF16))
            xnfm_tb = S_f1.enter_context(nc.sbuf_tensor("xnfm_tb", [128, 2 * 4096], BF16))
            hT_buf = S_f1.enter_context(nc.sbuf_tensor("hT_buf", [128, 24 * 512], BF16))
            silu_sb = S_f1.enter_context(nc.sbuf_tensor("silu_sb", [128, 4 * 512], F32))
            ps_g = S_f1.enter_context(nc.psum_tensor("ps_g", [128, 2048], F32))
            ps_u = S_f1.enter_context(nc.psum_tensor("ps_u", [128, 2048], F32))

            vfwc = [0] * 8
            wg3 = wgT_in.rearrange("(c p) j -> p c j", p=128)
            wu3 = wuT_in.rearrange("(c p) j -> p c j", p=128)
            wgo = wgbf.rearrange("p (c j) -> p c j", c=8)
            wuo = wubf.rearrange("p (c j) -> p c j", c=8)
            for r in range(8):
                E("gpsimd", lambda e, r=r: e.dma_start(
                    out=wgo[:, :, r * 384:(r + 1) * 384],
                    in_=wg3[:, :, r * 384:(r + 1) * 384]), sm[f"s_fwc{r}"], 16)
                vfwc[r] = E("gpsimd", lambda e, r=r: e.dma_start(
                    out=wuo[:, :, r * 384:(r + 1) * 384],
                    in_=wu3[:, :, r * 384:(r + 1) * 384]), sm[f"s_fwc{r}"], 16)

            vhv_seq = []
            vsil_seq = []
            vhtw = [0] * 4
            vgu_tb = [0] * 4
            for tb in range(4):
                par = tb % 2
                xft = xnfm_tb[:, par * 4096:(par + 1) * 4096]
                xft3 = xnfm_tb.rearrange("p (b c q) -> p b c q", b=2, c=8)
                if tb >= 2:
                    W("sync", sm["s_gu"], vgu_tb[tb - 2])
                sft = sm["s_fmtb0" if par == 0 else "s_fmtb1"]
                vld = 0
                for s in range(4):
                    blkj = 15 - (4 * tb + s)
                    src = xnfm_d[blkj].rearrange("p (c q) -> p c q", c=8)[:, :, 384:512]
                    vld = E("sync", lambda e, src=src, xft3=xft3, par=par, s=s: e.dma_start(
                        out=xft3[:, par, :, s * 128:(s + 1) * 128], in_=src), sft, 16)
                W("tensor", sft, vld)
                for jc in range(24):
                    if tb == 0 and jc % 3 == 0:
                        W("tensor", sm[f"s_fwc{jc // 3}"], vfwc[jc // 3])
                    idx = tb * 24 + jc
                    bank = idx % 4
                    if idx >= 4:
                        W("tensor", sm["s_silu"], vsil_seq[idx - 4])
                        W("tensor", sm["s_hv"], vhv_seq[idx - 4])
                    for c in range(8):
                        E("tensor", lambda e, bank=bank, c=c, jc=jc, xft=xft: e.matmul(
                            ps_g[:, bank * 512:(bank + 1) * 512],
                            wgbf[:, c * HID + jc * 128: c * HID + (jc + 1) * 128],
                            xft[:, c * 512:(c + 1) * 512],
                            start=(c == 0), stop=(c == 7)))
                    for c in range(8):
                        r = E("tensor", lambda e, bank=bank, c=c, jc=jc, xft=xft: e.matmul(
                            ps_u[:, bank * 512:(bank + 1) * 512],
                            wubf[:, c * HID + jc * 128: c * HID + (jc + 1) * 128],
                            xft[:, c * 512:(c + 1) * 512],
                            start=(c == 0), stop=(c == 7)),
                            sm["s_gu"] if c == 7 else None)
                        if r:
                            vgu_tb[tb] = r
                    W("scalar", sm["s_gu"], vgu_tb[tb])
                    vsil_seq.append(E("scalar", lambda e, bank=bank: e.activation(
                        out=silu_sb[:, bank * 512:(bank + 1) * 512],
                        in_=ps_g[:, bank * 512:(bank + 1) * 512],
                        func=AF.Silu, scale=INV127), sm["s_silu"]))
                    W("vector", sm["s_silu"], vsil_seq[-1])
                    W("vector", sm["s_gu"], vgu_tb[tb])
                    if jc == 0 and tb >= 1:
                        W("vector", sm["s_htw"], vhtw[tb - 1])
                    vhv_seq.append(E("vector", lambda e, bank=bank, jc=jc: e.scalar_tensor_tensor(
                        out=hT_buf[:, jc * 512:(jc + 1) * 512],
                        in0=ps_u[:, bank * 512:(bank + 1) * 512], scalar=INV127,
                        in1=silu_sb[:, bank * 512:(bank + 1) * 512],
                        op0=ALU.mult, op1=ALU.mult), sm["s_hv"]))
                W("gpsimd", sm["s_hv"], vhv_seq[-1])
                vhtw[tb] = E("gpsimd", lambda e, tb=tb: e.dma_start(
                    out=ht_d[tb, :, :], in_=hT_buf[:, :]), sm["s_htw"], 16)
            W("gpsimd", sm["s_htw"], sm["s_htw"].v)
            flush("f1")

        S_wvx = ExitStack()
        wv32 = S_wvx.enter_context(nc.sbuf_tensor("wv32", [128, 8192], F32))
        wvbf = S_wvx.enter_context(nc.sbuf_tensor("wvbf", [128, 2 * 8192], BF16))
        ident2 = S_wvx.enter_context(nc.sbuf_tensor("ident2", [128, 128], BF16))
        vwvc = [0] * H
        with ExitStack() as S_f2:
            wdbf = S_f2.enter_context(nc.sbuf_tensor("wdbf", [128, 24 * 1024], BF16))
            hT_rd = S_f2.enter_context(nc.sbuf_tensor("hT_rd", [128, 2 * 12288], BF16))
            g_sb = S_f2.enter_context(nc.sbuf_tensor("g_sb", [128, 2 * 1024], F32))
            ps_d = S_f2.enter_context(nc.psum_tensor("ps_d", [128, 2048], F32))

            v_id2 = E("sync", lambda e: e.dma_start(out=ident2[:, :], in_=ident_in[:, :]), sm["s_wvc"], 16)
            vwv0 = E("sync", lambda e: e.dma_start(
                out=wv32.rearrange("p (c v) -> p c v", c=8),
                in_=wvT_in[0].rearrange("(c p) v -> p c v", p=128)), sm["s_wv0"], 16)
            W("vector", sm["s_wv0"], vwv0)
            E("vector", lambda e: e.tensor_copy(wvbf[:, 0:4096], wv32[:, 0:4096]), sm["s_wv1"])
            vwvc[0] = E("vector", lambda e: e.tensor_copy(wvbf[:, 4096:8192], wv32[:, 4096:8192]),
                        sm["s_wv1"])

            vfwd = [0] * 8
            wd3 = wdT_in.rearrange("(c p) d -> p c d", p=128)
            wdo = wdbf.rearrange("p (c d) -> p c d", c=24)
            for r in range(8):
                vfwd[r] = E("gpsimd", lambda e, r=r: e.dma_start(
                    out=wdo[:, r * 3:(r + 1) * 3, :],
                    in_=wd3[:, r * 3:(r + 1) * 3, :]), sm[f"s_fwc{r}"], 16)

            vgs_seq = []
            vgw_seq = []
            vdm_seq = []
            for tb in range(4):
                par = tb % 2
                hrt = hT_rd[:, par * 12288:(par + 1) * 12288]
                if tb >= 2:
                    W("sync", sm["s_dmm"], vdm_seq[4 * (tb - 2) + 3])
                sht = sm["s_htr0" if par == 0 else "s_htr1"]
                vht = E("sync", lambda e, hrt=hrt, tb=tb: e.dma_start(
                    out=hrt[:, :], in_=ht_d[tb, :, :]), sht, 16)
                for s in range(4):
                    u = 4 * tb + s
                    bank = u % 2
                    W("tensor", sht, vht)
                    if u >= 2:
                        W("tensor", sm["s_gs"], vgs_seq[u - 2])
                    vdm = 0
                    for jc in range(24):
                        if u == 0 and jc % 3 == 0:
                            W("tensor", sm[f"s_fwc{jc // 3}"], vfwd[jc // 3])
                        for dh in range(2):
                            r = E("tensor", lambda e, bank=bank, jc=jc, dh=dh, hrt=hrt, s=s: e.matmul(
                                ps_d[:, bank * 1024 + dh * 512: bank * 1024 + (dh + 1) * 512],
                                hrt[:, jc * 512 + s * 128: jc * 512 + (s + 1) * 128],
                                wdbf[:, jc * 1024 + dh * 512: jc * 1024 + (dh + 1) * 512],
                                start=(jc == 0), stop=(jc == 23)),
                                sm["s_dmm"] if (jc == 23 and dh == 1) else None)
                            if r:
                                vdm = r
                    vdm_seq.append(vdm)
                    W("scalar", sm["s_dmm"], vdm)
                    if u >= 2:
                        W("scalar", sm["s_gw0" if u % 2 == 0 else "s_gw1"], vgw_seq[u - 2])
                    vgs = 0
                    for dh in range(2):
                        vgs = E("scalar", lambda e, bank=bank, dh=dh: e.activation(
                            out=g_sb[:, bank * 1024 + dh * 512: bank * 1024 + (dh + 1) * 512],
                            in_=ps_d[:, bank * 1024 + dh * 512: bank * 1024 + (dh + 1) * 512],
                            func=AF.Silu, scale=INV127), sm["s_gs"])
                    vgs_seq.append(vgs)
                    W("gpsimd", sm["s_gs"], vgs)
                    vgw_seq.append(E("gpsimd", lambda e, bank=bank, u=u: e.dma_start(
                        out=g_d[u, :, :], in_=g_sb[:, bank * 1024:(bank + 1) * 1024]),
                        sm["s_gw0" if u % 2 == 0 else "s_gw1"], 16))
            W("gpsimd", sm["s_gw0"], sm["s_gw0"].v)
            W("gpsimd", sm["s_gw1"], sm["s_gw1"].v)
            flush("f2")

        with ExitStack() as S_p3o:
            ho_sb = S_p3o.enter_context(nc.sbuf_tensor("ho_sb", [128, NSLOT * 1024], F32))
            a_rd = S_p3o.enter_context(nc.sbuf_tensor("a_rd", [128, 3 * 1024], BF16))
            at_sb = S_p3o.enter_context(nc.sbuf_tensor("at_sb", [128, 3 * 1024], BF16))
            xn_b = S_p3o.enter_context(nc.sbuf_tensor("xn_b", [128, 2 * 1024], F32))
            g_b = S_p3o.enter_context(nc.sbuf_tensor("g_b", [128, 2 * 1024], F32))
            x_b = S_p3o.enter_context(nc.sbuf_tensor("x_b", [128, 2 * 1024], F32))
            s_b = S_p3o.enter_context(nc.sbuf_tensor("s_b", [128, 2 * 1024], F32))
            o_b = S_p3o.enter_context(nc.sbuf_tensor("o_b", [128, 2 * 1024], F32))
            ps_tr3 = S_p3o.enter_context(nc.psum_tensor("ps_tr3", [128, 3 * 1024], BF16))
            ps_ho = S_p3o.enter_context(nc.psum_tensor("ps_ho", [128, 2048], F32))

            if True:
                UTOT = H * NSLOT
                vard = [0] * UTOT
                vtr3 = [0] * UTOT
                vatc = [0] * UTOT
                vhom = [0] * UTOT
                vhoa = [0] * UTOT
                vev = [0] * NSLOT
                vout = [0] * NSLOT

                vwvl = [vwv0] + [0] * (H - 1)

                def plan_wv(h):
                    W("sync", sm["s_wv1"], vwvc[h - 1])
                    vwvl[h] = E("sync", lambda e, h=h: e.dma_start(
                        out=wv32.rearrange("p (c v) -> p c v", c=8),
                        in_=wvT_in[h].rearrange("(c p) v -> p c v", p=128)),
                        sm["s_wv0"], 16)
                    W("vector", sm["s_wv0"], vwvl[h])
                    if h >= 2:
                        W("vector", sm["s_hom"], vhom[(h - 1) * NSLOT - 1])
                    base = (h % 2) * 8192
                    E("vector", lambda e, base=base: e.tensor_copy(
                        wvbf[:, base:base + 4096], wv32[:, 0:4096]), sm["s_wv1"])
                    vwvc[h] = E("vector", lambda e, base=base: e.tensor_copy(
                        wvbf[:, base + 4096:base + 8192], wv32[:, 4096:8192]), sm["s_wv1"])

                def plan_ard(u):
                    h, i = u // NSLOT, u % NSLOT
                    p, tt = i // 2, i % 2
                    if u >= 3:
                        W("sync", sm["s_tr3"], vtr3[u - 3])
                    vard[u] = E("sync", lambda e, p=p, h=h, tt=tt, u=u: e.dma_start(
                        out=a_rd[:, (u % 3) * 1024:(u % 3 + 1) * 1024],
                        in_=a_d[p, h, :, tt * 1024:(tt + 1) * 1024]),
                        sm[f"s_ard{u % 3}"], 16)

                def plan_tr(u):
                    W("tensor", sm[f"s_ard{u % 3}"], vard[u])
                    if u >= 3:
                        W("tensor", sm["s_atc"], vatc[u - 3][0])
                        W("tensor", sm["s_atcv"], vatc[u - 3][1])
                    if u == 0:
                        W("tensor", sm["s_wvc"], 16)
                    for c in range(8):
                        r = E("tensor", lambda e, c=c, u=u: e.transpose(
                            ps_tr3[:, (u % 3) * 1024 + c * 128: (u % 3) * 1024 + (c + 1) * 128],
                            a_rd[:, (u % 3) * 1024 + c * 128: (u % 3) * 1024 + (c + 1) * 128],
                            ident2[:, :]), sm["s_tr3"] if c == 7 else None)
                        if r:
                            vtr3[u] = r

                def plan_atc(u):
                    W("scalar", sm["s_tr3"], vtr3[u])
                    W("vector", sm["s_tr3"], vtr3[u])
                    va = E("scalar", lambda e, u=u: e.copy(
                        out=at_sb[:, (u % 3) * 1024:(u % 3) * 1024 + 512],
                        in_=ps_tr3[:, (u % 3) * 1024:(u % 3) * 1024 + 512]), sm["s_atc"])
                    vb = E("vector", lambda e, u=u: e.tensor_copy(
                        at_sb[:, (u % 3) * 1024 + 512:(u % 3 + 1) * 1024],
                        ps_tr3[:, (u % 3) * 1024 + 512:(u % 3 + 1) * 1024]), sm["s_atcv"])
                    vatc[u] = (va, vb)

                def plan_mm(u):
                    h = u // NSLOT
                    W("tensor", sm["s_atc"], vatc[u][0])
                    if (u % NSLOT) == 0:
                        W("tensor", sm["s_wv1"], vwvc[h])
                    if u >= 2:
                        W("tensor", sm["s_hoa"], vhoa[u - 2])
                    for c in range(8):
                        if c == 4:
                            W("tensor", sm["s_atcv"], vatc[u][1])
                        for vh in range(2):
                            r = E("tensor", lambda e, c=c, vh=vh, u=u, h=h: e.matmul(
                                ps_ho[:, (u % 2) * 1024 + vh * 512: (u % 2) * 1024 + vh * 512 + 512],
                                at_sb[:, (u % 3) * 1024 + c * 128: (u % 3) * 1024 + (c + 1) * 128],
                                wvbf[:, (h % 2) * 8192 + c * 1024 + vh * 512: (h % 2) * 8192 + c * 1024 + vh * 512 + 512],
                                start=(c == 0), stop=(c == 7)),
                                sm["s_hom"] if (c == 7 and vh == 1) else None)
                            if r:
                                vhom[u] = r

                def plan_add(u):
                    h, i = u // NSLOT, u % NSLOT
                    W("vector", sm["s_hom"], vhom[u])
                    if h == 0:
                        vhoa[u] = E("vector", lambda e, u=u, i=i: e.tensor_copy(
                            ho_sb[:, i * 1024:(i + 1) * 1024],
                            ps_ho[:, (u % 2) * 1024:(u % 2 + 1) * 1024]), sm["s_hoa"])
                    else:
                        vhoa[u] = E("vector", lambda e, u=u, i=i: e.tensor_add(
                            ho_sb[:, i * 1024:(i + 1) * 1024],
                            ps_ho[:, (u % 2) * 1024:(u % 2 + 1) * 1024],
                            ho_sb[:, i * 1024:(i + 1) * 1024]), sm["s_hoa"])

                def plan_epi(i):
                    par = i % 2
                    ser = sm["s_er0" if par == 0 else "s_er1"]
                    sou = sm["s_out0" if par == 0 else "s_out1"]
                    if i >= 2:
                        W("sync", sm["s_ev"], vev[i - 2])
                    E("sync", lambda e, i=i, par=par: e.dma_start(
                        out=xn_b[:, par * 1024:(par + 1) * 1024], in_=xnf32_d[i, :, :]), ser, 16)
                    E("sync", lambda e, i=i, par=par: e.dma_start(
                        out=g_b[:, par * 1024:(par + 1) * 1024], in_=g_d[i, :, :]), ser, 16)
                    ver = E("sync", lambda e, i=i, par=par: e.dma_start(
                        out=x_b[:, par * 1024:(par + 1) * 1024], in_=x3[63 - 4 * i, :, :]), ser, 16)

                    W("vector", ser, ver)
                    W("vector", sm["s_hoa"], vhoa_slot_final[i])
                    if i >= 2:
                        W("vector", sou, vout[i - 2])
                    v1 = E("vector", lambda e, i=i, par=par: e.tensor_add(
                        s_b[:, par * 1024:(par + 1) * 1024],
                        xn_b[:, par * 1024:(par + 1) * 1024],
                        ho_sb[:, i * 1024:(i + 1) * 1024]), sm["s_ev"])
                    W("vector", sm["s_ev"], v1)
                    v2 = E("vector", lambda e, par=par: e.tensor_mul(
                        s_b[:, par * 1024:(par + 1) * 1024],
                        s_b[:, par * 1024:(par + 1) * 1024],
                        g_b[:, par * 1024:(par + 1) * 1024]), sm["s_ev"])
                    W("vector", sm["s_ev"], v2)
                    vev[i] = E("vector", lambda e, par=par: e.tensor_add(
                        o_b[:, par * 1024:(par + 1) * 1024],
                        s_b[:, par * 1024:(par + 1) * 1024],
                        x_b[:, par * 1024:(par + 1) * 1024]), sm["s_ev"])
                    W("gpsimd", sm["s_ev"], vev[i])
                    vout[i] = E("gpsimd", lambda e, i=i, par=par: e.dma_start(
                        out=y_out[i * 128:(i + 1) * 128, :],
                        in_=o_b[:, par * 1024:(par + 1) * 1024]), sou, 16)

                vhoa_slot_final = [0] * NSLOT

                def maybe_epi(u):
                    h, i = u // NSLOT, u % NSLOT
                    if h == H - 1:
                        vhoa_slot_final[i] = vhoa[u]
                        plan_epi(i)

                plan_ard(0)
                plan_tr(0)
                plan_ard(1)
                plan_tr(1)
                for u in range(UTOT):
                    if u % NSLOT == 0 and u > 0:
                        plan_wv(u // NSLOT)
                    if u + 2 < UTOT:
                        plan_ard(u + 2)
                        plan_tr(u + 2)
                    plan_atc(u)
                    plan_mm(u)
                    plan_add(u)
                    maybe_epi(u)
                W("gpsimd", sm["s_out0"], sm["s_out0"].v)
                W("gpsimd", sm["s_out1"], sm["s_out1"].v)
                flush("p3")
        S_wvx.close()


    return nc


_NC_CACHE = None


def _get_nc():
    global _NC_CACHE
    if _NC_CACHE is None:
        _NC_CACHE = build_nc()
    return _NC_CACHE


def kernel(L0_attn_x, L0_attn_Wq, L0_attn_Wk, L0_attn_Wv,
           L0_ffn_w_gate, L0_ffn_w_up, L0_ffn_w_down):
    x = np.asarray(L0_attn_x, dtype=np.float32)
    Wq = np.asarray(L0_attn_Wq, dtype=np.float32)
    Wk = np.asarray(L0_attn_Wk, dtype=np.float32)
    Wv = np.asarray(L0_attn_Wv, dtype=np.float32)
    wg = np.asarray(L0_ffn_w_gate, dtype=np.int32)
    wu = np.asarray(L0_ffn_w_up, dtype=np.int32)
    wd = np.asarray(L0_ffn_w_down, dtype=np.int32)

    wqT = np.ascontiguousarray(np.transpose(Wq, (2, 0, 1)).reshape(D, H * DK))
    wkT = np.ascontiguousarray(np.transpose(Wk, (2, 0, 1)).reshape(D, H * DK))
    wvT = np.ascontiguousarray(np.transpose(Wv, (0, 2, 1)))
    wgT = np.ascontiguousarray(wg.T)
    wuT = np.ascontiguousarray(wu.T)
    wdT = np.ascontiguousarray(wd.T)
    ident = np.eye(128, dtype=ml_dtypes.bfloat16)

    in_maps = []
    metas = []
    for c in range(8):
        b, cc = c // 4, c % 4
        loc2glob = np.array([4 * (lt // 4) + ((lt % 4) + cc + 1) % 4 for lt in range(T)])
        x_loc = np.ascontiguousarray(
            x[b].reshape(T, 128, D)[loc2glob].reshape(N, D))
        kglob = (loc2glob[None, :] * 128 + np.arange(128)[:, None]).astype(np.float32)
        qglob = np.zeros((NPAIR, 128, 256), dtype=np.float32)
        tg = [60 - 4 * i + cc for i in range(NSLOT)]
        for p in range(NPAIR):
            qglob[p, :, 0:128] = tg[2 * p] * 128 + np.arange(128)[None, :]
            qglob[p, :, 128:256] = tg[2 * p + 1] * 128 + np.arange(128)[None, :]
        in_maps.append({
            "x": x_loc, "wqT": wqT, "wkT": wkT, "wvT": wvT,
            "wgT": wgT, "wuT": wuT, "wdT": wdT,
            "ident": ident, "kglob": kglob, "qglob": qglob,
        })
        metas.append((b, tg))

    nc = _get_nc()
    res = run_bass_kernel_spmd(nc, in_maps, core_ids=list(range(8)))

    out = np.zeros((B, N, D), dtype=np.float32)
    for c in range(8):
        b, tg = metas[c]
        yc = np.asarray(res.results[c]["y"], dtype=np.float32)
        for i in range(NSLOT):
            out[b, tg[i] * 128:(tg[i] + 1) * 128, :] = yc[i * 128:(i + 1) * 128, :]
    return out
